# version 01
# baseline (speedup 1.0000x reference)
import numpy as np
import ml_dtypes
from contextlib import ExitStack
import concourse.bass as bass
import concourse.mybir as mybir
from concourse.bass_utils import run_bass_kernel_spmd

F32 = mybir.dt.float32
BF16 = mybir.dt.bfloat16
U32 = mybir.dt.uint32
AF = mybir.ActivationFunctionType
ALU = mybir.AluOpType
AXX = mybir.AxisListType.X
BF = ml_dtypes.bfloat16

B, T, D = 16, 2048, 1024
SPB = 2
DC = D // 128
NT = T // 128
NB = T // 512
KMAX = 15
HID = 64
NSLOT = 4096
HIDF = 4096
NHEAD, HD = 16, 64
LN_EPS = 1e-5


class Sem:
    __slots__ = ("h", "v")

    def __init__(self, h):
        self.h = h
        self.v = 0


class Bld:
    def __init__(self):
        self.es = ExitStack()
        nc = bass.Bass("TRN2", target_bir_lowering=False)
        self.nc = nc
        self.s = {}
        for name in ["dw", "dg", "ds", "pe", "ve", "ac", "po", "mA", "mB", "mC",
                     "dA", "dB", "dC", "dD", "dE", "dF", "dH", "dI"]:
            self.s[name] = Sem(self.es.enter_context(nc.semaphore(name)))
        self.inp = {}
        self.out = {}
        self.dram = {}
        self.sb = {}
        self.vecidx = {}
        self.kvals = {}

    def ein(self, name):
        if name not in self.inp:
            shape, dtype = INPUT_SPECS[name]
            self.inp[name] = self.nc.dram_tensor(name, list(shape), dtype, kind="ExternalInput").ap()
        return self.inp[name]

    def eout(self, name, shape, dtype):
        if name not in self.out:
            self.out[name] = self.nc.dram_tensor(name, list(shape), dtype, kind="ExternalOutput").ap()
        return self.out[name]

    def dr(self, name, shape, dtype):
        if name not in self.dram:
            self.dram[name] = self.nc.dram_tensor(name, list(shape), dtype, kind="Internal")
        return self.dram[name].ap()

    def sbt(self, name, shape=None, dtype=None):
        if name not in self.sb:
            self.sb[name] = self.es.enter_context(self.nc.sbuf_tensor(name, list(shape), dtype))
        return self.sb[name]


def host_constants():
    c = {}
    c["c_ident_bf"] = np.eye(128, dtype=BF)
    c["c_ident_f32"] = np.eye(128, dtype=np.float32)
    half = D // 2
    inv_freq = 1.0 / 10000.0 ** (np.arange(half, dtype=np.float64) / half)
    ang = inv_freq[:, None] * np.arange(T, dtype=np.float64)[None, :]
    c["c_ropecos"] = np.cos(ang).astype(BF)
    c["c_ropesin"] = np.sin(ang).astype(BF)
    c["c_iota_row"] = np.arange(T, dtype=np.float32)[None, :]
    c["c_iota_col"] = np.arange(128, dtype=np.float32)[:, None]
    c["c_hat_iota"] = np.tile(np.arange(16, dtype=np.float32), 16)[None, :]
    tri = np.where(np.arange(128)[:, None] <= np.arange(128)[None, :], 0.0, -1e30).astype(np.float32)
    c["c_triT"] = np.ascontiguousarray(tri)
    c["c_tri_bf"] = np.ascontiguousarray(tri).astype(BF)
    c["c_hat_tile"] = np.broadcast_to(np.tile(np.arange(16, dtype=np.float32), 16)[None, :],
                                      (128, 256)).copy()
    c["c_triTT"] = np.ascontiguousarray(tri.T).astype(BF)
    c["c_neg30r_bf"] = np.full((1, 128), -1e30, dtype=BF)
    c["c_ones_bf"] = np.ones((128, 1), dtype=BF)
    c["c_ones_f32"] = np.ones((128, 1), dtype=np.float32)
    c["c_onesr_bf"] = np.ones((1, 512), dtype=BF)
    c["c_onesr_f32"] = np.ones((1, 512), dtype=np.float32)
    return c


VEC_NAMES = ["gamma_x_T", "gamma_slot_T", "bproj_T", "lngam_T", "lnbet_T", "mur_T", "muk_T"]

INPUT_SPECS = {
    "x_tok": ([SPB * T, D], F32),
    "Wi": ([D, D], F32), "Wp": ([D, D], F32),
    "pal": ([D, 256], F32),
    "Wo": ([D, D], F32),
    "W1": ([17, HID], F32), "b1": ([HID, 1], F32),
    "W2": ([HID, HID], F32), "b2": ([HID, 1], F32),
    "W3": ([HID, 4], F32), "b3": ([4, 1], F32),
    "se_flat": ([NSLOT, D], F32),
    "w1f": ([D, HIDF], F32), "w2f": ([HIDF, D], F32),
    "wqkv": ([D, 3 * D], F32), "wproj": ([D, D], F32),
    "WrT": ([D, D], F32), "WkT": ([D, D], F32), "WvT": ([D, D], F32),
    "c_ident_bf": ([128, 128], BF16),
    "c_ident_f32": ([128, 128], F32),
    "c_ropecos": ([512, T], BF16),
    "c_ropesin": ([512, T], BF16),
    "c_iota_row": ([1, T], F32),
    "c_iota_col": ([128, 1], F32),
    "c_hat_iota": ([1, 256], F32),
    "c_triT": ([128, 128], F32),
    "c_tri_bf": ([128, 128], BF16),
    "c_hat_tile": ([128, 256], F32),
    "c_triTT": ([128, 128], BF16),
    "c_neg30r_bf": ([1, 128], BF16),
    "c_ones_bf": ([128, 1], BF16),
    "c_ones_f32": ([128, 1], F32),
    "c_onesr_bf": ([1, 512], BF16),
    "c_onesr_f32": ([1, 512], F32),
}
for _nm in VEC_NAMES:
    INPUT_SPECS[_nm] = ([128, DC], F32)


def host_inputs_for_core(inputs, core):
    f32 = np.float32
    m = {}
    x = np.asarray(inputs["L0_attn_x"], f32)
    m["x_tok"] = np.ascontiguousarray(x[core * SPB:(core + 1) * SPB].reshape(SPB * T, D))
    m["Wi"] = np.ascontiguousarray(np.asarray(inputs["L0_attn_Wi"], f32))
    m["Wp"] = np.ascontiguousarray(np.asarray(inputs["L0_attn_Wp"], f32))
    m["pal"] = np.ascontiguousarray(np.asarray(inputs["L0_attn_palette"], f32).reshape(D, 256))
    m["Wo"] = np.ascontiguousarray(np.asarray(inputs["L0_attn_Wo"], f32)[0])
    m["W1"] = np.ascontiguousarray(np.asarray(inputs["L0_attn_W1"], f32))
    m["b1"] = np.ascontiguousarray(np.asarray(inputs["L0_attn_b1"], f32)[:, None])
    m["W2"] = np.ascontiguousarray(np.asarray(inputs["L0_attn_W2"], f32))
    m["b2"] = np.ascontiguousarray(np.asarray(inputs["L0_attn_b2"], f32)[:, None])
    wm = np.asarray(inputs["L0_attn_Wm"], f32)
    m["W3"] = np.ascontiguousarray(np.concatenate(
        [np.asarray(inputs["L0_attn_Wc"], f32), wm, wm], axis=1))
    bm = np.asarray(inputs["L0_attn_bm"], f32)
    m["b3"] = np.ascontiguousarray(np.concatenate(
        [np.asarray(inputs["L0_attn_bc"], f32), bm, bm])[:, None])
    m["se_flat"] = np.ascontiguousarray(np.asarray(inputs["L0_ffn_slot_embeds"], f32).reshape(NSLOT, D))
    m["w1f"] = np.ascontiguousarray(np.asarray(inputs["L0_ffn_w1"], f32))
    m["w2f"] = np.ascontiguousarray(np.asarray(inputs["L0_ffn_w2"], f32))
    m["wqkv"] = np.ascontiguousarray(np.asarray(inputs["L1_attn_w_qkv"], f32))
    m["wproj"] = np.ascontiguousarray(np.asarray(inputs["L1_attn_w_proj"], f32))
    m["WrT"] = np.ascontiguousarray(np.asarray(inputs["L1_ffn_W_r"], f32).T)
    m["WkT"] = np.ascontiguousarray(np.asarray(inputs["L1_ffn_W_k"], f32).T)
    m["WvT"] = np.ascontiguousarray(np.asarray(inputs["L1_ffn_W_v"], f32).T)

    def vt(v):
        return np.ascontiguousarray(np.asarray(v, f32).reshape(DC, 128).T)
    m["gamma_x_T"] = vt(inputs["L0_ffn_gamma_x"])
    m["gamma_slot_T"] = vt(inputs["L0_ffn_gamma_slot"])
    m["bproj_T"] = vt(inputs["L1_attn_b_proj"])
    m["lngam_T"] = vt(inputs["L1_attn_ln_gamma"])
    m["lnbet_T"] = vt(inputs["L1_attn_ln_beta"])
    m["mur_T"] = vt(inputs["L1_ffn_mu_r"])
    m["muk_T"] = vt(inputs["L1_ffn_mu_k"])
    m.update(host_constants())
    return m


KONST_VALS = [LN_EPS, 1.0, 0.0, -1e30, 7.5]


def emit_setup(b: Bld):
    nc = b.nc
    dw = b.s["dw"]
    vt = b.sbt("vecs", [128, DC * len(VEC_NAMES)], F32)
    b.vecidx = {nm: i for i, nm in enumerate(VEC_NAMES)}
    kp = b.sbt("kpool", [128, len(KONST_VALS)], F32)
    b.kvals = {v: i for i, v in enumerate(KONST_VALS)}
    consts = [("c_ident_bf", "identb"), ("c_ident_f32", "identf"),
              ("c_iota_col", "iota_col"),
              ("c_triTT", "triTT"), ("c_neg30r_bf", "neg30r_bf"),
              ("c_tri_bf", "tri_bf"), ("c_hat_tile", "hat_tile"),
              ("c_ones_bf", "ones_bf"), ("c_ones_f32", "ones_f32"),
              ("c_onesr_bf", "onesr_bf"), ("c_onesr_f32", "onesr_f32")]
    dw0 = dw.v
    n = 0
    with nc.Block("setup") as blk:
        @blk.vector
        def _(e):
            for v, i in b.kvals.items():
                e.memset(kp[:, i:i + 1], float(v))

        @blk.sync
        def _(e):
            nonlocal n
            for i, nm in enumerate(VEC_NAMES):
                e.dma_start(out=vt[:, i * DC:(i + 1) * DC], in_=b.ein(nm)).then_inc(dw.h, 16)
                n += 1
            for nm, sb in consts:
                shape, dtype = INPUT_SPECS[nm]
                t = b.sbt(sb, shape, dtype)
                e.dma_start(out=t[:], in_=b.ein(nm)).then_inc(dw.h, 16)
                n += 1
            e.wait_ge(dw.h, dw0 + 16 * n)
    dw.v = dw0 + 16 * n
    b.sbt("xT", [128, DC * T], F32)
    b.sbt("xnT", [128, DC * T], BF16)
    b.sbt("f_row", [1, T], F32)
    b.sbt("l0_mc", [128, 64], F32)
    b.sbt("l0_cmx", [32, 128], F32)


def vcol(b: Bld, name, c):
    i = b.vecidx[name]
    return b.sb["vecs"][:, i * DC + c: i * DC + c + 1]


def konst(b: Bld, val):
    i = b.kvals[val]
    return b.sb["kpool"][:, i:i + 1]


def emit_weight_prep(b: Bld):
    nc = b.nc
    dg = b.s["dg"]
    casts = [("Wi", [D, D]), ("Wp", [D, D]), ("w1f", [D, HIDF]), ("w2f", [HIDF, D]),
             ("wqkv", [D, 3 * D]), ("wproj", [D, D]),
             ("WrT", [D, D]), ("WkT", [D, D]), ("WvT", [D, D]),
             ("pal", [D, 256]), ("Wo", [D, D])]
    dg0 = dg.v
    with nc.Block("wprep") as blk:
        @blk.gpsimd
        def _(g):
            for nm, shp in casts:
                d = b.dr(nm + "_bf", shp, BF16)
                g.dma_start(out=d, in_=b.ein(nm)).then_inc(dg.h, 16)
            g.wait_ge(dg.h, dg0 + 16 * len(casts))
    dg.v = dg0 + 16 * len(casts)


def emit_load_x(b: Bld, seq, tag=""):
    nc = b.nc
    pe, ve = b.s["pe"], b.s["ve"]
    dsl = [b.s["dA"], b.s["dB"]]
    d0 = [s.v for s in dsl]
    xT, identf = b.sb["xT"], b.sb["identf"]
    xtok = b.ein("x_tok")
    pe0, ve0 = pe.v, ve.v
    with nc.sbuf_tensor(f"lxt{tag}", [128, 2 * D], F32) as tile, \
         nc.psum_tensor(f"lxp0{tag}", [128, 128], F32) as p0, \
         nc.psum_tensor(f"lxp1{tag}", [128, 128], F32) as p1, \
         nc.psum_tensor(f"lxp2{tag}", [128, 128], F32) as p2, \
         nc.psum_tensor(f"lxp3{tag}", [128, 128], F32) as p3, \
         nc.Block(f"loadx{tag}") as blk:
        pss = [p0, p1, p2, p3]

        @blk.sync
        def _(e):
            for i in range(NT):
                if i >= 2:
                    e.wait_ge(pe.h, pe0 + DC * (i - 1))
                e.dma_start(out=tile[:, (i % 2) * D:(i % 2) * D + D],
                            in_=xtok[seq * T + i * 128: seq * T + (i + 1) * 128, :]
                            ).then_inc(dsl[i % 2].h, 16)

        @blk.tensor
        def _(e):
            for i in range(NT):
                e.wait_ge(dsl[i % 2].h, d0[i % 2] + 16 * (i // 2 + 1))
                buf = tile[:, (i % 2) * D:(i % 2) * D + D]
                for c in range(DC):
                    k = i * DC + c
                    if k >= 4:
                        e.wait_ge(ve.h, ve0 + k - 3)
                    e.transpose(pss[k % 4][:], buf[:, c * 128:(c + 1) * 128], identf[:]).then_inc(pe.h, 1)

        @blk.vector
        def _(e):
            for i in range(NT):
                for c in range(DC):
                    k = i * DC + c
                    e.wait_ge(pe.h, pe0 + k + 1)
                    e.tensor_copy(out=xT[:, c * T + i * 128: c * T + (i + 1) * 128],
                                  in_=pss[k % 4][:]).then_inc(ve.h, 1)
    for i, s in enumerate(dsl):
        s.v = d0[i] + 16 * (NT // 2)
    pe.v = pe0 + NT * DC
    ve.v = ve0 + NT * DC


def emit_store_x(b: Bld, seq, tag=""):
    nc = b.nc
    pe, ve = b.s["pe"], b.s["ve"]
    dsl = [b.s["dE"], b.s["dF"]]
    d0 = [s.v for s in dsl]
    xT, identf = b.sb["xT"], b.sb["identf"]
    ytok = b.eout("y_tok", [SPB * T, D], F32)
    pe0, ve0 = pe.v, ve.v
    with nc.sbuf_tensor(f"sxt{tag}", [128, 2 * D], F32) as tile, \
         nc.psum_tensor(f"sxp0{tag}", [128, 128], F32) as p0, \
         nc.psum_tensor(f"sxp1{tag}", [128, 128], F32) as p1, \
         nc.psum_tensor(f"sxp2{tag}", [128, 128], F32) as p2, \
         nc.psum_tensor(f"sxp3{tag}", [128, 128], F32) as p3, \
         nc.Block(f"storex{tag}") as blk:
        pss = [p0, p1, p2, p3]

        @blk.tensor
        def _(e):
            for i in range(NT):
                if i >= 2:
                    e.wait_ge(dsl[i % 2].h, d0[i % 2] + 16 * (i // 2))
                for c in range(DC):
                    k = i * DC + c
                    if k >= 4:
                        e.wait_ge(ve.h, ve0 + k - 3)
                    e.transpose(pss[k % 4][:], xT[:, c * T + i * 128: c * T + (i + 1) * 128],
                                identf[:]).then_inc(pe.h, 1)

        @blk.vector
        def _(e):
            for i in range(NT):
                buf = tile[:, (i % 2) * D:(i % 2) * D + D]
                for c in range(DC):
                    k = i * DC + c
                    e.wait_ge(pe.h, pe0 + k + 1)
                    e.tensor_copy(out=buf[:, c * 128:(c + 1) * 128], in_=pss[k % 4][:]).then_inc(ve.h, 1)

        @blk.sync
        def _(e):
            for i in range(NT):
                e.wait_ge(ve.h, ve0 + DC * (i + 1))
                e.dma_start(out=ytok[seq * T + i * 128: seq * T + (i + 1) * 128, :],
                            in_=tile[:, (i % 2) * D:(i % 2) * D + D]).then_inc(dsl[i % 2].h, 16)
            e.wait_ge(dsl[0].h, d0[0] + 16 * (NT // 2))
            e.wait_ge(dsl[1].h, d0[1] + 16 * (NT // 2))
    for i, s in enumerate(dsl):
        s.v = d0[i] + 16 * (NT // 2)
    pe.v = pe0 + NT * DC
    ve.v = ve0 + NT * DC


def emit_ln(b: Bld, src, dst, src_is_f32, with_f, gamma=None, beta=None,
            add_into=None, tag=""):
    nc = b.nc
    pe, ve, ac, po, mA = b.s["pe"], b.s["ve"], b.s["ac"], b.s["po"], b.s["mA"]
    ones_bf, onesr_f32 = b.sb["ones_bf"], b.sb["onesr_f32"]
    f_r = b.sb["f_row"][:]
    NK = NB * DC

    def xsl(c, blki):
        return src[:, c * T + blki * 512: c * T + blki * 512 + 512]

    with nc.sbuf_tensor(f"lnrw{tag}", [1, 4 * T], F32) as rows, \
         nc.sbuf_tensor(f"lnhi{tag}", [128, 4 * 512], BF16) as hi, \
         nc.sbuf_tensor(f"lnsq{tag}", [128, 4 * 512], BF16) as sq, \
         nc.psum_tensor(f"lnpA{tag}", [128, 512], F32) as pA, \
         nc.psum_tensor(f"lnpB{tag}", [128, 512], F32) as pB, \
         nc.psum_tensor(f"lnpC{tag}", [128, 512], F32) as pC, \
         nc.psum_tensor(f"lnpD{tag}", [128, 512], F32) as pD:
        ps_s = [pA, pB]
        ps_q = [pC, pD]
        mu_r, rs_r = rows[:, 0:T], rows[:, T:2 * T]
        sA_r, sB_r = rows[:, 2 * T:3 * T], rows[:, 3 * T:4 * T]
        s1_r, s2_r = sA_r, sB_r
        pe0, ve0, ac0, mA0 = pe.v, ve.v, ac.v, mA.v

        def hslot(k):
            return hi[:, (k % 4) * 512:(k % 4) * 512 + 512]

        def qslot(k):
            return sq[:, (k % 4) * 512:(k % 4) * 512 + 512]

        with nc.Block(f"lnstat{tag}") as blk:
            @blk.vector
            def _(e):
                for blki in range(NB):
                    for c in range(DC):
                        k = blki * DC + c
                        if k >= 4:
                            e.wait_ge(pe.h, pe0 + k - 3)
                        e.tensor_copy(out=hslot(k), in_=xsl(c, blki)).then_inc(ve.h, 1)
                    if blki >= 1:
                        e.wait_ge(pe.h, pe0 + DC * blki)
                        e.tensor_copy(out=s1_r[:, (blki - 1) * 512:blki * 512],
                                      in_=ps_s[(blki - 1) % 2][0:1, :])
                        e.tensor_copy(out=s2_r[:, (blki - 1) * 512:blki * 512],
                                      in_=ps_q[(blki - 1) % 2][0:1, :]).then_inc(mA.h, 1)
                e.wait_ge(pe.h, pe0 + DC * NB)
                e.tensor_copy(out=s1_r[:, (NB - 1) * 512:NB * 512], in_=ps_s[(NB - 1) % 2][0:1, :])
                e.tensor_copy(out=s2_r[:, (NB - 1) * 512:NB * 512],
                              in_=ps_q[(NB - 1) % 2][0:1, :]).then_inc(mA.h, 1)

            @blk.scalar
            def _(e):
                for k in range(NK):
                    e.wait_ge(ve.h, ve0 + k + 1)
                    if k >= 4:
                        e.wait_ge(pe.h, pe0 + k - 3)
                    e.activation(out=qslot(k), in_=hslot(k), func=AF.Square).then_inc(ac.h, 1)

            @blk.tensor
            def _(e):
                for blki in range(NB):
                    if blki >= 2:
                        e.wait_ge(mA.h, mA0 + blki - 1)
                    for c in range(DC):
                        k = blki * DC + c
                        e.wait_ge(ac.h, ac0 + k + 1)
                        e.matmul(ps_s[blki % 2][0:1, :], ones_bf[:, 0:1], hslot(k),
                                 start=(c == 0), stop=(c == DC - 1))
                        e.matmul(ps_q[blki % 2][0:1, :], ones_bf[:, 0:1], qslot(k),
                                 start=(c == 0), stop=(c == DC - 1)).then_inc(pe.h, 1)
        pe.v = pe0 + NK
        ve.v = ve0 + NK
        ac.v = ac0 + NK
        mA.v = mA0 + NB

        ve0, ac0 = ve.v, ac.v
        with nc.Block(f"lnrow{tag}") as blk:
            @blk.vector
            def _(e):
                v = ve0

                def step(instr):
                    nonlocal v
                    v += 1
                    instr.then_inc(ve.h, 1)
                    e.wait_ge(ve.h, v)
                step(e.tensor_scalar_mul(mu_r, sA_r, 1.0 / D))
                step(e.tensor_scalar_mul(sA_r, sB_r, 1.0 / D))
                step(e.tensor_mul(sB_r, mu_r, mu_r))
                step(e.tensor_sub(sA_r, sA_r, sB_r))
                step(e.tensor_scalar_max(sA_r, sA_r, 1e-30))
                if with_f:
                    e.wait_ge(ac.h, ac0 + 2)
                    step(e.tensor_sub(sA_r, sB_r, f_r))

            @blk.scalar
            def _(e):
                a = ac0

                def step(instr):
                    nonlocal a
                    a += 1
                    instr.then_inc(ac.h, 1)
                    e.wait_ge(ac.h, a)
                e.wait_ge(ve.h, ve0 + 5)
                step(e.activation(out=sB_r, in_=sA_r, func=AF.Ln, bias=konst(b, LN_EPS)[0:1, :]))
                if with_f:
                    step(e.activation(out=f_r, in_=sA_r, func=AF.Ln))
                step(e.activation(out=rs_r, in_=sB_r, func=AF.Exp, scale=-0.5))
                if with_f:
                    e.wait_ge(ve.h, ve0 + 6)
                    step(e.activation(out=f_r, in_=sA_r, func=AF.Exp, scale=0.5))
        ve.v = ve0 + (6 if with_f else 5)
        ac.v = ac0 + (4 if with_f else 2)

        pe0, ac0, po0, mA0 = pe.v, ac.v, b.s["po"].v, mA.v
        INCPS = 3 if gamma is not None else 1
        bcm = [pA, pB]
        bcr = [pC, pD]
        with nc.sbuf_tensor(f"lnbc{tag}", [128, 2 * 1024], F32) as bc, \
             nc.sbuf_tensor(f"lntmp{tag}", [128, 4 * 512], F32) as tmp, \
             nc.Block(f"lnnorm{tag}") as blk:
            @blk.tensor
            def _(e):
                for blki in range(NB):
                    if blki >= 2:
                        e.wait_ge(ac.h, ac0 + blki - 1)
                    e.matmul(bcm[blki % 2][:], onesr_f32[:, 0:128],
                             mu_r[:, blki * 512:(blki + 1) * 512], start=True, stop=True)
                    e.matmul(bcr[blki % 2][:], onesr_f32[:, 0:128],
                             rs_r[:, blki * 512:(blki + 1) * 512], start=True, stop=True).then_inc(pe.h, 1)

            @blk.scalar
            def _(e):
                for blki in range(NB):
                    e.wait_ge(pe.h, pe0 + blki + 1)
                    if blki >= 2:
                        e.wait_ge(mA.h, mA0 + INCPS * (blki - 1) * DC)
                    s = (blki % 2) * 1024
                    e.copy(out=bc[:, s:s + 512], in_=bcm[blki % 2][:])
                    e.copy(out=bc[:, s + 512:s + 1024], in_=bcr[blki % 2][:]).then_inc(ac.h, 1)

            @blk.gpsimd
            def _(e):
                po = b.s["po"]
                for blki in range(NB):
                    e.wait_ge(ac.h, ac0 + blki + 1)
                    s = (blki % 2) * 1024
                    for c in range(DC):
                        k = blki * DC + c
                        if k >= 4:
                            e.wait_ge(mA.h, mA0 + INCPS * (k - 3))
                        e.tensor_sub(tmp[:, (k % 4) * 512:(k % 4) * 512 + 512],
                                     xsl(c, blki), bc[:, s:s + 512]).then_inc(po.h, 1)

            @blk.vector
            def _(e):
                po = b.s["po"]
                for blki in range(NB):
                    s = (blki % 2) * 1024
                    for c in range(DC):
                        k = blki * DC + c
                        e.wait_ge(po.h, po0 + k + 1)
                        tsl = tmp[:, (k % 4) * 512:(k % 4) * 512 + 512]
                        if gamma is None:
                            dsl_ = dst[:, c * T + blki * 512: c * T + blki * 512 + 512]
                            e.tensor_mul(dsl_, tsl, bc[:, s + 512:s + 1024]).then_inc(mA.h, 1)
                        else:
                            e.tensor_mul(tsl, tsl, bc[:, s + 512:s + 1024]).then_inc(mA.h, 1)
                            e.wait_ge(mA.h, mA0 + 3 * k + 1)
                            if add_into is None:
                                dsl_ = dst[:, c * T + blki * 512: c * T + blki * 512 + 512]
                                e.tensor_scalar(out=dsl_, in0=tsl, scalar1=vcol(b, gamma, c),
                                                scalar2=vcol(b, beta, c), op0=ALU.mult,
                                                op1=ALU.add).then_inc(mA.h, 2)
                            else:
                                e.tensor_scalar(out=tsl, in0=tsl, scalar1=vcol(b, gamma, c),
                                                scalar2=vcol(b, beta, c), op0=ALU.mult,
                                                op1=ALU.add).then_inc(mA.h, 1)
                                e.wait_ge(mA.h, mA0 + 3 * k + 2)
                                asl = add_into[:, c * T + blki * 512: c * T + blki * 512 + 512]
                                e.tensor_add(asl, asl, tsl).then_inc(mA.h, 1)
        pe.v = pe0 + NB
        ac.v = ac0 + NB
        b.s["po"].v = po0 + NK
        mA.v = mA0 + INCPS * NK


def emit_l1ffn(b: Bld, tag=""):
    nc = b.nc
    pe, ve, ac, po = b.s["pe"], b.s["ve"], b.s["ac"], b.s["po"]
    xT, xnT = b.sb["xT"], b.sb["xnT"]
    wr = b.dr("WrT_bf", [D, D], BF16)
    wk = b.dr("WkT_bf", [D, D], BF16)
    wv = b.dr("WvT_bf", [D, D], BF16)

    def shift_build(dst, mu_name, t2, sub):
        ve0 = ve.v
        with nc.Block(f"l1f_sh{tag}_{sub}") as blk:
            @blk.vector
            def _(e):
                v = ve0
                for c in range(DC):
                    e.tensor_scalar(out=t2[:, c:c + 1], in0=vcol(b, mu_name, c),
                                    scalar1=-1.0, scalar2=1.0, op0=ALU.mult, op1=ALU.add
                                    ).then_inc(ve.h, 1)
                    v += 1
                e.wait_ge(ve.h, v)
                for c in range(DC):
                    e.tensor_scalar_mul(dst[:, c * T:(c + 1) * T],
                                        xnT[:, c * T:(c + 1) * T],
                                        vcol(b, mu_name, c)).then_inc(ve.h, 1)
                    v += 1
                    e.wait_ge(ve.h, v)
                    e.scalar_tensor_tensor(out=dst[:, c * T + 1:(c + 1) * T],
                                           in0=xnT[:, c * T:(c + 1) * T - 1],
                                           scalar=t2[:, c:c + 1],
                                           in1=dst[:, c * T + 1:(c + 1) * T],
                                           op0=ALU.mult, op1=ALU.add).then_inc(ve.h, 1)
                    v += 1
        ve.v = ve0 + 3 * DC

    with nc.sbuf_tensor(f"l1f_a{tag}", [128, DC * T], BF16) as xa, \
         nc.sbuf_tensor(f"l1f_k{tag}", [128, DC * T], BF16) as kact, \
         nc.sbuf_tensor(f"l1f_r{tag}", [128, T], BF16) as rch, \
         nc.sbuf_tensor(f"l1f_w{tag}", [128, 2 * D], BF16) as wsl, \
         nc.sbuf_tensor(f"l1f_t{tag}", [128, 2 * 512], F32) as tmpf, \
         nc.sbuf_tensor(f"l1f_c{tag}", [128, 2 * DC], F32) as omc, \
         nc.psum_tensor(f"l1f_p0{tag}", [128, 512], F32) as q0, \
         nc.psum_tensor(f"l1f_p1{tag}", [128, 512], F32) as q1:
        qs = [q0, q1]
        shift_build(xa, "muk_T", omc, "k")
        dsl = [b.s["dA"], b.s["dB"]]
        d0 = [s.v for s in dsl]
        pe0, ve0, ac0 = pe.v, ve.v, ac.v
        with nc.Block(f"l1f_k{tag}") as blk:
            @blk.sync
            def _(e):
                for o in range(DC):
                    if o >= 2:
                        e.wait_ge(pe.h, pe0 + (o - 1) * NB)
                    e.dma_start(out=wsl[:, (o % 2) * D:(o % 2) * D + D].rearrange("p (c j) -> p c j", c=DC),
                                in_=wk[:, o * 128:(o + 1) * 128].rearrange("(c p) j -> p c j", p=128)
                                ).then_inc(dsl[o % 2].h, 16)

            @blk.tensor
            def _(e):
                for o in range(DC):
                    e.wait_ge(dsl[o % 2].h, d0[o % 2] + 16 * (o // 2 + 1))
                    for blki in range(NB):
                        j = o * NB + blki
                        if j >= 2:
                            e.wait_ge(ve.h, ve0 + j - 1)
                        for c in range(DC):
                            mm = e.matmul(qs[j % 2][:],
                                          wsl[:, (o % 2) * D + c * 128:(o % 2) * D + c * 128 + 128],
                                          xa[:, c * T + blki * 512: c * T + blki * 512 + 512],
                                          start=(c == 0), stop=(c == DC - 1))
                            if c == DC - 1:
                                mm.then_inc(pe.h, 1)

            @blk.vector
            def _(e):
                for j in range(DC * NB):
                    e.wait_ge(pe.h, pe0 + j + 1)
                    if j >= 2:
                        e.wait_ge(ac.h, ac0 + j - 1)
                    e.tensor_scalar_max(tmpf[:, (j % 2) * 512:(j % 2) * 512 + 512],
                                        qs[j % 2][:], 0.0).then_inc(ve.h, 1)

            @blk.scalar
            def _(e):
                for o in range(DC):
                    for blki in range(NB):
                        j = o * NB + blki
                        e.wait_ge(ve.h, ve0 + j + 1)
                        e.activation(out=kact[:, o * T + blki * 512: o * T + blki * 512 + 512],
                                     in_=tmpf[:, (j % 2) * 512:(j % 2) * 512 + 512],
                                     func=AF.Square).then_inc(ac.h, 1)
        pe.v = pe0 + DC * NB
        ve.v = ve0 + DC * NB
        ac.v = ac0 + DC * NB
        for i, s in enumerate(dsl):
            s.v = d0[i] + 16 * (DC // 2)
        shift_build(xa, "mur_T", omc, "r")
        pe0, ve0, ac0, po0 = pe.v, ve.v, ac.v, po.v
        d0 = [s.v for s in dsl]
        with nc.psum_tensor(f"l1f_p2{tag}", [128, 512], F32) as q2, \
             nc.psum_tensor(f"l1f_p3{tag}", [128, 512], F32) as q3, \
             nc.Block(f"l1f_rv{tag}") as blk:
            q_r = [q0, q1]
            q_v = [q2, q3]

            @blk.sync
            def _(e):
                for o in range(DC):
                    if o >= 1:
                        e.wait_ge(pe.h, pe0 + o * 2 * NB)
                    e.dma_start(out=wsl[:, 0:D].rearrange("p (c j) -> p c j", c=DC),
                                in_=wr[:, o * 128:(o + 1) * 128].rearrange("(c p) j -> p c j", p=128)
                                ).then_inc(dsl[0].h, 16)
                    e.dma_start(out=wsl[:, D:2 * D].rearrange("p (c j) -> p c j", c=DC),
                                in_=wv[:, o * 128:(o + 1) * 128].rearrange("(c p) j -> p c j", p=128)
                                ).then_inc(dsl[1].h, 16)

            @blk.tensor
            def _(e):
                for o in range(DC):
                    e.wait_ge(dsl[0].h, d0[0] + 16 * (o + 1))
                    e.wait_ge(dsl[1].h, d0[1] + 16 * (o + 1))
                    for blki in range(NB):
                        jr = o * NB + blki
                        if jr >= 2:
                            e.wait_ge(ac.h, ac0 + jr - 1)
                        for c in range(DC):
                            mm = e.matmul(q_r[blki % 2][:], wsl[:, c * 128:c * 128 + 128],
                                          xa[:, c * T + blki * 512: c * T + blki * 512 + 512],
                                          start=(c == 0), stop=(c == DC - 1))
                            if c == DC - 1:
                                mm.then_inc(pe.h, 1)
                        jv = jr
                        if jv >= 2:
                            e.wait_ge(ve.h, ve0 + jv - 1)
                        for c in range(DC):
                            mm = e.matmul(q_v[blki % 2][:], wsl[:, D + c * 128:D + c * 128 + 128],
                                          kact[:, c * T + blki * 512: c * T + blki * 512 + 512],
                                          start=(c == 0), stop=(c == DC - 1))
                            if c == DC - 1:
                                mm.then_inc(pe.h, 1)

            @blk.scalar
            def _(e):
                for o in range(DC):
                    for blki in range(NB):
                        j = o * NB + blki
                        e.wait_ge(pe.h, pe0 + 2 * j + 1)
                        if o >= 1:
                            e.wait_ge(ve.h, ve0 + j - NB + 1)
                        e.activation(out=rch[:, blki * 512:(blki + 1) * 512],
                                     in_=q_r[blki % 2][:], func=AF.Sigmoid).then_inc(ac.h, 1)

            @blk.vector
            def _(e):
                for o in range(DC):
                    for blki in range(NB):
                        j = o * NB + blki
                        e.wait_ge(pe.h, pe0 + 2 * j + 2)
                        e.wait_ge(ac.h, ac0 + j + 1)
                        if j >= 2:
                            e.wait_ge(po.h, po0 + j - 1)
                        e.tensor_mul(tmpf[:, (j % 2) * 512:(j % 2) * 512 + 512],
                                     rch[:, blki * 512:(blki + 1) * 512],
                                     q_v[blki % 2][:]).then_inc(ve.h, 1)

            @blk.gpsimd
            def _(e):
                for o in range(DC):
                    for blki in range(NB):
                        j = o * NB + blki
                        e.wait_ge(ve.h, ve0 + j + 1)
                        xsl_ = xT[:, o * T + blki * 512: o * T + blki * 512 + 512]
                        e.tensor_add(xsl_, xsl_, tmpf[:, (j % 2) * 512:(j % 2) * 512 + 512]
                                     ).then_inc(po.h, 1)
        pe.v = pe0 + 2 * DC * NB
        ac.v = ac0 + DC * NB
        ve.v = ve0 + DC * NB
        po.v = po0 + DC * NB
        for i, s in enumerate(dsl):
            s.v = d0[i] + 16 * DC


ENG_SEM = {"tensor": "pe", "vector": "ve", "scalar": "ac", "gpsimd": "po"}


class ES:
    def __init__(self, b: Bld):
        self.b = b
        self.evs = {k: [] for k in ["sync", "tensor", "vector", "scalar", "gpsimd"]}

    def ev(self, eng, waits, fn, sem=None, inc=None):
        b = self.b
        if sem is None:
            sem = ENG_SEM[eng]
        if inc is None:
            inc = 16 if eng == "sync" else 1
        s = b.s[sem]
        s.v += inc
        h = (s.h, s.v)
        self.evs[eng].append((list(waits), fn, s.h, inc))
        return h

    def emit(self, name):
        nc = self.b.nc
        dma_final = {}
        for waits, fn, semh, inc in self.evs["sync"]:
            dma_final[id(semh)] = (semh, None)
        for waits, fn, semh, inc in self.evs["gpsimd"]:
            if inc >= 16:
                dma_final[id(semh)] = (semh, None)
        finals = {}
        for nm in ["dA", "dB", "dC", "dD", "dE", "dF", "dH", "dI", "dw", "dg", "ds"]:
            s = self.b.s[nm]
            if id(s.h) in dma_final:
                finals[nm] = (s.h, s.v)
        with nc.Block(name) as blk:
            need_gp_final = any(nm in ("dH", "dI", "dg") for nm in finals)
            for eng in ["sync", "tensor", "vector", "scalar", "gpsimd"]:
                evl = self.evs[eng]
                if not evl and not (eng == "sync" and finals) and not (eng == "gpsimd" and need_gp_final):
                    continue

                def body(e, evl=evl, eng=eng):
                    for waits, fn, semh, inc in evl:
                        for (h, v) in waits:
                            e.wait_ge(h, v)
                        last = fn(e)
                        assert last is not None, "event fn must return its last instruction"
                        last.then_inc(semh, inc)
                    if eng == "sync":
                        for nm, (h, v) in finals.items():
                            if nm not in ("dH", "dI", "dg"):
                                e.wait_ge(h, v)
                    elif eng == "gpsimd":
                        for nm, (h, v) in finals.items():
                            if nm in ("dH", "dI", "dg"):
                                e.wait_ge(h, v)
                getattr(blk, eng)(body)


def emit_l1attn(b: Bld, tag=""):
    nc = b.nc
    xT, xnT = b.sb["xT"], b.sb["xnT"]
    identb = b.sb["identb"]
    ones_bf, onesr_f32, onesr_bf = b.sb["ones_bf"], b.sb["onesr_f32"], b.sb["onesr_bf"]
    triTT = b.sb["triTT"]
    neg30r_bf = b.sb["neg30r_bf"]
    wq = b.dr("wqkv_bf", [D, 3 * D], BF16)
    wp = b.dr("wproj_bf", [D, D], BF16)

    lrow = b.sb["f_row"]
    zctx = ExitStack()
    qk = zctx.enter_context(nc.sbuf_tensor(f"laqk{tag}", [128, 8 * T], BF16))
    zT = qk
    with nc.sbuf_tensor(f"lao{tag}", [128, DC * T], BF16) as aoT, \
         nc.sbuf_tensor(f"lbr{tag}", [1, 64], F32) as brow, \
         nc.sbuf_tensor(f"lbc{tag}", [128, 16], F32) as nBbc, \
         nc.sbuf_tensor(f"lwsl{tag}", [128, 2 * D], BF16) as wsl, \
         nc.sbuf_tensor(f"lET{tag}", [128, 2 * 512], BF16) as ET, \
         nc.sbuf_tensor(f"lbcs{tag}", [128, 512], F32) as bcs:
        with nc.sbuf_tensor(f"lav{tag}", [128, 16 * 512], BF16) as vtok:
            qT = qk[:, 0:4 * T]
            kT = qk[:, 4 * T:8 * T]
            for g in range(2):
                sqt = aoT[:, 4 * T * g: 4 * T * g + 2 * T]
                es = ES(b)
                with nc.psum_tensor(f"lap0{tag}{g}", [128, 512], F32) as pq0, \
                     nc.psum_tensor(f"lap1{tag}{g}", [128, 512], F32) as pq1:
                    pqs = [pq0, pq1]
                    wload = [None] * 10
                    mmdone = {}
                    evdone = {}
                    for j in range(8):
                        col = (0 if j < 4 else D) + g * 512 + (j % 4) * 128
                        deps = [mmdone[(j - 2, NB - 1)]] if j >= 2 else []

                        def ld(e, j=j, col=col):
                            return e.dma_start(
                                out=wsl[:, (j % 2) * D:(j % 2) * D + D].rearrange("p (c j) -> p c j", c=DC),
                                in_=wq[:, col:col + 128].rearrange("(c p) j -> p c j", p=128))
                        wload[j] = es.ev("sync", deps, ld, sem="dA" if j % 2 == 0 else "dB", inc=16)
                        for blki in range(NB):
                            jj = j * NB + blki
                            deps2 = [wload[j]]
                            if jj >= 2:
                                deps2.append(evdone[jj - 2])

                            def mm(e, j=j, blki=blki, jj=jj):
                                r = None
                                for c in range(DC):
                                    r = e.matmul(pqs[jj % 2][:],
                                                 wsl[:, (j % 2) * D + c * 128:(j % 2) * D + c * 128 + 128],
                                                 xnT[:, c * T + blki * 512: c * T + blki * 512 + 512],
                                                 start=(c == 0), stop=(c == DC - 1))
                                return r
                            mmdone[(j, blki)] = es.ev("tensor", deps2, mm)

                            def ev_(e, j=j, blki=blki, jj=jj):
                                dstq = qT if j < 4 else kT
                                return e.tensor_copy(
                                    out=dstq[:, (j % 4) * T + blki * 512:(j % 4) * T + blki * 512 + 512],
                                    in_=pqs[jj % 2][:])
                            evdone[jj] = es.ev("vector", [mmdone[(j, blki)]], ev_)
                    for hv in range(2):
                        depsv = [mmdone[(7, NB - 1)]] if hv == 0 else [evdone[32 + 16 * hv - 1]]

                        def ldv(e, hv=hv):
                            return e.dma_start(
                                out=wsl[:, 0:2 * D].rearrange("p (c j) -> p c j", c=DC),
                                in_=wq[:, 2 * D + g * 512 + hv * 256:2 * D + g * 512 + hv * 256 + 256
                                       ].rearrange("(c p) j -> p c j", p=128))
                        wload[8 + hv] = es.ev("sync", depsv, ldv, sem="dA" if hv == 0 else "dB", inc=16)
                        for ti in range(NT):
                            jj = 32 + 16 * hv + ti
                            deps3 = [wload[8 + hv]]
                            if jj >= 2:
                                deps3.append(evdone[jj - 2])

                            def mmv(e, ti=ti, jj=jj):
                                r = None
                                for c in range(DC):
                                    r = e.matmul(pqs[jj % 2][0:128, 0:256],
                                                 xnT[:, c * T + ti * 128: c * T + (ti + 1) * 128],
                                                 wsl[:, c * 256:c * 256 + 256],
                                                 start=(c == 0), stop=(c == DC - 1))
                                return r
                            h_mm = es.ev("tensor", deps3, mmv)

                            def evv(e, ti=ti, jj=jj, hv=hv):
                                return e.tensor_copy(out=vtok[:, ti * 512 + hv * 256:ti * 512 + hv * 256 + 256],
                                                     in_=pqs[jj % 2][0:128, 0:256])
                            evdone[jj] = es.ev("vector", [h_mm], evv)
                    es.emit(f"laqkv{tag}{g}")

                es = ES(b)
                with nc.psum_tensor(f"lapB{tag}{g}", [1, 512], F32) as pB0, \
                     nc.psum_tensor(f"lapBb{tag}{g}", [1, 512], F32) as pB1:
                    pBs = [pB0, pB1]
                    sqdone = {}
                    grpdone = {}
                    red = {}
                    for j in range(8):
                        deps = [] if j < 2 else [grpdone[2 * (j - 2) + 1]]

                        def sqf(e, j=j):
                            srcq = qT if j < 4 else kT
                            return e.activation(out=sqt[:, (j % 2) * T:(j % 2) * T + T],
                                                in_=srcq[:, (j % 4) * T:(j % 4) * T + T], func=AF.Square)
                        sqdone[j] = es.ev("scalar", deps, sqf)
                        for half in range(2):
                            gi = 2 * j + half
                            col = (0 if j < 4 else 16) + 8 * g + (j % 4) * 2 + half
                            deps2 = [sqdone[j]]
                            if gi >= 2:
                                deps2.append(red[gi - 2])

                            def mmb(e, j=j, half=half, gi=gi):
                                r = None
                                for blki in range(NB):
                                    r = e.matmul(pBs[gi % 2][:],
                                                 ones_bf[half * 64:half * 64 + 64, 0:1],
                                                 sqt[half * 64:half * 64 + 64,
                                                     (j % 2) * T + blki * 512:(j % 2) * T + blki * 512 + 512],
                                                 start=(blki == 0), stop=(blki == NB - 1))
                                return r
                            grpdone[gi] = es.ev("tensor", deps2, mmb)

                            def redf(e, gi=gi, col=col):
                                return e.reduce_sum(out=brow[:, col:col + 1],
                                                    in_=pBs[gi % 2][:], axis=AXX)
                            red[gi] = es.ev("vector", [grpdone[gi]], redf)
                    es.emit(f"laB{tag}{g}")
                es = ES(b)
                with nc.psum_tensor(f"lapN{tag}{g}", [128, 8], F32) as pN:
                    q0c, k0c, n0c, s0c = 8 * g, 16 + 8 * g, 32 + 8 * g, 48 + 8 * g
                    h1 = es.ev("vector", [], lambda e: e.tensor_mul(
                        brow[:, n0c:n0c + 8], brow[:, q0c:q0c + 8], brow[:, k0c:k0c + 8]))
                    h2 = es.ev("scalar", [h1], lambda e: e.activation(
                        out=brow[:, s0c:s0c + 8], in_=brow[:, n0c:n0c + 8], func=AF.Sqrt))
                    h3 = es.ev("vector", [h2], lambda e: e.tensor_scalar_mul(
                        brow[:, n0c:n0c + 8], brow[:, s0c:s0c + 8], -1.0 / (2.0 * T)))
                    h4 = es.ev("tensor", [h3], lambda e: e.matmul(
                        pN[:], onesr_f32[:, 0:128], brow[:, n0c:n0c + 8], start=True, stop=True))
                    h5 = es.ev("scalar", [h4], lambda e: e.copy(out=nBbc[:, 8 * g:8 * g + 8], in_=pN[:]))
                    es.emit(f"laNB{tag}{g}")
                es = ES(b)
                with nc.psum_tensor(f"laS0{tag}{g}", [128, 512], F32) as pS0, \
                     nc.psum_tensor(f"laS1{tag}{g}", [128, 512], F32) as pS1, \
                     nc.psum_tensor(f"laO{tag}{g}", [64, 512], F32) as pO, \
                     nc.psum_tensor(f"laL{tag}{g}", [1, 512], F32) as pL, \
                     nc.psum_tensor(f"laNb{tag}{g}", [64, 512], F32) as pNb:
                    pSs = [pS0, pS1]
                    normdone = {}
                    exp_hist = []
                    for hl in range(8):
                        h = 8 * g + hl
                        qc = hl // 2
                        h2 = hl % 2
                        for tqb in range(NB):
                            ntsj = 4 * tqb + 4
                            last = None
                            for tsj in range(ntsj):
                                deps = []
                                if len(exp_hist) >= 2:
                                    deps.append(exp_hist[-2])

                                def mmS(e, qc=qc, h2=h2, tqb=tqb, tsj=tsj):
                                    c0 = 128 * tsj - 512 * tqb
                                    has_mask = 0 <= c0 <= 384
                                    r = e.matmul(pSs[tsj % 2][:],
                                                 kT[h2 * 64:h2 * 64 + 64, qc * T + tsj * 128: qc * T + (tsj + 1) * 128],
                                                 qT[h2 * 64:h2 * 64 + 64, qc * T + tqb * 512: qc * T + tqb * 512 + 512],
                                                 start=True, stop=not has_mask)
                                    if has_mask:
                                        if c0 > 0:
                                            r = e.matmul(pSs[tsj % 2][:, 0:c0], neg30r_bf[:],
                                                         onesr_bf[:, 0:c0], start=False, stop=False)
                                        r = e.matmul(pSs[tsj % 2][:, c0:c0 + 128], triTT[:], identb[:],
                                                     start=False, stop=True)
                                    return r
                                hS = es.ev("tensor", deps, mmS)

                                def exf(e, h=h, tsj=tsj):
                                    return e.activation(out=ET[:, (tsj % 2) * 512:(tsj % 2) * 512 + 512],
                                                        in_=pSs[tsj % 2][:], func=AF.Exp, scale=0.125,
                                                        bias=nBbc[:, h:h + 1])
                                eh = es.ev("scalar", [hS], exf)
                                exp_hist.append(eh)

                                dep2 = [eh]
                                if (h, tqb - 1) in normdone:
                                    dep2.append(normdone[(h, tqb - 1)])
                                elif tqb == 0 and (h - 1, NB - 1) in normdone:
                                    dep2.append(normdone[(h - 1, NB - 1)])

                                def mmOL(e, hl=hl, tqb=tqb, tsj=tsj):
                                    e.matmul(pO[:], vtok[:, tsj * 512 + 64 * hl: tsj * 512 + 64 * hl + 64],
                                             ET[:, (tsj % 2) * 512:(tsj % 2) * 512 + 512],
                                             start=(tsj == 0), stop=(tsj == 4 * tqb + 3))
                                    return e.matmul(pL[:], ones_bf[:, 0:1],
                                                    ET[:, (tsj % 2) * 512:(tsj % 2) * 512 + 512],
                                                    start=(tsj == 0), stop=(tsj == 4 * tqb + 3))
                                last = es.ev("tensor", dep2, mmOL)
                            hln0 = es.ev("scalar", [last], lambda e: e.activation(
                                out=lrow[:, 0:512], in_=pL[:], func=AF.Ln))
                            hln = es.ev("scalar", [hln0], lambda e: e.activation(
                                out=lrow[:, 512:1024], in_=lrow[:, 0:512], func=AF.Exp, scale=-1.0))
                            hbc = es.ev("tensor", [hln], lambda e: e.matmul(
                                pNb[:], onesr_f32[:, 0:64], lrow[:, 512:1024], start=True, stop=True))
                            hcp = es.ev("scalar", [hbc], lambda e: e.copy(out=bcs[0:64, 0:512], in_=pNb[:]))

                            def nrm(e, h2=h2, hl=hl, g=g, tqb=tqb):
                                return e.tensor_mul(
                                    aoT[h2 * 64:h2 * 64 + 64, (hl // 2 + 4 * g) * T + tqb * 512:
                                        (hl // 2 + 4 * g) * T + tqb * 512 + 512],
                                    pO[:], bcs[0:64, 0:512])
                            normdone[(h, tqb)] = es.ev("vector", [hcp, last], nrm)
                es.emit(f"lafl{tag}{g}")
        es = ES(b)
        with nc.psum_tensor(f"laP0{tag}", [128, 512], F32) as pP0, \
             nc.psum_tensor(f"laP1{tag}", [128, 512], F32) as pP1:
            pPs = [pP0, pP1]
            wld = {}
            mmd = {}
            evd = {}
            for o in range(DC):
                deps = [mmd[(o - 2, NB - 1)]] if o >= 2 else []

                def ldp(e, o=o):
                    return e.dma_start(
                        out=wsl[:, (o % 2) * D:(o % 2) * D + D].rearrange("p (c j) -> p c j", c=DC),
                        in_=wp[:, o * 128:(o + 1) * 128].rearrange("(c p) j -> p c j", p=128))
                wld[o] = es.ev("sync", deps, ldp, sem="dA" if o % 2 == 0 else "dB", inc=16)
                for blki in range(NB):
                    jj = o * NB + blki
                    deps2 = [wld[o]]
                    if jj >= 2:
                        deps2.append(evd[jj - 2])

                    def mmp(e, o=o, blki=blki, jj=jj):
                        r = None
                        for c in range(DC):
                            r = e.matmul(pPs[jj % 2][:],
                                         wsl[:, (o % 2) * D + c * 128:(o % 2) * D + c * 128 + 128],
                                         aoT[:, c * T + blki * 512: c * T + blki * 512 + 512],
                                         start=(c == 0), stop=(c == DC - 1))
                        return r
                    mmd[(o, blki)] = es.ev("tensor", deps2, mmp)

                    def evp(e, o=o, blki=blki, jj=jj):
                        return e.activation(out=zT[:, o * T + blki * 512: o * T + blki * 512 + 512],
                                            in_=pPs[jj % 2][:], func=AF.Identity,
                                            bias=vcol(b, "bproj_T", o))
                    hz = es.ev("scalar", [mmd[(o, blki)]], evp)

                    def addxn(e, o=o, blki=blki):
                        sl = slice(o * T + blki * 512, o * T + blki * 512 + 512)
                        return e.tensor_add(zT[:, sl], zT[:, sl], xnT[:, sl])
                    evd[jj] = es.ev("vector", [hz], addxn)
            es.emit(f"laproj{tag}")
    emit_ln(b, zT, None, False, False, gamma="lngam_T", beta="lnbet_T",
            add_into=xT, tag=f"li{tag}")
    zctx.close()


def emit_seprep(b: Bld):
    nc = b.nc
    identb = b.sb["identb"]
    seF = b.ein("se_flat")
    seT = b.dr("seT_bf", [D, NSLOT], BF16)
    es = ES(b)
    with nc.sbuf_tensor("sp_in", [128, 2 * D], F32) as xin, \
         nc.sbuf_tensor("sp_sq", [128, D], F32) as sqv, \
         nc.sbuf_tensor("sp_c", [128, 8], F32) as cols, \
         nc.sbuf_tensor("sp_nb", [128, 2 * D], BF16) as nb, \
         nc.sbuf_tensor("sp_st", [128, 2 * D], BF16) as strip, \
         nc.psum_tensor("sp_p0", [128, 128], BF16) as tp0, \
         nc.psum_tensor("sp_p1", [128, 128], BF16) as tp1:
        tps = [tp0, tp1]
        lddone = {}
        nrmdone = {}
        trdone = {}
        evdone = {}
        stdone = {}
        for i in range(NSLOT // 128):
            sl = (i % 2) * D

            def ld(e, i=i, sl=sl):
                return e.dma_start(out=xin[:, sl:sl + D], in_=seF[i * 128:(i + 1) * 128, :])
            deps = [nrmdone[i - 2]] if i >= 2 else []
            lddone[i] = es.ev("sync", deps, ld, sem="dA" if i % 2 == 0 else "dB", inc=16)

            def sq(e, i=i, sl=sl):
                return e.activation(out=sqv[:], in_=xin[:, sl:sl + D], func=AF.Square)
            hsq = es.ev("scalar", [lddone[i]] + ([nrmdone[i - 1]] if i >= 1 else []), sq)

            def red(e, i=i):
                return e.reduce_sum(out=cols[:, 0:1], in_=sqv[:], axis=AXX)
            hred = es.ev("vector", [hsq], red)
            hsq2 = es.ev("scalar", [hred], lambda e: e.activation(out=cols[:, 1:2], in_=cols[:, 0:1], func=AF.Sqrt))
            hmx = es.ev("vector", [hsq2], lambda e: e.tensor_scalar_max(cols[:, 2:3], cols[:, 1:2], 1e-12))
            hrc = es.ev("vector", [hmx], lambda e: e.reciprocal(out=cols[:, 3:4], in_=cols[:, 2:3]))
            hsc = es.ev("vector", [hrc], lambda e: e.tensor_scalar_mul(cols[:, 4:5], cols[:, 3:4], float(np.sqrt(D))))

            def nrm(e, i=i, sl=sl):
                return e.tensor_scalar_mul(nb[:, sl:sl + D], xin[:, sl:sl + D], cols[:, 4:5])
            nrmdone[i] = es.ev("vector", [hsc], nrm)
            for c in range(DC):
                k = i * DC + c
                deps = [nrmdone[i]]
                if k >= 2:
                    deps.append(evdone[k - 2])

                def tr(e, i=i, c=c, k=k, sl=sl):
                    return e.transpose(tps[k % 2][:], nb[:, sl + c * 128:sl + c * 128 + 128], identb[:])
                trdone[k] = es.ev("tensor", deps, tr)

                def ev_(e, i=i, c=c, k=k, sl=sl):
                    return e.tensor_scalar_mul(strip[:, sl + c * 128:sl + c * 128 + 128],
                                               tps[k % 2][:], vcol(b, "gamma_slot_T", c))
                deps = [trdone[k]]
                if i >= 2:
                    deps.append(stdone[i - 2])
                evdone[k] = es.ev("vector", deps, ev_)

            def st(e, i=i, sl=sl):
                return e.dma_start(
                    out=seT.rearrange("(c p) n -> p c n", p=128)[:, :, i * 128:(i + 1) * 128],
                    in_=strip[:, sl:sl + D].rearrange("p (c j) -> p c j", c=DC))
            stdone[i] = es.ev("sync", [evdone[i * DC + DC - 1]], st,
                              sem="dE" if i % 2 == 0 else "dF", inc=16)
        es.emit("seprep")


def emit_l0ffn(b: Bld, tag=""):
    nc = b.nc
    xT, xnT = b.sb["xT"], b.sb["xnT"]
    identb = b.sb["identb"]
    ones_bf, onesr_f32 = b.sb["ones_bf"], b.sb["onesr_f32"]
    f_r = b.sb["f_row"][:]
    seT = b.dr("seT_bf", [D, NSLOT], BF16)
    w1d = b.dr("w1f_bf", [D, HIDF], BF16)
    w2d = b.dr("w2f_bf", [HIDF, D], BF16)
    Lscr = b.dr("l0_Lscr", [T, NSLOT], F32)
    Escr = b.dr("l0_Escr", [NSLOT, T], BF16)
    Pscr = b.dr("l0_proc", [NSLOT, D], BF16)
    RSscr = b.dr("l0_rs", [1, T + 64], F32)

    es = ES(b)
    with nc.sbuf_tensor(f"l0fa{tag}", [128, DC * T], BF16) as xtok:
        actx = ExitStack()
        pbs = [actx.enter_context(nc.psum_tensor(f"l0fp{i}{tag}", [128, 512], F32)) for i in range(2)]
        tqs = [actx.enter_context(nc.psum_tensor(f"l0ft{i}{tag}", [128, 128], BF16)) for i in range(4)]
        bcev = {}
        cpev = {}
        xnrev = {}
        trev = {}
        tevv = {}
        with nc.sbuf_tensor(f"l0fbc{tag}", [128, 2 * 512], F32) as fbc:
            for blki in range(NB):
                deps = [cpev[blki - 2]] if blki >= 2 else []

                def bc(e, blki=blki):
                    return e.matmul(pbs[blki % 2][:], onesr_f32[:, 0:128],
                                    f_r[:, blki * 512:(blki + 1) * 512], start=True, stop=True)
                bcev[blki] = es.ev("tensor", deps, bc)
                deps = [bcev[blki]]
                if blki >= 2:
                    deps.append(xnrev[(blki - 2, DC - 1)])

                def cp(e, blki=blki):
                    return e.copy(out=fbc[:, (blki % 2) * 512:(blki % 2) * 512 + 512],
                                  in_=pbs[blki % 2][:])
                cpev[blki] = es.ev("scalar", deps, cp)
                for c in range(DC):
                    def xr1(e, blki=blki, c=c):
                        sl = slice(c * T + blki * 512, c * T + blki * 512 + 512)
                        return e.tensor_mul(xnT[:, sl], xnT[:, sl],
                                            fbc[:, (blki % 2) * 512:(blki % 2) * 512 + 512])
                    h1 = es.ev("vector", [cpev[blki]], xr1)

                    def xr2(e, blki=blki, c=c):
                        sl = slice(c * T + blki * 512, c * T + blki * 512 + 512)
                        return e.tensor_scalar_mul(xnT[:, sl], xnT[:, sl], vcol(b, "gamma_x_T", c))
                    xnrev[(blki, c)] = es.ev("vector", [h1], xr2)
            for i in range(NT):
                blki = i // 4
                for c in range(DC):
                    k = i * DC + c
                    deps = [xnrev[(blki, c)]]
                    if k >= 4:
                        deps.append(tevv[k - 4])

                    def tr(e, i=i, c=c, k=k):
                        return e.transpose(tqs[k % 4][:], xnT[:, c * T + i * 128: c * T + (i + 1) * 128],
                                           identb[:])
                    trev[k] = es.ev("tensor", [d for d in deps], tr)

                    def tv(e, i=i, c=c, k=k):
                        return e.tensor_copy(out=xtok[:, i * D + c * 128: i * D + c * 128 + 128],
                                             in_=tqs[k % 4][:])
                    tevv[k] = es.ev("vector", [trev[k]], tv)
            es.emit(f"l0fA{tag}")
        actx.close()

        es = ES(b)
        NCH = 8
        with nc.sbuf_tensor(f"l0fse{tag}", [128, 4096], BF16) as sesl, \
             nc.sbuf_tensor(f"l0fls{tag}", [128, 2 * 512], F32) as lsl, \
             nc.sbuf_tensor(f"l0fmc{tag}", [128, 64], F32) as mc, \
             nc.psum_tensor(f"l0fL0{tag}", [128, 512], F32) as pL0, \
             nc.psum_tensor(f"l0fL1{tag}", [128, 512], F32) as pL1, \
             nc.psum_tensor(f"l0fT0{tag}", [128, 128], F32) as pt0, \
             nc.psum_tensor(f"l0fT1{tag}", [128, 128], F32) as pt1:
            pLs = [pL0, pL1]
            pts = [pt0, pt1]
            ldse = {}
            mmL = {}
            evL = {}
            stL = {}
            redm = {}
            trE = {}
            redc = {}
            mxc = {}
            ini = es.ev("vector", [], lambda e: e.memset(mc[:], -1e30))
            for nch in range(NCH):
                deps = [mmL[(nch - 1, NT - 1)]] if nch >= 1 else []

                def lse(e, nch=nch):
                    return e.dma_start(
                        out=sesl[:].rearrange("p (c j) -> p c j", c=DC),
                        in_=seT[:, nch * 512:(nch + 1) * 512].rearrange("(c p) j -> p c j", p=128))
                ldse[nch] = es.ev("sync", deps, lse, sem="dA" if nch % 2 == 0 else "dB", inc=16)
                for i in range(NT):
                    j = nch * NT + i
                    deps = [ldse[nch], ini]
                    if j >= 2:
                        deps.append(evL[j - 2])

                    def mm(e, nch=nch, i=i, j=j):
                        r = None
                        for c in range(DC):
                            r = e.matmul(pLs[j % 2][:],
                                         xnT[:, c * T + i * 128: c * T + (i + 1) * 128],
                                         sesl[:, c * 512:(c + 1) * 512],
                                         start=(c == 0), stop=(c == DC - 1))
                        return r
                    mmL[(nch, i)] = es.ev("tensor", [d for d in deps], mm)

                    def rm1(e, nch=nch, i=i, j=j):
                        return e.reduce_max(out=mc[:, 48:49], in_=pLs[j % 2][:], axis=AXX)
                    rmh = es.ev("vector", [mmL[(nch, i)]] + ([redm[j - 1]] if j >= 1 else []), rm1)

                    def rm2(e, i=i):
                        return e.tensor_max(mc[:, i:i + 1], mc[:, i:i + 1], mc[:, 48:49])
                    redm[j] = es.ev("vector", [rmh], rm2)

                    def evc(e, nch=nch, i=i, j=j):
                        return e.activation(out=lsl[:, (j % 2) * 512:(j % 2) * 512 + 512],
                                            in_=pLs[j % 2][:], func=AF.Identity)
                    deps = [redm[j]]
                    if j >= 2:
                        deps.append(stL[j - 2])
                        deps.append(trE[(j - 2, 3)])
                    evL[j] = es.ev("scalar", [d for d in deps], evc)

                    def st(e, nch=nch, i=i, j=j):
                        return e.dma_start(
                            out=Lscr[i * 128:(i + 1) * 128, nch * 512:(nch + 1) * 512],
                            in_=lsl[:, (j % 2) * 512:(j % 2) * 512 + 512])
                    stL[j] = es.ev("sync", [evL[j]], st, sem="dE" if j % 2 == 0 else "dF", inc=16)
                    for sub in range(4):
                        kk = j * 4 + sub
                        deps = [evL[j]]
                        if kk >= 2:
                            deps.append(redc[kk - 2])

                        def tr(e, j=j, sub=sub, kk=kk):
                            return e.transpose(pts[kk % 2][:],
                                               lsl[:, (j % 2) * 512 + sub * 128:(j % 2) * 512 + sub * 128 + 128],
                                               b.sb["identf"][:])
                        trE[(j, sub)] = es.ev("tensor", [d for d in deps], tr)

                        def rc(e, nch=nch, sub=sub, kk=kk):
                            return e.reduce_max(out=mc[:, 48 + 2 + (kk % 2):48 + 3 + (kk % 2)],
                                                in_=pts[kk % 2][:], axis=AXX)
                        rdeps = [trE[(j, sub)]]
                        if kk >= 2:
                            rdeps.append(mxc[kk - 2])
                        redc[kk] = es.ev("vector", rdeps, rc)

                        def mx(e, nch=nch, sub=sub, kk=kk):
                            col = 16 + nch * 4 + sub
                            return e.tensor_max(mc[:, col:col + 1], mc[:, col:col + 1],
                                                mc[:, 48 + 2 + (kk % 2):48 + 3 + (kk % 2)])
                        mxc[kk] = es.ev("vector", [redc[kk]] + ([mxc[kk - 1]] if kk >= 1 else []), mx)
            def negm(e):
                return e.tensor_scalar_mul(mc[:, 0:16], mc[:, 0:16], -1.0)
            fin2 = es.ev("vector", [redm[NCH * NT - 1], mxc[NCH * NT * 4 - 1]], negm)
            with nc.psum_tensor(f"l0fTc{tag}", [128, 128], F32) as ptc:
                cmx = b.sbt("l0_cmx", [32, 128], F32)
                trc = es.ev("tensor", [fin2], lambda e: e.transpose(
                    ptc[0:32, :], mc[:, 16:48], b.sb["identf"][:]))
                evc2 = es.ev("vector", [trc], lambda e: e.tensor_copy(out=cmx[:], in_=ptc[0:32, :]))
                stcmh = es.ev("sync", [evc2], lambda e: e.dma_start(
                    out=b.dr("l0_cmrow", [1, NSLOT], F32).rearrange("o (c j) -> c j", c=32),
                    in_=cmx[:]), sem="dE", inc=16)
            es.emit(f"l0fB{tag}")

            mcP = b.sbt("l0_mc", [128, 64], F32)
            es = ES(b)
            hcpy = es.ev("vector", [], lambda e: e.tensor_copy(out=mcP[:], in_=mc[:]))
            es.ev("vector", [hcpy], lambda e: e.memset(mcP[:, 16:32], 0.0))
            es.emit(f"l0fBc{tag}")

        EdStrip = xnT[:, 0:16 * 512]
        hcache = xnT[:, 0:32 * 512]
        cmrow_d = b.dr("l0_cmrow", [1, NSLOT], F32)
        es = ES(b)
        octx = ExitStack()
        if True:
            cmbc = octx.enter_context(nc.sbuf_tensor(f"l0cmb{tag}", [128, NSLOT], F32))
            pctx = ExitStack()
            cmr = pctx.enter_context(nc.sbuf_tensor(f"l0cmr{tag}", [1, NSLOT], F32))
            pc0 = pctx.enter_context(nc.psum_tensor(f"l0pc0{tag}", [128, 512], F32))
            pc1 = pctx.enter_context(nc.psum_tensor(f"l0pc1{tag}", [128, 512], F32))
            ld = es.ev("sync", [], lambda e: e.dma_start(out=cmr[:], in_=cmrow_d),
                       sem="dA", inc=16)
            prev = {}
            for nch2 in range(8):
                deps = [ld] + ([prev[nch2 - 2]] if nch2 >= 2 else [])

                def bcc(e, nch2=nch2):
                    return e.matmul(pc0[:] if nch2 % 2 == 0 else pc1[:], onesr_f32[:, 0:128],
                                    cmr[:, nch2 * 512:(nch2 + 1) * 512], start=True, stop=True)
                h1 = es.ev("tensor", deps, bcc)

                def cpc(e, nch2=nch2):
                    return e.copy(out=cmbc[:, nch2 * 512:(nch2 + 1) * 512],
                                  in_=pc0[:] if nch2 % 2 == 0 else pc1[:])
                prev[nch2] = es.ev("scalar", [h1], cpc)
            es.emit(f"l0fCbc{tag}")
            pctx.close()

            lbuf = octx.enter_context(nc.sbuf_tensor(f"l0led{tag}", [128, 4 * 512], F32))
            ebuf = octx.enter_context(nc.sbuf_tensor(f"l0esl{tag}", [128, 2 * 512], BF16))
            epc = octx.enter_context(nc.sbuf_tensor(f"l0epc{tag}", [128, 2 * 512], BF16))
            slostr = octx.enter_context(nc.sbuf_tensor(f"l0slo{tag}", [128, DC * 512], BF16))
            w1s = octx.enter_context(nc.sbuf_tensor(f"l0w1s{tag}", [128, 2 * D], BF16))
            w2s = octx.enter_context(nc.sbuf_tensor(f"l0w2s{tag}", [128, 2 * D], BF16))
            rrow = octx.enter_context(nc.sbuf_tensor(f"l0rr{tag}", [1, 2 * 512], F32))
            bcs = octx.enter_context(nc.sbuf_tensor(f"l0bcs{tag}", [128, 512], F32))
            if True:
                for et in range(8):
                    es = ES(b)
                    cctx = ExitStack()
                    if True:
                        psl = [cctx.enter_context(nc.psum_tensor(f"l0s{i}{tag}{et}", [128, 512], F32))
                               for i in range(6)]
                        ps0, ps1 = psl[0], psl[1]
                        pcs = cctx.enter_context(nc.psum_tensor(f"l0cs{tag}{et}", [128, 512], F32))
                        ptE = cctx.enter_context(nc.psum_tensor(f"l0tE{tag}{et}", [128, 128], BF16))
                        ldL = {}
                        edv = {}
                        eds = {}
                        mmS = {}
                        esub = {}
                        etr = {}
                        eev = {}
                        est = {}
                        rsum = {}
                        radd = {}
                        for i in range(NT):
                            deps = [eds[i - 2]] if i >= 2 else []

                            def lL(e, i=i, et=et):
                                return e.dma_start(out=lbuf[:, (i % 2) * 512:(i % 2) * 512 + 512],
                                                   in_=Lscr[i * 128:(i + 1) * 128, et * 512:(et + 1) * 512])
                            ldL[i] = es.ev("sync", deps, lL, sem="dA" if i % 2 == 0 else "dB", inc=16)
                            deps = [ldL[i]]
                            if i >= 1:
                                deps.append(edv[i - 1])

                            def sub(e, i=i, et=et):
                                return e.tensor_sub(lbuf[:, 2 * 512 + (i % 2) * 512:2 * 512 + (i % 2) * 512 + 512],
                                                    lbuf[:, (i % 2) * 512:(i % 2) * 512 + 512],
                                                    cmbc[:, et * 512:(et + 1) * 512])
                            hsub = es.ev("vector", deps, sub)

                            def edx(e, i=i):
                                return e.activation(out=EdStrip[:, i * 512:(i + 1) * 512],
                                                    in_=lbuf[:, 2 * 512 + (i % 2) * 512:2 * 512 + (i % 2) * 512 + 512],
                                                    func=AF.Exp)
                            edv[i] = es.ev("scalar", [hsub], edx)
                            deps = [ldL[i]]
                            if i >= 2:
                                deps.append(etr[(i - 2, 3)])
                                deps.append(rsum[i - 2])

                            def ex2(e, i=i):
                                return e.activation(out=ebuf[:, (i % 2) * 512:(i % 2) * 512 + 512],
                                                    in_=lbuf[:, (i % 2) * 512:(i % 2) * 512 + 512],
                                                    func=AF.Exp, bias=mcP[:, i:i + 1])
                            eds[i] = es.ev("scalar", deps, ex2)

                            def rs1(e, i=i):
                                return e.reduce_sum(out=mcP[:, 48:49], in_=ebuf[:, (i % 2) * 512:(i % 2) * 512 + 512],
                                                    axis=AXX)
                            rdeps = [eds[i]]
                            if i >= 1:
                                rdeps.append(radd[i - 1])
                            rsum[i] = es.ev("vector", rdeps, rs1)

                            def rs2(e, i=i):
                                return e.tensor_add(mcP[:, 16 + i:17 + i], mcP[:, 16 + i:17 + i], mcP[:, 48:49])
                            radd[i] = es.ev("gpsimd", [rsum[i]], rs2)
                            for sub2 in range(4):
                                kk = i * 4 + sub2
                                deps = [eds[i]]
                                if kk >= 1:
                                    deps.append(eev[kk - 1])

                                def trE2(e, i=i, sub2=sub2):
                                    return e.transpose(ptE[:],
                                                       ebuf[:, (i % 2) * 512 + sub2 * 128:(i % 2) * 512 + sub2 * 128 + 128],
                                                       identb[:])
                                etr[(i, sub2)] = es.ev("tensor", deps, trE2)

                                def evE(e, i=i, sub2=sub2, kk=kk):
                                    return e.tensor_copy(out=epc[:, (i % 2) * 512 + sub2 * 128:
                                                                 (i % 2) * 512 + sub2 * 128 + 128],
                                                         in_=ptE[:])
                                deps = [etr[(i, sub2)]]
                                if i >= 2:
                                    deps.append(est[i - 2])
                                eev[kk] = es.ev("vector", deps, evE)

                            def stE(e, i=i, et=et):
                                return e.dma_start(
                                    out=Escr[et * 512:(et + 1) * 512, i * 128:(i + 1) * 128
                                             ].rearrange("(s p) j -> p s j", p=128),
                                    in_=epc[:, (i % 2) * 512:(i % 2) * 512 + 512
                                            ].rearrange("p (s j) -> p s j", s=4))
                            est[i] = es.ev("sync", [eev[i * 4 + 3]], stE,
                                           sem="dE" if i % 2 == 0 else "dF", inc=16)
                            deps = [edv[i]]

                            def mmp1(e, i=i):
                                for j in range(6):
                                    e.matmul(psl[j][:], xtok[:, i * D + j * 128: i * D + j * 128 + 128],
                                             EdStrip[:, i * 512:(i + 1) * 512],
                                             start=(i == 0), stop=(i == NT - 1))
                                return e.matmul(pcs[0:1, :], ones_bf[:, 0:1],
                                                EdStrip[:, i * 512:(i + 1) * 512],
                                                start=(i == 0), stop=(i == NT - 1))
                            mmS[i] = es.ev("tensor", deps, mmp1)
                        lastm = mmS[NT - 1]
                        hl1 = es.ev("scalar", [lastm], lambda e: e.activation(
                            out=rrow[:, 0:512], in_=pcs[0:1, :], func=AF.Ln))
                        hl2 = es.ev("scalar", [hl1], lambda e: e.activation(
                            out=rrow[:, 512:1024], in_=rrow[:, 0:512], func=AF.Exp, scale=-1.0))
                        hbc2 = es.ev("tensor", [hl2], lambda e: e.matmul(
                            pcs[:], onesr_f32[:, 0:128], rrow[:, 512:1024], start=True, stop=True))
                        hcp2 = es.ev("scalar", [hbc2], lambda e: e.copy(out=bcs[:], in_=pcs[:]))
                        sev = {}
                        for j in range(6):
                            def sv(e, j=j):
                                return e.tensor_mul(slostr[:, j * 512:(j + 1) * 512], psl[j][:], bcs[:])
                            sev[j] = es.ev("vector", [hcp2], sv)
                        mm2 = {}
                        for i in range(NT):
                            deps = [sev[0], sev[1]] if i == 0 else [mm2[i - 1]]

                            def mmp2(e, i=i):
                                e.matmul(ps0[:], xtok[:, i * D + 6 * 128: i * D + 6 * 128 + 128],
                                         EdStrip[:, i * 512:(i + 1) * 512],
                                         start=(i == 0), stop=(i == NT - 1))
                                return e.matmul(ps1[:], xtok[:, i * D + 7 * 128: i * D + 7 * 128 + 128],
                                                EdStrip[:, i * 512:(i + 1) * 512],
                                                start=(i == 0), stop=(i == NT - 1))
                            mm2[i] = es.ev("tensor", deps, mmp2)
                        for j in range(2):
                            def sv2(e, j=j):
                                return e.tensor_mul(slostr[:, (6 + j) * 512:(7 + j) * 512],
                                                    (ps0 if j == 0 else ps1)[:], bcs[:])
                            sev[6 + j] = es.ev("vector", [mm2[NT - 1]], sv2)
                        es.emit(f"l0fC{tag}{et}")
                    cctx.close()

                    es = ES(b)
                    cctx = ExitStack()
                    if True:
                        ph = cctx.enter_context(nc.psum_tensor(f"l0h{tag}{et}", [128, 512], F32))
                        prs = [cctx.enter_context(nc.psum_tensor(f"l0pr{i}{tag}{et}", [128, 512], F32))
                               for i in range(6)]
                        pr0, pr1 = prs[0], prs[1]
                        ldw1 = {}
                        ldw2 = {}
                        mmh = {}
                        gel = {}
                        mpr = {}
                        for hc in range(32):
                            deps = [mmh[hc - 2]] if hc >= 2 else []

                            def lw1(e, hc=hc):
                                return e.dma_start(
                                    out=w1s[:, (hc % 2) * D:(hc % 2) * D + D].rearrange("p (c j) -> p c j", c=DC),
                                    in_=w1d[:, hc * 128:(hc + 1) * 128].rearrange("(c p) j -> p c j", p=128))
                            ldw1[hc] = es.ev("sync", deps, lw1, sem="dA" if hc % 2 == 0 else "dB", inc=16)
                            deps = [mpr[hc - 2]] if hc >= 2 else []

                            def lw2(e, hc=hc):
                                return e.dma_start(out=w2s[:, (hc % 2) * D:(hc % 2) * D + D],
                                                   in_=w2d[hc * 128:(hc + 1) * 128, :])
                            ldw2[hc] = es.ev("sync", deps, lw2, sem="dC" if hc % 2 == 0 else "dD", inc=16)

                            def mh(e, hc=hc):
                                r = None
                                for c in range(DC):
                                    r = e.matmul(ph[:], w1s[:, (hc % 2) * D + c * 128:(hc % 2) * D + c * 128 + 128],
                                                 slostr[:, c * 512:(c + 1) * 512],
                                                 start=(c == 0), stop=(c == DC - 1))
                                return r
                            deps = [ldw1[hc]]
                            if hc >= 1:
                                deps.append(gel[hc - 1])
                            mmh[hc] = es.ev("tensor", deps, mh)

                            def gl(e, hc=hc):
                                return e.activation(out=hcache[:, hc * 512:(hc + 1) * 512],
                                                    in_=ph[:], func=AF.Gelu)
                            gel[hc] = es.ev("scalar", [mmh[hc]], gl)

                            def mp(e, hc=hc):
                                r = None
                                for sub3 in range(3):
                                    for dh in range(2):
                                        r = e.matmul(prs[sub3 * 2 + dh][:],
                                                     hcache[:, hc * 512 + sub3 * 128: hc * 512 + sub3 * 128 + 128],
                                                     w2s[:, (hc % 2) * D + dh * 512:(hc % 2) * D + dh * 512 + 512],
                                                     start=(hc == 0), stop=(hc == 31))
                                return r
                            mpr[hc] = es.ev("tensor", [gel[hc], ldw2[hc]], mp)
                        pstr = {}
                        pst = {}
                        for sub3 in range(3):
                            def pv(e, sub3=sub3):
                                e.tensor_copy(out=epc[:, 0:512], in_=prs[sub3 * 2][:])
                                return e.tensor_copy(out=epc[:, 512:1024], in_=prs[sub3 * 2 + 1][:])
                            deps = [mpr[31]]
                            if sub3 >= 1:
                                deps.append(pst[sub3 - 1])
                            pstr[sub3] = es.ev("vector", deps, pv, inc=2)

                            def pw(e, sub3=sub3, et=et):
                                return e.dma_start(
                                    out=Pscr[et * 512 + sub3 * 128: et * 512 + (sub3 + 1) * 128, :],
                                    in_=epc[:, 0:1024])
                            pst[sub3] = es.ev("sync", [pstr[sub3]], pw, sem="dE", inc=16)
                        mm3 = {}
                        lw2b = {}
                        for hc in range(32):
                            deps = [pst[0]] if hc == 0 else [mm3[hc - 1]]

                            def lw2c(e, hc=hc):
                                return e.dma_start(out=w2s[:, (hc % 2) * D:(hc % 2) * D + D],
                                                   in_=w2d[hc * 128:(hc + 1) * 128, :])
                            lw2b[hc] = es.ev("sync", deps + ([lw2b[hc - 2]] if hc >= 2 else []), lw2c,
                                             sem="dC" if hc % 2 == 0 else "dD", inc=16)

                            def m3(e, hc=hc):
                                e.matmul(pr0[:], hcache[:, hc * 512 + 3 * 128: hc * 512 + 4 * 128],
                                         w2s[:, (hc % 2) * D: (hc % 2) * D + 512],
                                         start=(hc == 0), stop=(hc == 31))
                                return e.matmul(pr1[:], hcache[:, hc * 512 + 3 * 128: hc * 512 + 4 * 128],
                                                w2s[:, (hc % 2) * D + 512: (hc % 2) * D + 1024],
                                                start=(hc == 0), stop=(hc == 31))
                            mm3[hc] = es.ev("tensor", [lw2b[hc]], m3)

                        def pv3(e):
                            e.tensor_copy(out=epc[:, 0:512], in_=pr0[:])
                            return e.tensor_copy(out=epc[:, 512:1024], in_=pr1[:])
                        h3v = es.ev("vector", [mm3[31], pst[2]], pv3, inc=2)

                        def pw3(e, et=et):
                            return e.dma_start(out=Pscr[et * 512 + 3 * 128: et * 512 + 4 * 128, :],
                                               in_=epc[:, 0:1024])
                        es.ev("sync", [h3v], pw3, sem="dE", inc=16)
                        es.emit(f"l0fD{tag}{et}")
                    cctx.close()
            octx.close()

        es = ES(b)
        with nc.sbuf_tensor(f"l0rsx{tag}", [16, 128], F32) as rsx, \
             nc.sbuf_tensor(f"l0rsr{tag}", [1, T], F32) as rsr, \
             nc.sbuf_tensor(f"l0rb{tag}", [128, 4 * 512], F32) as rsbc:
            petx = ExitStack()
            pE0 = petx.enter_context(nc.psum_tensor(f"l0pe0{tag}", [128, 128], F32))
            h1 = es.ev("vector", [], lambda e: e.reciprocal(out=mcP[:, 48:64], in_=mcP[:, 16:32]))
            h2 = es.ev("tensor", [h1], lambda e: e.transpose(pE0[0:16, :], mcP[:, 48:64], b.sb["identf"][:]))
            h3 = es.ev("vector", [h2], lambda e: e.tensor_copy(out=rsx[:], in_=pE0[0:16, :]))
            h4 = es.ev("sync", [h3], lambda e: e.dma_start(
                out=RSscr[:, 0:T].rearrange("o (c j) -> c j", c=16), in_=rsx[:]), sem="dE", inc=16)
            h5 = es.ev("sync", [h4], lambda e: e.dma_start(out=rsr[:], in_=RSscr[:, 0:T]), sem="dF", inc=16)
            prevq = {}
            for q in range(NB):
                def bcq(e, q=q):
                    return e.matmul(pE0[:] if False else pE0[:, 0:128], onesr_f32[:, 0:128],
                                    rsr[:, q * 512:q * 512 + 128], start=True, stop=True)
                for pp in range(4):
                    def bcp(e, q=q, pp=pp):
                        return e.matmul(pE0[:, 0:128], onesr_f32[:, 0:128],
                                        rsr[:, q * 512 + pp * 128:q * 512 + pp * 128 + 128],
                                        start=True, stop=True)
                    deps = [h5] + ([prevq[(q, pp - 1)]] if pp >= 1 else ([prevq[(q - 1, 3)]] if q >= 1 else []))
                    hb = es.ev("tensor", deps, bcp)

                    def cpq(e, q=q, pp=pp):
                        return e.copy(out=rsbc[:, q * 512 + pp * 128:q * 512 + pp * 128 + 128],
                                      in_=pE0[:, 0:128])
                    prevq[(q, pp)] = es.ev("scalar", [hb], cpq)
            es.emit(f"l0fEbc{tag}")
            petx.close()

            es = ES(b)
            ectx = ExitStack()
            if True:
                psl2 = ectx.enter_context(nc.sbuf_tensor(f"l0pp{tag}", [128, 2 * D], BF16))
                esl2 = ectx.enter_context(nc.sbuf_tensor(f"l0ee{tag}", [128, 2 * 512], BF16))
                tmq = ectx.enter_context(nc.sbuf_tensor(f"l0tm{tag}", [128, 2 * 512], F32))
                qs2 = [ectx.enter_context(nc.psum_tensor(f"l0q{i}{tag}", [128, 512], F32))
                       for i in range(8)]
                ldp = {}
                lde = {}
                mmq = {}
                evq = {}
                adq = {}
                for q in range(NB):
                    for nt2 in range(32):
                        jj = q * 32 + nt2
                        deps = [mmq[jj - 2]] if jj >= 2 else []

                        def lp(e, nt2=nt2, jj=jj):
                            return e.dma_start(out=psl2[:, (jj % 2) * D:(jj % 2) * D + D],
                                               in_=Pscr[nt2 * 128:(nt2 + 1) * 128, :])
                        ldp[jj] = es.ev("sync", deps, lp, sem="dA" if jj % 2 == 0 else "dB", inc=16)

                        def le_(e, q=q, nt2=nt2, jj=jj):
                            return e.dma_start(out=esl2[:, (jj % 2) * 512:(jj % 2) * 512 + 512],
                                               in_=Escr[nt2 * 128:(nt2 + 1) * 128, q * 512:(q + 1) * 512])
                        lde[jj] = es.ev("sync", deps, le_, sem="dC" if jj % 2 == 0 else "dD", inc=16)

                        def mq(e, q=q, nt2=nt2, jj=jj):
                            r = None
                            for c in range(DC):
                                r = e.matmul(qs2[c][:], psl2[:, (jj % 2) * D + c * 128:(jj % 2) * D + c * 128 + 128],
                                             esl2[:, (jj % 2) * 512:(jj % 2) * 512 + 512],
                                             start=(nt2 == 0), stop=(nt2 == 31))
                            return r
                        deps2 = [ldp[jj], lde[jj]]
                        if nt2 == 0 and q >= 1:
                            deps2.append(adq[(q - 1, DC - 1)])
                        mmq[jj] = es.ev("tensor", deps2, mq)
                    for c in range(DC):
                        def eq(e, q=q, c=c):
                            return e.tensor_mul(tmq[:, (c % 2) * 512:(c % 2) * 512 + 512],
                                                qs2[c][:], rsbc[:, q * 512:(q + 1) * 512])
                        deps = [mmq[q * 32 + 31]]
                        if c >= 2:
                            deps.append(adq[(q, c - 2)])
                        evq[(q, c)] = es.ev("vector", deps, eq)

                        def aq(e, q=q, c=c):
                            sl = slice(c * T + q * 512, c * T + q * 512 + 512)
                            return e.tensor_add(xT[:, sl], xT[:, sl], tmq[:, (c % 2) * 512:(c % 2) * 512 + 512])
                        adq[(q, c)] = es.ev("gpsimd", [evq[(q, c)]], aq)
                es.emit(f"l0fE{tag}")
            ectx.close()
    return


def emit_l0attn_prologue(b: Bld):
    nc = b.nc
    pald = b.dr("pal_bf", [D, 256], BF16)
    wod = b.dr("Wo_bf", [D, D], BF16)
    wop = b.sbt("wop", [128, 2 * 1024], BF16)
    w1b = b.sbt("w1b", [17, HID], BF16)
    w2b = b.sbt("w2b", [HID, HID], BF16)
    w3b = b.sbt("w3b", [HID, 4], BF16)
    bcol = b.sbt("bcol", [HID, 4], F32)
    es = ES(b)
    with nc.sbuf_tensor("wopsl", [128, 2 * 640], BF16) as wsl, \
         nc.sbuf_tensor("wopf", [HID, 3 * HID], F32) as wf, \
         nc.psum_tensor("wopp0", [128, 512], F32) as p0, \
         nc.psum_tensor("wopp1", [128, 512], F32) as p1:
        ps = [p0, p1]
        l1 = es.ev("sync", [], lambda e: e.dma_start(out=wf[0:17, 0:HID], in_=b.ein("W1")),
                   sem="dC", inc=16)
        c1 = es.ev("vector", [l1], lambda e: e.tensor_copy(out=w1b[:], in_=wf[0:17, 0:HID]))
        l2 = es.ev("sync", [c1], lambda e: e.dma_start(out=wf[:, 0:HID], in_=b.ein("W2")), sem="dC", inc=16)
        c2 = es.ev("vector", [l2], lambda e: e.tensor_copy(out=w2b[:], in_=wf[:, 0:HID]))
        l3 = es.ev("sync", [c2], lambda e: e.dma_start(out=wf[:, 0:4], in_=b.ein("W3")), sem="dC", inc=16)
        c3 = es.ev("vector", [l3], lambda e: e.tensor_copy(out=w3b[:], in_=wf[:, 0:4]))
        hb1 = es.ev("sync", [c3], lambda e: e.dma_start(out=bcol[:, 0:1], in_=b.ein("b1")), sem="dC", inc=16)
        hb2 = es.ev("sync", [], lambda e: e.dma_start(out=bcol[:, 1:2], in_=b.ein("b2")), sem="dD", inc=16)
        hb3 = es.ev("sync", [], lambda e: e.dma_start(out=bcol[0:4, 2:3], in_=b.ein("b3")), sem="dE", inc=16)
        ldv = {}
        mmv = {}
        evv = {}
        ldw = {}
        for jj in range(4):
            cc, dh = jj // 2, jj % 2
            for c in range(DC):
                k = jj * DC + c
                deps = [mmv[k - 2]] if k >= 2 else []

                def ldp_(e, cc=cc, c=c, k=k):
                    return e.dma_start(out=wsl[:, (k % 2) * 640:(k % 2) * 640 + 128],
                                       in_=pald[c * 128:(c + 1) * 128, cc * 128:(cc + 1) * 128])
                ldv[k] = es.ev("sync", deps, ldp_, sem="dA" if k % 2 == 0 else "dB", inc=16)

                def ldw_(e, dh=dh, c=c, k=k):
                    return e.dma_start(out=wsl[:, (k % 2) * 640 + 128:(k % 2) * 640 + 640],
                                       in_=wod[c * 128:(c + 1) * 128, dh * 512:(dh + 1) * 512])
                wdeps = list(deps)
                if k == 0:
                    wdeps.append(hb3)
                elif k == 1:
                    wdeps.append(hb2)
                ldw[k] = es.ev("sync", wdeps, ldw_, sem="dE" if k % 2 == 0 else "dF", inc=16)

                def mm(e, jj=jj, c=c, k=k):
                    return e.matmul(ps[jj % 2][:], wsl[:, (k % 2) * 640:(k % 2) * 640 + 128],
                                    wsl[:, (k % 2) * 640 + 128:(k % 2) * 640 + 640],
                                    start=(c == 0), stop=(c == DC - 1))
                deps = [ldv[k], ldw[k]]
                if jj >= 2 and c == 0:
                    deps.append(evv[jj - 2])
                mmv[k] = es.ev("tensor", deps, mm)

            def ev_(e, cc=cc, dh=dh, jj=jj):
                return e.tensor_copy(out=wop[:, cc * 1024 + dh * 512: cc * 1024 + dh * 512 + 512],
                                     in_=ps[jj % 2][:])
            evv[jj] = es.ev("vector", [mmv[jj * DC + DC - 1]], ev_)
        es.emit("woprep")


def emit_l0attn(b: Bld, tag=""):
    nc = b.nc
    xT, xnT = b.sb["xT"], b.sb["xnT"]
    identb, identf = b.sb["identb"], b.sb["identf"]
    ones_bf, onesr_f32, onesr_bf = b.sb["ones_bf"], b.sb["onesr_f32"], b.sb["onesr_bf"]
    tri_bf, neg30r_bf = b.sb["tri_bf"], b.sb["neg30r_bf"]
    iota_col = b.sb["iota_col"]
    wid = b.dr("Wi_bf", [D, D], BF16)
    wpd = b.dr("Wp_bf", [D, D], BF16)
    PHAT = b.dr("l0a_phat", [2 * T, D], BF16)
    TKI = b.dr("l0a_tki", [T * 16], F32)
    KEEP = b.dr("l0a_keep", [T * 16], F32)
    DELTA = b.dr("l0a_delta", [T * 16], BF16)
    GRM = b.dr("l0a_grm", [256 * 128, 128], BF16)
    Z3 = b.dr("l0a_z3", [4, T * 16], BF16)
    ropec = b.ein("c_ropecos")
    ropes = b.ein("c_ropesin")

    itctx = ExitStack()
    ITb = itctx.enter_context(nc.sbuf_tensor(f"l0aI{tag}", [128, DC * T], BF16))
    PTb = itctx.enter_context(nc.sbuf_tensor(f"l0aP{tag}", [128, DC * T], BF16))

    for mi, (wsrc, dstT) in enumerate([(wid, ITb), (wpd, PTb)]):
        es = ES(b)
        with nc.sbuf_tensor(f"l0aw{tag}{mi}", [128, 2 * D], BF16) as wsl, \
             nc.sbuf_tensor(f"l0acs{tag}{mi}", [128, 4 * 512], BF16) as cssl, \
             nc.sbuf_tensor(f"l0at{tag}{mi}", [128, 2 * 512], F32) as tbuf, \
             nc.psum_tensor(f"l0ap0{tag}{mi}", [128, 512], F32) as q0, \
             nc.psum_tensor(f"l0ap1{tag}{mi}", [128, 512], F32) as q1:
            qs = [q0, q1]
            wld = {}
            mmd = {}
            rpd = {}
            csd = {}
            for p in range(4):
                for half in range(2):
                    o = p + 4 * half
                    jj = p * 2 + half
                    deps = [mmd[(p - 1, half)]] if p >= 1 else []

                    def ld(e, o=o, jj=jj, wsrc=wsrc):
                        return e.dma_start(
                            out=wsl[:, (jj % 2) * D:(jj % 2) * D + D].rearrange("p (c j) -> p c j", c=DC),
                            in_=wsrc[:, o * 128:(o + 1) * 128].rearrange("(c p) j -> p c j", p=128))
                    wld[(p, half)] = es.ev("sync", deps, ld, sem="dA" if jj % 2 == 0 else "dB", inc=16)
                for blk in range(NB):
                    if blk >= 2:
                        deps = [rpd[(p, blk - 2)]]
                    elif p >= 1:
                        deps = [rpd[(p - 1, blk + 2)]]
                    else:
                        deps = []

                    def ldco(e, p=p, blk=blk):
                        return e.dma_start(out=cssl[:, (blk % 2) * 1024:(blk % 2) * 1024 + 512],
                                           in_=ropec[p * 128:(p + 1) * 128, blk * 512:(blk + 1) * 512])
                    csd1 = es.ev("sync", deps, ldco, sem="dC" if blk % 2 == 0 else "dD", inc=16)

                    def ldsi(e, p=p, blk=blk):
                        return e.dma_start(out=cssl[:, (blk % 2) * 1024 + 512:(blk % 2) * 1024 + 1024],
                                           in_=ropes[p * 128:(p + 1) * 128, blk * 512:(blk + 1) * 512])
                    csd2 = es.ev("sync", list(deps), ldsi, sem="dE" if blk % 2 == 0 else "dF", inc=16)
                    csd[(p, blk)] = csd2
                    for half in range(2):
                        o = p + 4 * half

                        def mm(e, o=o, half=half, blk=blk, jj=p * 2 + half):
                            r = None
                            for c in range(DC):
                                r = e.matmul(qs[half][:],
                                             wsl[:, (jj % 2) * D + c * 128:(jj % 2) * D + c * 128 + 128],
                                             xnT[:, c * T + blk * 512: c * T + blk * 512 + 512],
                                             start=(c == 0), stop=(c == DC - 1))
                            return r
                        deps = [wld[(p, half)]]
                        if blk == 0 and p >= 1:
                            deps.append(rpd[(p - 1, NB - 1)])
                        elif blk >= 1:
                            deps.append(rpd[(p, blk - 1)])
                        mmd[(p, half, blk)] = es.ev("tensor", deps, mm)
                    dep2 = [mmd[(p, 0, blk)], mmd[(p, 1, blk)], csd1, csd2]

                    def rp1(e, p=p, blk=blk):
                        return e.tensor_mul(tbuf[:, 0:512], q0[:],
                                            cssl[:, (blk % 2) * 1024:(blk % 2) * 1024 + 512])
                    h1 = es.ev("vector", dep2, rp1)

                    def rp2(e, p=p, blk=blk):
                        return e.tensor_mul(tbuf[:, 512:1024], q1[:],
                                            cssl[:, (blk % 2) * 1024 + 512:(blk % 2) * 1024 + 1024])
                    h2 = es.ev("gpsimd", [h1], rp2) if False else es.ev("vector", [h1], rp2)

                    def rp3(e, p=p, blk=blk, dstT=dstT):
                        return e.scalar_tensor_tensor(
                            out=dstT[:, p * T + blk * 512: p * T + blk * 512 + 512],
                            in0=tbuf[:, 512:1024], scalar=-1.0, in1=tbuf[:, 0:512],
                            op0=ALU.mult, op1=ALU.add)
                    h3 = es.ev("vector", [h2], rp3)

                    def rp4(e, p=p, blk=blk):
                        return e.tensor_mul(tbuf[:, 0:512], q0[:],
                                            cssl[:, (blk % 2) * 1024 + 512:(blk % 2) * 1024 + 1024])
                    h4 = es.ev("vector", [h3], rp4)

                    def rp5(e, p=p, blk=blk):
                        return e.tensor_mul(tbuf[:, 512:1024], q1[:],
                                            cssl[:, (blk % 2) * 1024:(blk % 2) * 1024 + 512])
                    h5 = es.ev("vector", [h4], rp5)

                    def rp6(e, p=p, blk=blk, dstT=dstT):
                        return e.tensor_add(
                            dstT[:, (p + 4) * T + blk * 512: (p + 4) * T + blk * 512 + 512],
                            tbuf[:, 0:512], tbuf[:, 512:1024])
                    rpd[(p, blk)] = es.ev("vector", [h5], rp6)
                mmd[(p, 0)] = mmd[(p, 0, NB - 1)]
                mmd[(p, 1)] = mmd[(p, 1, NB - 1)]
            es.emit(f"l0aproj{tag}{mi}")

        es = ES(b)
        with nc.sbuf_tensor(f"l0an{tag}{mi}", [128, 1024], BF16) as sqb, \
             nc.sbuf_tensor(f"l0anr{tag}{mi}", [1, T], F32) as nrows, \
             nc.sbuf_tensor(f"l0abc{tag}{mi}", [128, 160], F32) as nbc, \
             nc.sbuf_tensor(f"l0aht{tag}{mi}", [128, 2 * 1024], BF16) as htok, \
             nc.psum_tensor(f"l0an0{tag}{mi}", [1, 512], F32) as pn0, \
             nc.psum_tensor(f"l0an1{tag}{mi}", [1, 512], F32) as pn1, \
             nc.psum_tensor(f"l0anb{tag}{mi}", [128, 512], F32) as pnb, \
             nc.psum_tensor(f"l0att{tag}{mi}", [128, 128], BF16) as ptt0, \
             nc.psum_tensor(f"l0att2{tag}{mi}", [128, 128], BF16) as ptt1:
            pns = [pn0, pn1]
            ptts = [ptt0, ptt1]
            sqd = {}
            mmn = {}
            evn = {}
            for blk in range(NB):
                for c in range(DC):
                    k = blk * DC + c
                    deps = [mmn[k - 2]] if k >= 2 else []

                    def sq(e, blk=blk, c=c, k=k, dstT=dstT):
                        return e.activation(out=sqb[:, (k % 2) * 512:(k % 2) * 512 + 512],
                                            in_=dstT[:, c * T + blk * 512: c * T + blk * 512 + 512],
                                            func=AF.Square)
                    sqd[k] = es.ev("scalar", deps, sq)

                    def mmq(e, blk=blk, c=c, k=k):
                        return e.matmul(pns[blk % 2][0:1, :], ones_bf[:, 0:1],
                                        sqb[:, (k % 2) * 512:(k % 2) * 512 + 512],
                                        start=(c == 0), stop=(c == DC - 1))
                    mmn[k] = es.ev("tensor", [sqd[k]] + ([evn[blk - 2]] if blk >= 2 and c == 0 else []), mmq)
                deps = [mmn[blk * DC + DC - 1]]
                if blk >= 1:
                    deps.append(deps[0])

                def evq(e, blk=blk):
                    return e.tensor_copy(out=nrows[:, blk * 512:(blk + 1) * 512], in_=pns[blk % 2][0:1, :])
                evn[blk] = es.ev("vector", deps, evq)
            hq = es.ev("scalar", [evn[NB - 1]], lambda e: e.activation(
                out=nrows[:], in_=nrows[:], func=AF.Sqrt))
            hm = es.ev("vector", [hq], lambda e: e.tensor_scalar_max(nrows[:], nrows[:], 1e-12))
            hln = es.ev("scalar", [hm], lambda e: e.activation(
                out=nrows[:], in_=nrows[:], func=AF.Ln))
            hex = es.ev("scalar", [hln], lambda e: e.activation(
                out=nrows[:], in_=nrows[:], func=AF.Exp, scale=-1.0))
            RSb = b.dr("l0a_rnb", [1, T], F32)
            hst = es.ev("sync", [hex], lambda e: e.dma_start(out=RSb, in_=nrows[:]),
                        sem="dE", inc=16)
            hld = es.ev("sync", [hst], lambda e: e.dma_start(
                out=nbc[0:16, 0:128], in_=RSb.rearrange("o (i p) -> i p", i=16)), sem="dF", inc=16)
            htr = es.ev("tensor", [hld], lambda e: e.transpose(
                pnb[0:128, 0:16], nbc[0:16, 0:128], identf[0:16, 0:16]))
            hrc = es.ev("vector", [htr], lambda e: e.tensor_copy(out=nbc[:, 128:128 + 16],
                                                                 in_=pnb[0:128, 0:16]))
            trd = {}
            tvd = {}
            std = {}
            for i in range(NT):
                for c in range(DC):
                    k = i * DC + c
                    deps = [hrc]
                    if k >= 2:
                        deps.append(tvd[k - 2])

                    def trh(e, i=i, c=c, k=k, dstT=dstT):
                        return e.transpose(ptts[k % 2][:],
                                           dstT[:, c * T + i * 128: c * T + (i + 1) * 128], identb[:])
                    trd[k] = es.ev("tensor", deps, trh)
                    deps = [trd[k]]
                    if i >= 2:
                        deps.append(std[i - 2])

                    def tvh(e, i=i, c=c, k=k):
                        return e.tensor_scalar_mul(
                            htok[:, (i % 2) * 1024 + c * 128:(i % 2) * 1024 + c * 128 + 128],
                            ptts[k % 2][:], nbc[:, 128 + i:128 + i + 1])
                    tvd[k] = es.ev("vector", deps, tvh)

                def sth(e, i=i, mi=mi):
                    return e.dma_start(out=PHAT[mi * T + i * 128: mi * T + (i + 1) * 128, :],
                                       in_=htok[:, (i % 2) * 1024:(i % 2) * 1024 + 1024])
                std[i] = es.ev("sync", [tvd[i * DC + DC - 1]], sth,
                               sem="dE" if i % 2 == 0 else "dF", inc=16)
            es.emit(f"l0anorm{tag}{mi}")

    es = ES(b)
    with nc.sbuf_tensor(f"l0aL{tag}", [128, T], F32) as Lrow, \
         nc.sbuf_tensor(f"l0atv{tag}", [128, 16], F32) as tv, \
         nc.sbuf_tensor(f"l0ati{tag}", [128, 16], U32) as tiu, \
         nc.sbuf_tensor(f"l0atf{tag}", [128, 4 * 16], F32) as tf, \
         nc.sbuf_tensor(f"l0adl{tag}", [128, 16], BF16) as dlt, \
         nc.sbuf_tensor(f"l0atc{tag}", [128, 2], F32) as tcc, \
         nc.psum_tensor(f"l0aq0{tag}", [128, 512], F32) as q0, \
         nc.psum_tensor(f"l0aq1{tag}", [128, 512], F32) as q1:
        qs = [q0, q1]
        prev_done = None
        for q in range(NT):
            nblk = q // 4 + 1
            evd = {}
            for tsb in range(nblk):
                boundary = (tsb == nblk - 1)

                deps = []
                if prev_done is not None and tsb < 2:
                    deps.extend(prev_done)
                if tsb >= 2:
                    deps.append(evd[tsb - 2])

                def mmq2(e, q=q, tsb=tsb, boundary=boundary):
                    r = None
                    for c in range(DC):
                        r = e.matmul(qs[tsb % 2][:],
                                     ITb[:, c * T + q * 128: c * T + (q + 1) * 128],
                                     PTb[:, c * T + tsb * 512: c * T + tsb * 512 + 512],
                                     start=(c == 0), stop=(c == DC - 1) and not boundary)
                    if boundary:
                        c0 = q * 128 - tsb * 512
                        r = e.matmul(qs[tsb % 2][:, c0:c0 + 128], tri_bf[:], identb[:],
                                     start=False, stop=(c0 + 128 >= 512))
                        if c0 + 128 < 512:
                            r = e.matmul(qs[tsb % 2][:, c0 + 128:512], neg30r_bf[:],
                                         onesr_bf[:, 0:512 - c0 - 128], start=False, stop=True)
                    return r
                hmm = es.ev("tensor", deps, mmq2)

                def evc(e, q=q, tsb=tsb):
                    return e.activation(out=Lrow[:, tsb * 512:(tsb + 1) * 512],
                                        in_=qs[tsb % 2][:], func=AF.Identity)
                evd[tsb] = es.ev("scalar", [hmm], evc)
            w = nblk * 512
            hdep = [evd[t2] for t2 in range(nblk)]

            ma = es.ev("vector", hdep, lambda e, w=w: e.max(out=tv[:, 0:8], in_=Lrow[:, 0:w]))
            ia = es.ev("vector", [ma], lambda e, w=w: e.max_index(
                out=tiu[:, 0:8], in_max=tv[:, 0:8], in_values=Lrow[:, 0:w]))
            rp = es.ev("vector", [ia], lambda e, w=w: e.match_replace(
                out=Lrow[:, 0:w], in_to_replace=tv[:, 0:8], in_values=Lrow[:, 0:w], imm_value=-1e30))
            mb = es.ev("vector", [rp], lambda e, w=w: e.max(out=tv[:, 8:16], in_=Lrow[:, 0:w]))
            h2 = es.ev("vector", [mb], lambda e, w=w: e.max_index(
                out=tiu[:, 8:16], in_max=tv[:, 8:16], in_values=Lrow[:, 0:w]))

            def kef(e):
                return e.tensor_single_scalar(out=tf[:, 16:32], in_=tv[:], scalar=-1e29, op=ALU.is_gt)
            h3 = es.ev("vector", [h2], kef)

            def idxf(e):
                return e.tensor_copy(out=tf[:, 0:16], in_=tiu[:])
            h4 = es.ev("vector", [h2], idxf)
            h5 = es.ev("vector", [h4], lambda e, q=q: e.tensor_scalar_add(tcc[:, 0:1], iota_col[:], float(128 * q)))
            h6 = es.ev("vector", [h5], lambda e: e.tensor_scalar_add(tf[:, 15:16], tcc[:, 0:1], float(T)))
            h7 = es.ev("vector", [h6], lambda e: e.tensor_scalar(
                out=tf[:, 32:48], in0=tf[:, 0:16], scalar1=tcc[:, 0:1], scalar2=-1.0 / T,
                op0=ALU.subtract, op1=ALU.mult))
            h8 = es.ev("vector", [h7], lambda e: e.tensor_scalar_max(tf[:, 32:48], tf[:, 32:48], 0.0))
            h9 = es.ev("vector", [h8, h3], lambda e: e.tensor_mul(dlt[:], tf[:, 32:48], tf[:, 16:32]))
            s1 = es.ev("sync", [h6], lambda e, q=q: e.dma_start(
                out=TKI.rearrange("(t k) -> t k", k=16)[q * 128:(q + 1) * 128, :], in_=tf[:, 0:16]),
                sem="dE", inc=16)
            s2 = es.ev("sync", [h3], lambda e, q=q: e.dma_start(
                out=KEEP.rearrange("(t k) -> t k", k=16)[q * 128:(q + 1) * 128, :], in_=tf[:, 16:32]),
                sem="dF", inc=16)
            s3 = es.ev("sync", [h9], lambda e, q=q: e.dma_start(
                out=DELTA.rearrange("(t k) -> t k", k=16)[q * 128:(q + 1) * 128, :], in_=dlt[:]),
                sem="dC", inc=16)
            prev_done = [h9, s1, s2, s3]
        es.emit(f"l0atopk{tag}")
    itctx.close()

    es = ES(b)
    with nc.sbuf_tensor(f"l0agx{tag}", [128, 256], U32) as gidxT, \
         nc.sbuf_tensor(f"l0akp{tag}", [128, 256], F32) as keepT, \
         nc.sbuf_tensor(f"l0atm{tag}", [128, 2 * 128], F32) as tmf, \
         nc.psum_tensor(f"l0agp{tag}", [128, 128], F32) as pg0:
        h3b = None
        for r in range(2):
            h1 = es.ev("sync", ([h3b] if h3b else []), lambda e, r=r: e.dma_start(
                out=tmf[:, 0:128] if r == 0 else tmf[:, 128:256],
                in_=TKI.rearrange("(r p j) -> r p j", r=2, p=128)[r]),
                sem="dA" if r == 0 else "dB", inc=16)
            h2 = es.ev("tensor", [h1], lambda e, r=r: e.transpose(
                pg0[:], tmf[:, r * 128:(r + 1) * 128], identf[:]))
            h3 = es.ev("vector", [h2], lambda e, r=r: e.tensor_copy(
                out=gidxT[:, r * 128:(r + 1) * 128], in_=pg0[:]))
            h1b = es.ev("sync", [h3], lambda e, r=r: e.dma_start(
                out=tmf[:, 0:128] if r == 0 else tmf[:, 128:256],
                in_=KEEP.rearrange("(r p j) -> r p j", r=2, p=128)[r]),
                sem="dA" if r == 0 else "dB", inc=16)
            h2b = es.ev("tensor", [h1b], lambda e, r=r: e.transpose(
                pg0[:], tmf[:, r * 128:(r + 1) * 128], identf[:]))
            h3b = es.ev("vector", [h2b], lambda e, r=r: e.tensor_copy(
                out=keepT[:, r * 128:(r + 1) * 128], in_=pg0[:]))
        es.emit(f"l0agidx{tag}")

        es = ES(b)
        with nc.sbuf_tensor(f"l0axg{tag}", [128, 2 * D], BF16) as Xg, \
             nc.sbuf_tensor(f"l0axt{tag}", [128, 2 * D], BF16) as XgT, \
             nc.sbuf_tensor(f"l0ags{tag}", [128, 8 * 128], BF16) as Gstr, \
             nc.psum_tensor(f"l0agt0{tag}", [128, 128], BF16) as pt0, \
             nc.psum_tensor(f"l0agt1{tag}", [128, 128], BF16) as pt1, \
             nc.psum_tensor(f"l0agg0{tag}", [128, 128], F32) as pG0, \
             nc.psum_tensor(f"l0agg1{tag}", [128, 128], F32) as pG1:
            pts = [pt0, pt1]
            pGs = [pG0, pG1]
            gld = {}
            zrd = {}
            trd = {}
            tvd = {}
            mmg = {}
            evg = {}
            stg = {}
            for g in range(256):
                deps = [trd[(g - 2, DC - 1)]] if g >= 2 else []

                def gl(e, g=g):
                    return e.indirect_dma_start(
                        Xg[:, (g % 2) * D:(g % 2) * D + D], None, PHAT,
                        bass.IndirectOffsetOnAxis(gidxT[:, g:g + 1], 0))
                gld[g] = es.ev("gpsimd", deps, gl, sem="dH" if g % 2 == 0 else "dI", inc=16)
                zdep = [gld[g]]
                if g < 2:
                    def zr(e, g=g):
                        return e.tensor_scalar_mul(Xg[:, (g % 2) * D:(g % 2) * D + D],
                                                   Xg[:, (g % 2) * D:(g % 2) * D + D],
                                                   keepT[:, g:g + 1])
                    zrd[g] = es.ev("vector", zdep, zr)
                    zdep = [zrd[g]]
                for c in range(DC):
                    k = g * DC + c
                    deps = list(zdep)
                    if k >= 2:
                        deps.append(tvd[k - 2])

                    def tr(e, g=g, c=c, k=k):
                        return e.transpose(pts[k % 2][:],
                                           Xg[:, (g % 2) * D + c * 128:(g % 2) * D + c * 128 + 128],
                                           identb[:])
                    trd[(g, c)] = es.ev("tensor", deps, tr)

                    def tvf(e, g=g, c=c, k=k):
                        return e.tensor_copy(out=XgT[:, (g % 2) * D + c * 128:(g % 2) * D + c * 128 + 128],
                                             in_=pts[k % 2][:])
                    deps = [trd[(g, c)]]
                    if c == 0 and g >= 2:
                        deps.append(mmg[g - 2])
                    tvd[k] = es.ev("vector", deps, tvf)

                def mg(e, g=g):
                    r = None
                    for c in range(DC):
                        r = e.matmul(pGs[g % 2][:],
                                     XgT[:, (g % 2) * D + c * 128:(g % 2) * D + c * 128 + 128],
                                     XgT[:, (g % 2) * D + c * 128:(g % 2) * D + c * 128 + 128],
                                     start=(c == 0), stop=(c == DC - 1))
                    return r
                deps = [tvd[g * DC + DC - 1]]
                if g >= 2:
                    deps.append(evg[g - 2])
                mmg[g] = es.ev("tensor", deps, mg)

                def evc(e, g=g):
                    return e.tensor_scalar(out=Gstr[:, (g % 8) * 128:(g % 8) * 128 + 128],
                                           in0=pGs[g % 2][:], scalar1=1.0, scalar2=-1.0,
                                           op0=ALU.min, op1=ALU.max)
                deps = [mmg[g]]
                if g >= 8:
                    deps.append(stg[g // 8 - 1])
                evg[g] = es.ev("vector", deps, evc)
                if g % 8 == 7:
                    def st(e, g=g):
                        g0 = g - 7
                        return e.dma_start(
                            out=GRM[g0 * 128:(g0 + 8) * 128, :].rearrange("(gi p) j -> p gi j", p=128),
                            in_=Gstr[:].rearrange("p (gi j) -> p gi j", gi=8))
                    stg[g // 8] = es.ev("sync", [evg[g]], st,
                                        sem="dE" if (g // 8) % 2 == 0 else "dF", inc=16)
            es.emit(f"l0agram{tag}")

    es = ES(b)
    with nc.sbuf_tensor(f"l0ar{tag}", [17, 4096], BF16) as relT, \
         nc.sbuf_tensor(f"l0ah1{tag}", [HID, 4096], BF16) as h1b_, \
         nc.sbuf_tensor(f"l0ah2{tag}", [HID, 4096], BF16) as h2b_, \
         nc.sbuf_tensor(f"l0az{tag}", [4, 4096], BF16) as z3s, \
         nc.psum_tensor(f"l0am0{tag}", [64, 512], F32) as pm0, \
         nc.psum_tensor(f"l0am1{tag}", [64, 512], F32) as pm1, \
         nc.psum_tensor(f"l0az0{tag}", [4, 512], F32) as pz0, \
         nc.psum_tensor(f"l0az1{tag}", [4, 512], F32) as pz1:
        pms = [pm0, pm1]
        pzs = [pz0, pz1]
        w1b, w2b, w3b, bcol = b.sb["w1b"], b.sb["w2b"], b.sb["w3b"], b.sb["bcol"]
        prev_ch_done = None
        for ch in range(8):
            g0 = ch * 32

            def ldrel(e, ch=ch, g0=g0):
                return e.dma_start(
                    out=relT[0:16, :].rearrange("j (g i k) -> j g i k", g=32, i=8),
                    in_=GRM.rearrange("(g i j) k -> j g i k", g=256, i=8)[:, g0:g0 + 32, :, 0:16])
            hra = es.ev("sync", ([prev_ch_done] if prev_ch_done else []), ldrel, sem="dA", inc=16)

            def ldd(e, ch=ch):
                return e.dma_start(out=relT[16:17, :],
                                   in_=DELTA[ch * 4096:(ch + 1) * 4096].rearrange("(o n) -> o n", o=1))
            hrb = es.ev("sync", [], ldd, sem="dB", inc=16)
            hr = hrb
            hr_all = [hra, hrb]
            ev1 = {}
            ev2 = {}
            ev3 = {}
            for sb in range(8):
                def m1(e, sb=sb):
                    return e.matmul(pms[sb % 2][:], w1b[:], relT[:, sb * 512:(sb + 1) * 512],
                                    start=True, stop=True)
                deps = [hra, hrb] + ([ev1[sb - 2]] if sb >= 2 else [])
                hm1 = es.ev("tensor", deps, m1)

                def g1(e, sb=sb):
                    return e.activation(out=h1b_[:, sb * 512:(sb + 1) * 512], in_=pms[sb % 2][:],
                                        func=AF.Gelu, bias=bcol[0:HID, 0:1])
                ev1[sb] = es.ev("scalar", [hm1], g1)
            for sb in range(8):
                def m2(e, sb=sb):
                    return e.matmul(pms[sb % 2][:], w2b[:], h1b_[:, sb * 512:(sb + 1) * 512],
                                    start=True, stop=True)
                deps = [ev1[sb]] + ([ev2[sb - 2]] if sb >= 2 else [ev1[7]])
                hm2 = es.ev("tensor", deps, m2)

                def g2(e, sb=sb):
                    return e.activation(out=h2b_[:, sb * 512:(sb + 1) * 512], in_=pms[sb % 2][:],
                                        func=AF.Gelu, bias=bcol[0:HID, 1:2])
                ev2[sb] = es.ev("scalar", [hm2], g2)
            for sb in range(8):
                def m3(e, sb=sb):
                    return e.matmul(pzs[sb % 2][:], w3b[:], h2b_[:, sb * 512:(sb + 1) * 512],
                                    start=True, stop=True)
                deps = [ev2[sb]] + ([ev3[sb - 2]] if sb >= 2 else [ev2[7]])
                hm3 = es.ev("tensor", deps, m3)

                def g3a(e, sb=sb):
                    return e.activation(out=z3s[0:4, sb * 512:(sb + 1) * 512], in_=pzs[sb % 2][:],
                                        func=AF.Identity, bias=bcol[0:4, 2:3])
                hz = es.ev("scalar", [hm3], g3a)

                def g3b(e, sb=sb):
                    return e.activation(out=z3s[0:2, sb * 512:(sb + 1) * 512],
                                        in_=z3s[0:2, sb * 512:(sb + 1) * 512], func=AF.Tanh)
                ev3[sb] = es.ev("scalar", [hz], g3b)

            def stz(e, ch=ch):
                return e.dma_start(out=Z3[:, ch * 4096:(ch + 1) * 4096], in_=z3s[:])
            prev_ch_done = es.ev("sync", [ev3[7]], stz, sem="dE", inc=16)
        es.emit(f"l0amlp{tag}")

    es = ES(b)
    wop = b.sb["wop"]
    hat_tile = b.sb["hat_tile"]
    with nc.sbuf_tensor(f"l0azt{tag}", [128, 80], BF16) as zt, \
         nc.sbuf_tensor(f"l0akp2{tag}", [128, 16], F32) as kp2, \
         nc.sbuf_tensor(f"l0aw{tag}", [128, 48], F32) as wsc, \
         nc.sbuf_tensor(f"l0ahat{tag}", [128, 3 * 256], BF16) as hats, \
         nc.sbuf_tensor(f"l0aA{tag}", [128, 2 * 256], F32) as Aac, \
         nc.sbuf_tensor(f"l0aAb{tag}", [128, 256], BF16) as Abf, \
         nc.psum_tensor(f"l0ay0{tag}", [128, 128], F32) as py0, \
         nc.psum_tensor(f"l0ay1{tag}", [128, 128], F32) as py1, \
         nc.psum_tensor(f"l0aat{tag}", [128, 128], BF16) as pat:
        pys = [py0, py1]
        hatrow = 3 * 256
        Arow = 2 * 256

        def bc_inner(buf, off, rowlen):
            return bass.AP(buf, off, [[rowlen, 128], [1, 16], [0, 16]])

        def bc_outer(buf, off, rowlen):
            return bass.AP(buf, off, [[rowlen, 128], [0, 16], [1, 16]])

        def v3(buf, off, rowlen):
            return bass.AP(buf, off, [[rowlen, 128], [16, 16], [1, 16]])
        prevA = None
        for q in range(NT):
            dep0 = [prevA] if prevA else []

            def ldz(e, q=q):
                return e.dma_start(
                    out=zt[:, 0:48].rearrange("p (r k) -> p r k", r=3),
                    in_=Z3[0:3, :].rearrange("r (t k) -> t r k", k=16)[q * 128:(q + 1) * 128, :, :])
            hz = es.ev("sync", dep0, ldz, sem="dA", inc=16)
            hk = es.ev("sync", dep0, lambda e, q=q: e.dma_start(
                out=kp2[:], in_=KEEP.rearrange("(t k) -> t k", k=16)[q * 128:(q + 1) * 128, :]),
                sem="dB", inc=16)
            h1 = es.ev("vector", [hz, hk], lambda e: e.tensor_mul(wsc[:, 32:48], zt[:, 32:48], kp2[:]))
            h2 = es.ev("vector", [h1], lambda e: e.tensor_scalar(
                out=wsc[:, 16:32], in0=kp2[:], scalar1=1e30, scalar2=-1e30, op0=ALU.mult, op1=ALU.add))
            h3 = es.ev("vector", [h2], lambda e: e.tensor_add(wsc[:, 32:48], wsc[:, 32:48], wsc[:, 16:32]))
            h4 = es.ev("vector", [h3], lambda e: e.memset(wsc[:, 47:48], -1e30))
            h5 = es.ev("vector", [h4], lambda e: e.reduce_max(out=wsc[:, 1:2], in_=wsc[:, 32:48], axis=AXX))
            h6 = es.ev("vector", [h5], lambda e: e.tensor_scalar_mul(wsc[:, 2:3], wsc[:, 1:2], -1.0))
            h7 = es.ev("scalar", [h6], lambda e: e.activation(out=wsc[:, 16:32], in_=wsc[:, 32:48],
                                                              func=AF.Exp, bias=wsc[:, 2:3]))
            h8 = es.ev("vector", [h7], lambda e: e.reduce_sum(out=wsc[:, 3:4], in_=wsc[:, 16:32], axis=AXX))
            h9 = es.ev("vector", [h8], lambda e: e.reciprocal(out=wsc[:, 4:5], in_=wsc[:, 3:4]))
            hw = es.ev("vector", [h9], lambda e: e.tensor_scalar_mul(wsc[:, 16:32], wsc[:, 16:32],
                                                                     wsc[:, 4:5]))
            hf = {}
            for ax in range(2):
                a1 = es.ev("scalar", [hz] + ([hf[0]] if ax == 1 else []),
                           lambda e, ax=ax: e.activation(
                               out=zt[:, 64:80], in_=zt[:, ax * 16:(ax + 1) * 16],
                               func=AF.Identity, scale=7.5, bias=konst(b, 7.5)))
                a2 = es.ev("vector", [a1], lambda e: e.tensor_scalar(
                    out=zt[:, 64:80], in0=zt[:, 64:80], scalar1=15.0, scalar2=0.0,
                    op0=ALU.min, op1=ALU.max))
                a3 = es.ev("vector", [a2], lambda e, ax=ax: e.tensor_tensor(
                    out=v3(hats, ax * 256, hatrow), in0=bc_inner(zt, 64, 80),
                    in1=v3(hat_tile, 0, 256), op=ALU.subtract))
                a4 = es.ev("scalar", [a3], lambda e, ax=ax: e.activation(
                    out=hats[:, ax * 256:(ax + 1) * 256], in_=hats[:, ax * 256:(ax + 1) * 256],
                    func=AF.Abs))
                a5 = es.ev("scalar", [a4], lambda e, ax=ax: e.activation(
                    out=hats[:, ax * 256:(ax + 1) * 256], in_=hats[:, ax * 256:(ax + 1) * 256],
                    func=AF.Relu, scale=-1.0, bias=konst(b, 1.0)))
                hf[ax] = a5
            hwy = es.ev("vector", [hf[1], hw], lambda e: e.tensor_tensor(
                out=v3(hats, 512, hatrow), in0=v3(hats, 256, hatrow),
                in1=bc_inner(wsc, 16, 48), op=ALU.mult))
            hA = None
            for k2 in range(15):
                if k2 == 0:
                    def am(e, k2=k2):
                        return e.tensor_tensor(
                            out=bass.AP(Aac, 0, [[Arow, 128], [16, 16], [1, 16]]),
                            in0=bc_inner(hats, 512 + k2 * 16, hatrow),
                            in1=bc_outer(hats, 0 + k2 * 16, hatrow), op=ALU.mult)
                    hA = es.ev("vector", [hwy, hf[0]] + ([prevA] if prevA else []), am)
                else:
                    def am1(e, k2=k2):
                        return e.tensor_tensor(
                            out=bass.AP(Aac, 256, [[Arow, 128], [16, 16], [1, 16]]),
                            in0=bc_inner(hats, 512 + k2 * 16, hatrow),
                            in1=bc_outer(hats, 0 + k2 * 16, hatrow), op=ALU.mult)
                    hm = es.ev("gpsimd", [hwy, hf[0]] + ([hA] if k2 >= 2 else []), am1)

                    def am2(e, k2=k2):
                        return e.tensor_add(Aac[:, 0:256], Aac[:, 0:256], Aac[:, 256:512])
                    hA = es.ev("vector", [hm, hA], am2)
            hcst = es.ev("vector", [hA], lambda e: e.tensor_copy(out=Abf[:], in_=Aac[:, 0:256]))
            trA = {}
            for cc in range(2):
                t1 = es.ev("tensor", [hcst] if cc == 0 else [trA[0]],
                           lambda e, cc=cc: e.transpose(pat[:], Abf[:, cc * 128:(cc + 1) * 128],
                                                        identb[:]))
                trA[cc] = es.ev("vector", [t1], lambda e, cc=cc: e.tensor_copy(
                    out=hats[:, 512 + cc * 128:512 + cc * 128 + 128], in_=pat[:]))
            prevy = {}
            for c in range(DC):
                def ym(e, c=c, q=q):
                    for cc in range(2):
                        e.matmul(pys[c % 2][:], wop[:, cc * 1024 + c * 128: cc * 1024 + c * 128 + 128],
                                 hats[:, 512 + cc * 128:512 + cc * 128 + 128],
                                 start=(cc == 0), stop=(cc == 1))
                    return e.matmul(pys[c % 2][0:1, 0:1], identb[0:1, 0:1], identb[0:1, 0:1],
                                    start=False, stop=True) if False else None
                def ym2(e, c=c, q=q):
                    r = None
                    for cc in range(2):
                        r = e.matmul(pys[c % 2][:], wop[:, cc * 1024 + c * 128: cc * 1024 + c * 128 + 128],
                                     hats[:, 512 + cc * 128:512 + cc * 128 + 128],
                                     start=(cc == 0), stop=(cc == 1))
                    return r
                deps = [trA[1]]
                if c >= 2:
                    deps.append(prevy[c - 2])
                hy = es.ev("tensor", deps, ym2)

                def ya(e, c=c, q=q):
                    sl = slice(c * T + q * 128, c * T + (q + 1) * 128)
                    return e.tensor_add(xT[:, sl], xT[:, sl], pys[c % 2][:])
                prevy[c] = es.ev("vector", [hy], ya)
            prevA = prevy[DC - 1]
        es.emit(f"l0apal{tag}")


def build_stage1():
    b = Bld()
    emit_setup(b)
    emit_weight_prep(b)
    for seq in range(SPB):
        emit_load_x(b, seq, tag=f"s{seq}")
        emit_ln(b, b.sb["xT"], b.sb["xnT"], True, True, tag=f"s{seq}")
        emit_l1ffn(b, tag=f"s{seq}")
        emit_store_x(b, seq, tag=f"s{seq}")
    return b


def build_stage3():
    b = Bld()
    emit_setup(b)
    emit_weight_prep(b)
    emit_seprep(b)
    for seq in range(SPB):
        emit_load_x(b, seq, tag=f"s{seq}")
        emit_ln(b, b.sb["xT"], b.sb["xnT"], True, True, tag=f"s{seq}")
        emit_l0ffn(b, tag=f"s{seq}")
        emit_store_x(b, seq, tag=f"s{seq}")
    return b


def build_stage4():
    b = Bld()
    emit_setup(b)
    emit_weight_prep(b)
    emit_l0attn_prologue(b)
    for seq in range(SPB):
        emit_load_x(b, seq, tag=f"s{seq}")
        emit_ln(b, b.sb["xT"], b.sb["xnT"], True, False, tag=f"s{seq}")
        emit_l0attn(b, tag=f"s{seq}")
        emit_store_x(b, seq, tag=f"s{seq}")
    return b


def build_stage2():
    b = Bld()
    emit_setup(b)
    emit_weight_prep(b)
    for seq in range(SPB):
        emit_load_x(b, seq, tag=f"s{seq}")
        emit_ln(b, b.sb["xT"], b.sb["xnT"], True, False, tag=f"s{seq}")
        emit_l1attn(b, tag=f"s{seq}")
        emit_store_x(b, seq, tag=f"s{seq}")
    return b


def build_full():
    b = Bld()
    emit_setup(b)
    emit_weight_prep(b)
    emit_seprep(b)
    emit_l0attn_prologue(b)
    for seq in range(SPB):
        emit_load_x(b, seq, tag=f"s{seq}")
        emit_ln(b, b.sb["xT"], b.sb["xnT"], True, True, tag=f"al{seq}")
        emit_l0attn(b, tag=f"s{seq}")
        emit_l0ffn(b, tag=f"s{seq}")
        emit_ln(b, b.sb["xT"], b.sb["xnT"], True, False, tag=f"bl{seq}")
        emit_l1attn(b, tag=f"s{seq}")
        emit_l1ffn(b, tag=f"s{seq}")
        emit_store_x(b, seq, tag=f"s{seq}")
    return b


_CACHED = {}


def kernel(**inputs):
    b = build_full()
    in_maps = [host_inputs_for_core(inputs, c) for c in range(8)]
    res = run_bass_kernel_spmd(b.nc, in_maps, core_ids=list(range(8)))
    out = np.concatenate([r["y_tok"].reshape(SPB, T, D) for r in res.results], axis=0)
    return out


# version 02
# speedup vs baseline: 1.0668x; 1.0668x over previous
import numpy as np
import ml_dtypes
from contextlib import ExitStack
import concourse.bass as bass
import concourse.mybir as mybir
from concourse.bass_utils import run_bass_kernel_spmd

F32 = mybir.dt.float32
BF16 = mybir.dt.bfloat16
U32 = mybir.dt.uint32
AF = mybir.ActivationFunctionType
ALU = mybir.AluOpType
AXX = mybir.AxisListType.X
BF = ml_dtypes.bfloat16

B, T, D = 16, 2048, 1024
SPB = 2
DC = D // 128
NT = T // 128
NB = T // 512
KMAX = 15
HID = 64
NSLOT = 4096
HIDF = 4096
NHEAD, HD = 16, 64
LN_EPS = 1e-5


class Sem:
    __slots__ = ("h", "v")

    def __init__(self, h):
        self.h = h
        self.v = 0


class Bld:
    def __init__(self):
        self.es = ExitStack()
        nc = bass.Bass("TRN2", target_bir_lowering=False)
        self.nc = nc
        self.s = {}
        for name in ["dw", "dg", "ds", "pe", "ve", "ac", "po", "mA", "mB", "mC",
                     "dA", "dB", "dC", "dD", "dE", "dF", "dH", "dI"]:
            self.s[name] = Sem(self.es.enter_context(nc.semaphore(name)))
        self.inp = {}
        self.out = {}
        self.dram = {}
        self.sb = {}
        self.vecidx = {}
        self.kvals = {}

    def ein(self, name):
        if name not in self.inp:
            shape, dtype = INPUT_SPECS[name]
            self.inp[name] = self.nc.dram_tensor(name, list(shape), dtype, kind="ExternalInput").ap()
        return self.inp[name]

    def eout(self, name, shape, dtype):
        if name not in self.out:
            self.out[name] = self.nc.dram_tensor(name, list(shape), dtype, kind="ExternalOutput").ap()
        return self.out[name]

    def dr(self, name, shape, dtype):
        if name not in self.dram:
            self.dram[name] = self.nc.dram_tensor(name, list(shape), dtype, kind="Internal")
        return self.dram[name].ap()

    def sbt(self, name, shape=None, dtype=None):
        if name not in self.sb:
            self.sb[name] = self.es.enter_context(self.nc.sbuf_tensor(name, list(shape), dtype))
        return self.sb[name]


def host_constants():
    c = {}
    c["c_ident_bf"] = np.eye(128, dtype=BF)
    c["c_ident_f32"] = np.eye(128, dtype=np.float32)
    half = D // 2
    inv_freq = 1.0 / 10000.0 ** (np.arange(half, dtype=np.float64) / half)
    ang = inv_freq[:, None] * np.arange(T, dtype=np.float64)[None, :]
    c["c_ropecos"] = np.cos(ang).astype(BF)
    c["c_ropesin"] = np.sin(ang).astype(BF)
    c["c_iota_row"] = np.arange(T, dtype=np.float32)[None, :]
    c["c_iota_col"] = np.arange(128, dtype=np.float32)[:, None]
    c["c_hat_iota"] = np.tile(np.arange(16, dtype=np.float32), 16)[None, :]
    tri = np.where(np.arange(128)[:, None] <= np.arange(128)[None, :], 0.0, -1e30).astype(np.float32)
    c["c_triT"] = np.ascontiguousarray(tri)
    c["c_tri_bf"] = np.ascontiguousarray(tri).astype(BF)
    c["c_hat_tile"] = np.broadcast_to(np.tile(np.arange(16, dtype=np.float32), 16)[None, :],
                                      (128, 256)).copy()
    c["c_triTT"] = np.ascontiguousarray(tri.T).astype(BF)
    c["c_neg30r_bf"] = np.full((1, 128), -1e30, dtype=BF)
    c["c_ones_bf"] = np.ones((128, 1), dtype=BF)
    c["c_ones_f32"] = np.ones((128, 1), dtype=np.float32)
    c["c_onesr_bf"] = np.ones((1, 512), dtype=BF)
    c["c_onesr_f32"] = np.ones((1, 512), dtype=np.float32)
    return c


VEC_NAMES = ["gamma_x_T", "gamma_slot_T", "bproj_T", "lngam_T", "lnbet_T", "mur_T", "muk_T"]

INPUT_SPECS = {
    "x_tok": ([SPB * T, D], F32),
    "Wi": ([D, D], F32), "Wp": ([D, D], F32),
    "pal": ([D, 256], F32),
    "Wo": ([D, D], F32),
    "W1": ([17, HID], F32), "b1": ([HID, 1], F32),
    "W2": ([HID, HID], F32), "b2": ([HID, 1], F32),
    "W3": ([HID, 4], F32), "b3": ([4, 1], F32),
    "se_flat": ([NSLOT, D], F32),
    "w1f": ([D, HIDF], F32), "w2f": ([HIDF, D], F32),
    "wqkv": ([D, 3 * D], F32), "wproj": ([D, D], F32),
    "WrT": ([D, D], F32), "WkT": ([D, D], F32), "WvT": ([D, D], F32),
    "c_ident_bf": ([128, 128], BF16),
    "c_ident_f32": ([128, 128], F32),
    "c_ropecos": ([512, T], BF16),
    "c_ropesin": ([512, T], BF16),
    "c_iota_row": ([1, T], F32),
    "c_iota_col": ([128, 1], F32),
    "c_hat_iota": ([1, 256], F32),
    "c_triT": ([128, 128], F32),
    "c_tri_bf": ([128, 128], BF16),
    "c_hat_tile": ([128, 256], F32),
    "c_triTT": ([128, 128], BF16),
    "c_neg30r_bf": ([1, 128], BF16),
    "c_ones_bf": ([128, 1], BF16),
    "c_ones_f32": ([128, 1], F32),
    "c_onesr_bf": ([1, 512], BF16),
    "c_onesr_f32": ([1, 512], F32),
}
for _nm in VEC_NAMES:
    INPUT_SPECS[_nm] = ([128, DC], F32)


def host_inputs_for_core(inputs, core):
    f32 = np.float32
    m = {}
    x = np.asarray(inputs["L0_attn_x"], f32)
    m["x_tok"] = np.ascontiguousarray(x[core * SPB:(core + 1) * SPB].reshape(SPB * T, D))
    m["Wi"] = np.ascontiguousarray(np.asarray(inputs["L0_attn_Wi"], f32))
    m["Wp"] = np.ascontiguousarray(np.asarray(inputs["L0_attn_Wp"], f32))
    m["pal"] = np.ascontiguousarray(np.asarray(inputs["L0_attn_palette"], f32).reshape(D, 256))
    m["Wo"] = np.ascontiguousarray(np.asarray(inputs["L0_attn_Wo"], f32)[0])
    m["W1"] = np.ascontiguousarray(np.asarray(inputs["L0_attn_W1"], f32))
    m["b1"] = np.ascontiguousarray(np.asarray(inputs["L0_attn_b1"], f32)[:, None])
    m["W2"] = np.ascontiguousarray(np.asarray(inputs["L0_attn_W2"], f32))
    m["b2"] = np.ascontiguousarray(np.asarray(inputs["L0_attn_b2"], f32)[:, None])
    wm = np.asarray(inputs["L0_attn_Wm"], f32)
    m["W3"] = np.ascontiguousarray(np.concatenate(
        [np.asarray(inputs["L0_attn_Wc"], f32), wm, wm], axis=1))
    bm = np.asarray(inputs["L0_attn_bm"], f32)
    m["b3"] = np.ascontiguousarray(np.concatenate(
        [np.asarray(inputs["L0_attn_bc"], f32), bm, bm])[:, None])
    m["se_flat"] = np.ascontiguousarray(np.asarray(inputs["L0_ffn_slot_embeds"], f32).reshape(NSLOT, D))
    m["w1f"] = np.ascontiguousarray(np.asarray(inputs["L0_ffn_w1"], f32))
    m["w2f"] = np.ascontiguousarray(np.asarray(inputs["L0_ffn_w2"], f32))
    m["wqkv"] = np.ascontiguousarray(np.asarray(inputs["L1_attn_w_qkv"], f32))
    m["wproj"] = np.ascontiguousarray(np.asarray(inputs["L1_attn_w_proj"], f32))
    m["WrT"] = np.ascontiguousarray(np.asarray(inputs["L1_ffn_W_r"], f32).T)
    m["WkT"] = np.ascontiguousarray(np.asarray(inputs["L1_ffn_W_k"], f32).T)
    m["WvT"] = np.ascontiguousarray(np.asarray(inputs["L1_ffn_W_v"], f32).T)

    def vt(v):
        return np.ascontiguousarray(np.asarray(v, f32).reshape(DC, 128).T)
    m["gamma_x_T"] = vt(inputs["L0_ffn_gamma_x"])
    m["gamma_slot_T"] = vt(inputs["L0_ffn_gamma_slot"])
    m["bproj_T"] = vt(inputs["L1_attn_b_proj"])
    m["lngam_T"] = vt(inputs["L1_attn_ln_gamma"])
    m["lnbet_T"] = vt(inputs["L1_attn_ln_beta"])
    m["mur_T"] = vt(inputs["L1_ffn_mu_r"])
    m["muk_T"] = vt(inputs["L1_ffn_mu_k"])
    m.update(host_constants())
    return m


KONST_VALS = [LN_EPS, 1.0, 0.0, -1e30, 7.5]


def emit_setup(b: Bld):
    nc = b.nc
    dw = b.s["dw"]
    vt = b.sbt("vecs", [128, DC * len(VEC_NAMES)], F32)
    b.vecidx = {nm: i for i, nm in enumerate(VEC_NAMES)}
    kp = b.sbt("kpool", [128, len(KONST_VALS)], F32)
    b.kvals = {v: i for i, v in enumerate(KONST_VALS)}
    consts = [("c_ident_bf", "identb"), ("c_ident_f32", "identf"),
              ("c_iota_col", "iota_col"),
              ("c_triTT", "triTT"), ("c_neg30r_bf", "neg30r_bf"),
              ("c_tri_bf", "tri_bf"), ("c_hat_tile", "hat_tile"),
              ("c_ones_bf", "ones_bf"), ("c_ones_f32", "ones_f32"),
              ("c_onesr_bf", "onesr_bf"), ("c_onesr_f32", "onesr_f32")]
    dw0 = dw.v
    n = 0
    with nc.Block("setup") as blk:
        @blk.vector
        def _(e):
            for v, i in b.kvals.items():
                e.memset(kp[:, i:i + 1], float(v))

        @blk.sync
        def _(e):
            nonlocal n
            for i, nm in enumerate(VEC_NAMES):
                e.dma_start(out=vt[:, i * DC:(i + 1) * DC], in_=b.ein(nm)).then_inc(dw.h, 16)
                n += 1
            for nm, sb in consts:
                shape, dtype = INPUT_SPECS[nm]
                t = b.sbt(sb, shape, dtype)
                e.dma_start(out=t[:], in_=b.ein(nm)).then_inc(dw.h, 16)
                n += 1
            e.wait_ge(dw.h, dw0 + 16 * n)
    dw.v = dw0 + 16 * n
    b.sbt("xT", [128, DC * T], F32)
    b.sbt("xnT", [128, DC * T], BF16)
    b.sbt("f_row", [1, T], F32)
    b.sbt("l0_mc", [128, 64], F32)
    b.sbt("l0_cmx", [32, 128], F32)


def vcol(b: Bld, name, c):
    i = b.vecidx[name]
    return b.sb["vecs"][:, i * DC + c: i * DC + c + 1]


def konst(b: Bld, val):
    i = b.kvals[val]
    return b.sb["kpool"][:, i:i + 1]


def emit_weight_prep(b: Bld):
    nc = b.nc
    dg = b.s["dg"]
    casts = [("Wi", [D, D]), ("Wp", [D, D]), ("w1f", [D, HIDF]), ("w2f", [HIDF, D]),
             ("wqkv", [D, 3 * D]), ("wproj", [D, D]),
             ("WrT", [D, D]), ("WkT", [D, D]), ("WvT", [D, D]),
             ("pal", [D, 256]), ("Wo", [D, D])]
    dg0 = dg.v
    with nc.Block("wprep") as blk:
        @blk.gpsimd
        def _(g):
            for nm, shp in casts:
                d = b.dr(nm + "_bf", shp, BF16)
                g.dma_start(out=d, in_=b.ein(nm)).then_inc(dg.h, 16)
            g.wait_ge(dg.h, dg0 + 16 * len(casts))
    dg.v = dg0 + 16 * len(casts)


def emit_load_x(b: Bld, seq, tag=""):
    nc = b.nc
    pe, ve = b.s["pe"], b.s["ve"]
    dsl = [b.s["dA"], b.s["dB"]]
    d0 = [s.v for s in dsl]
    xT, identf = b.sb["xT"], b.sb["identf"]
    xtok = b.ein("x_tok")
    pe0, ve0 = pe.v, ve.v
    with nc.sbuf_tensor(f"lxt{tag}", [128, 2 * D], F32) as tile, \
         nc.psum_tensor(f"lxp0{tag}", [128, 128], F32) as p0, \
         nc.psum_tensor(f"lxp1{tag}", [128, 128], F32) as p1, \
         nc.psum_tensor(f"lxp2{tag}", [128, 128], F32) as p2, \
         nc.psum_tensor(f"lxp3{tag}", [128, 128], F32) as p3, \
         nc.Block(f"loadx{tag}") as blk:
        pss = [p0, p1, p2, p3]

        @blk.sync
        def _(e):
            for i in range(NT):
                if i >= 2:
                    e.wait_ge(pe.h, pe0 + DC * (i - 1))
                e.dma_start(out=tile[:, (i % 2) * D:(i % 2) * D + D],
                            in_=xtok[seq * T + i * 128: seq * T + (i + 1) * 128, :]
                            ).then_inc(dsl[i % 2].h, 16)

        @blk.tensor
        def _(e):
            for i in range(NT):
                e.wait_ge(dsl[i % 2].h, d0[i % 2] + 16 * (i // 2 + 1))
                buf = tile[:, (i % 2) * D:(i % 2) * D + D]
                for c in range(DC):
                    k = i * DC + c
                    if k >= 4:
                        e.wait_ge(ve.h, ve0 + k - 3)
                    e.transpose(pss[k % 4][:], buf[:, c * 128:(c + 1) * 128], identf[:]).then_inc(pe.h, 1)

        @blk.vector
        def _(e):
            for i in range(NT):
                for c in range(DC):
                    k = i * DC + c
                    e.wait_ge(pe.h, pe0 + k + 1)
                    e.tensor_copy(out=xT[:, c * T + i * 128: c * T + (i + 1) * 128],
                                  in_=pss[k % 4][:]).then_inc(ve.h, 1)
    for i, s in enumerate(dsl):
        s.v = d0[i] + 16 * (NT // 2)
    pe.v = pe0 + NT * DC
    ve.v = ve0 + NT * DC


def emit_store_x(b: Bld, seq, tag=""):
    nc = b.nc
    pe, ve = b.s["pe"], b.s["ve"]
    dsl = [b.s["dE"], b.s["dF"]]
    d0 = [s.v for s in dsl]
    xT, identf = b.sb["xT"], b.sb["identf"]
    ytok = b.eout("y_tok", [SPB * T, D], F32)
    pe0, ve0 = pe.v, ve.v
    with nc.sbuf_tensor(f"sxt{tag}", [128, 2 * D], F32) as tile, \
         nc.psum_tensor(f"sxp0{tag}", [128, 128], F32) as p0, \
         nc.psum_tensor(f"sxp1{tag}", [128, 128], F32) as p1, \
         nc.psum_tensor(f"sxp2{tag}", [128, 128], F32) as p2, \
         nc.psum_tensor(f"sxp3{tag}", [128, 128], F32) as p3, \
         nc.Block(f"storex{tag}") as blk:
        pss = [p0, p1, p2, p3]

        @blk.tensor
        def _(e):
            for i in range(NT):
                if i >= 2:
                    e.wait_ge(dsl[i % 2].h, d0[i % 2] + 16 * (i // 2))
                for c in range(DC):
                    k = i * DC + c
                    if k >= 4:
                        e.wait_ge(ve.h, ve0 + k - 3)
                    e.transpose(pss[k % 4][:], xT[:, c * T + i * 128: c * T + (i + 1) * 128],
                                identf[:]).then_inc(pe.h, 1)

        @blk.vector
        def _(e):
            for i in range(NT):
                buf = tile[:, (i % 2) * D:(i % 2) * D + D]
                for c in range(DC):
                    k = i * DC + c
                    e.wait_ge(pe.h, pe0 + k + 1)
                    e.tensor_copy(out=buf[:, c * 128:(c + 1) * 128], in_=pss[k % 4][:]).then_inc(ve.h, 1)

        @blk.sync
        def _(e):
            for i in range(NT):
                e.wait_ge(ve.h, ve0 + DC * (i + 1))
                e.dma_start(out=ytok[seq * T + i * 128: seq * T + (i + 1) * 128, :],
                            in_=tile[:, (i % 2) * D:(i % 2) * D + D]).then_inc(dsl[i % 2].h, 16)
            e.wait_ge(dsl[0].h, d0[0] + 16 * (NT // 2))
            e.wait_ge(dsl[1].h, d0[1] + 16 * (NT // 2))
    for i, s in enumerate(dsl):
        s.v = d0[i] + 16 * (NT // 2)
    pe.v = pe0 + NT * DC
    ve.v = ve0 + NT * DC


def emit_ln(b: Bld, src, dst, src_is_f32, with_f, gamma=None, beta=None,
            add_into=None, tag=""):
    nc = b.nc
    pe, ve, ac, po, mA = b.s["pe"], b.s["ve"], b.s["ac"], b.s["po"], b.s["mA"]
    ones_bf, onesr_f32 = b.sb["ones_bf"], b.sb["onesr_f32"]
    f_r = b.sb["f_row"][:]
    NK = NB * DC

    def xsl(c, blki):
        return src[:, c * T + blki * 512: c * T + blki * 512 + 512]

    with nc.sbuf_tensor(f"lnrw{tag}", [1, 4 * T], F32) as rows, \
         nc.sbuf_tensor(f"lnhi{tag}", [128, 4 * 512], BF16) as hi, \
         nc.sbuf_tensor(f"lnsq{tag}", [128, 4 * 512], BF16) as sq, \
         nc.psum_tensor(f"lnpA{tag}", [128, 512], F32) as pA, \
         nc.psum_tensor(f"lnpB{tag}", [128, 512], F32) as pB, \
         nc.psum_tensor(f"lnpC{tag}", [128, 512], F32) as pC, \
         nc.psum_tensor(f"lnpD{tag}", [128, 512], F32) as pD:
        ps_s = [pA, pB]
        ps_q = [pC, pD]
        mu_r, rs_r = rows[:, 0:T], rows[:, T:2 * T]
        sA_r, sB_r = rows[:, 2 * T:3 * T], rows[:, 3 * T:4 * T]
        s1_r, s2_r = sA_r, sB_r
        pe0, ve0, ac0, mA0 = pe.v, ve.v, ac.v, mA.v

        def hslot(k):
            return hi[:, (k % 4) * 512:(k % 4) * 512 + 512]

        def qslot(k):
            return sq[:, (k % 4) * 512:(k % 4) * 512 + 512]

        with nc.Block(f"lnstat{tag}") as blk:
            @blk.vector
            def _(e):
                for blki in range(NB):
                    for c in range(DC):
                        k = blki * DC + c
                        if k >= 4:
                            e.wait_ge(pe.h, pe0 + k - 3)
                        e.tensor_copy(out=hslot(k), in_=xsl(c, blki)).then_inc(ve.h, 1)
                    if blki >= 1:
                        e.wait_ge(pe.h, pe0 + DC * blki)
                        e.tensor_copy(out=s1_r[:, (blki - 1) * 512:blki * 512],
                                      in_=ps_s[(blki - 1) % 2][0:1, :])
                        e.tensor_copy(out=s2_r[:, (blki - 1) * 512:blki * 512],
                                      in_=ps_q[(blki - 1) % 2][0:1, :]).then_inc(mA.h, 1)
                e.wait_ge(pe.h, pe0 + DC * NB)
                e.tensor_copy(out=s1_r[:, (NB - 1) * 512:NB * 512], in_=ps_s[(NB - 1) % 2][0:1, :])
                e.tensor_copy(out=s2_r[:, (NB - 1) * 512:NB * 512],
                              in_=ps_q[(NB - 1) % 2][0:1, :]).then_inc(mA.h, 1)

            @blk.scalar
            def _(e):
                for k in range(NK):
                    e.wait_ge(ve.h, ve0 + k + 1)
                    if k >= 4:
                        e.wait_ge(pe.h, pe0 + k - 3)
                    e.activation(out=qslot(k), in_=hslot(k), func=AF.Square).then_inc(ac.h, 1)

            @blk.tensor
            def _(e):
                for blki in range(NB):
                    if blki >= 2:
                        e.wait_ge(mA.h, mA0 + blki - 1)
                    for c in range(DC):
                        k = blki * DC + c
                        e.wait_ge(ac.h, ac0 + k + 1)
                        e.matmul(ps_s[blki % 2][0:1, :], ones_bf[:, 0:1], hslot(k),
                                 start=(c == 0), stop=(c == DC - 1))
                        e.matmul(ps_q[blki % 2][0:1, :], ones_bf[:, 0:1], qslot(k),
                                 start=(c == 0), stop=(c == DC - 1)).then_inc(pe.h, 1)
        pe.v = pe0 + NK
        ve.v = ve0 + NK
        ac.v = ac0 + NK
        mA.v = mA0 + NB

        ve0, ac0 = ve.v, ac.v
        with nc.Block(f"lnrow{tag}") as blk:
            @blk.vector
            def _(e):
                v = ve0

                def step(instr):
                    nonlocal v
                    v += 1
                    instr.then_inc(ve.h, 1)
                    e.wait_ge(ve.h, v)
                step(e.tensor_scalar_mul(mu_r, sA_r, 1.0 / D))
                step(e.tensor_scalar_mul(sA_r, sB_r, 1.0 / D))
                step(e.tensor_mul(sB_r, mu_r, mu_r))
                step(e.tensor_sub(sA_r, sA_r, sB_r))
                step(e.tensor_scalar_max(sA_r, sA_r, 1e-30))
                if with_f:
                    e.wait_ge(ac.h, ac0 + 2)
                    step(e.tensor_sub(sA_r, sB_r, f_r))

            @blk.scalar
            def _(e):
                a = ac0

                def step(instr):
                    nonlocal a
                    a += 1
                    instr.then_inc(ac.h, 1)
                    e.wait_ge(ac.h, a)
                e.wait_ge(ve.h, ve0 + 5)
                step(e.activation(out=sB_r, in_=sA_r, func=AF.Ln, bias=konst(b, LN_EPS)[0:1, :]))
                if with_f:
                    step(e.activation(out=f_r, in_=sA_r, func=AF.Ln))
                step(e.activation(out=rs_r, in_=sB_r, func=AF.Exp, scale=-0.5))
                if with_f:
                    e.wait_ge(ve.h, ve0 + 6)
                    step(e.activation(out=f_r, in_=sA_r, func=AF.Exp, scale=0.5))
        ve.v = ve0 + (6 if with_f else 5)
        ac.v = ac0 + (4 if with_f else 2)

        pe0, ac0, po0, mA0 = pe.v, ac.v, b.s["po"].v, mA.v
        INCPS = 3 if gamma is not None else 1
        bcm = [pA, pB]
        bcr = [pC, pD]
        with nc.sbuf_tensor(f"lnbc{tag}", [128, 2 * 1024], F32) as bc, \
             nc.sbuf_tensor(f"lntmp{tag}", [128, 4 * 512], F32) as tmp, \
             nc.Block(f"lnnorm{tag}") as blk:
            @blk.tensor
            def _(e):
                for blki in range(NB):
                    if blki >= 2:
                        e.wait_ge(ac.h, ac0 + blki - 1)
                    e.matmul(bcm[blki % 2][:], onesr_f32[:, 0:128],
                             mu_r[:, blki * 512:(blki + 1) * 512], start=True, stop=True)
                    e.matmul(bcr[blki % 2][:], onesr_f32[:, 0:128],
                             rs_r[:, blki * 512:(blki + 1) * 512], start=True, stop=True).then_inc(pe.h, 1)

            @blk.scalar
            def _(e):
                for blki in range(NB):
                    e.wait_ge(pe.h, pe0 + blki + 1)
                    if blki >= 2:
                        e.wait_ge(mA.h, mA0 + INCPS * (blki - 1) * DC)
                    s = (blki % 2) * 1024
                    e.copy(out=bc[:, s:s + 512], in_=bcm[blki % 2][:])
                    e.copy(out=bc[:, s + 512:s + 1024], in_=bcr[blki % 2][:]).then_inc(ac.h, 1)

            @blk.gpsimd
            def _(e):
                po = b.s["po"]
                for blki in range(NB):
                    e.wait_ge(ac.h, ac0 + blki + 1)
                    s = (blki % 2) * 1024
                    for c in range(DC):
                        k = blki * DC + c
                        if k >= 4:
                            e.wait_ge(mA.h, mA0 + INCPS * (k - 3))
                        e.tensor_sub(tmp[:, (k % 4) * 512:(k % 4) * 512 + 512],
                                     xsl(c, blki), bc[:, s:s + 512]).then_inc(po.h, 1)

            @blk.vector
            def _(e):
                po = b.s["po"]
                for blki in range(NB):
                    s = (blki % 2) * 1024
                    for c in range(DC):
                        k = blki * DC + c
                        e.wait_ge(po.h, po0 + k + 1)
                        tsl = tmp[:, (k % 4) * 512:(k % 4) * 512 + 512]
                        if gamma is None:
                            dsl_ = dst[:, c * T + blki * 512: c * T + blki * 512 + 512]
                            e.tensor_mul(dsl_, tsl, bc[:, s + 512:s + 1024]).then_inc(mA.h, 1)
                        else:
                            e.tensor_mul(tsl, tsl, bc[:, s + 512:s + 1024]).then_inc(mA.h, 1)
                            e.wait_ge(mA.h, mA0 + 3 * k + 1)
                            if add_into is None:
                                dsl_ = dst[:, c * T + blki * 512: c * T + blki * 512 + 512]
                                e.tensor_scalar(out=dsl_, in0=tsl, scalar1=vcol(b, gamma, c),
                                                scalar2=vcol(b, beta, c), op0=ALU.mult,
                                                op1=ALU.add).then_inc(mA.h, 2)
                            else:
                                e.tensor_scalar(out=tsl, in0=tsl, scalar1=vcol(b, gamma, c),
                                                scalar2=vcol(b, beta, c), op0=ALU.mult,
                                                op1=ALU.add).then_inc(mA.h, 1)
                                e.wait_ge(mA.h, mA0 + 3 * k + 2)
                                asl = add_into[:, c * T + blki * 512: c * T + blki * 512 + 512]
                                e.tensor_add(asl, asl, tsl).then_inc(mA.h, 1)
        pe.v = pe0 + NB
        ac.v = ac0 + NB
        b.s["po"].v = po0 + NK
        mA.v = mA0 + INCPS * NK


def emit_l1ffn(b: Bld, tag=""):
    nc = b.nc
    pe, ve, ac, po = b.s["pe"], b.s["ve"], b.s["ac"], b.s["po"]
    xT, xnT = b.sb["xT"], b.sb["xnT"]
    wr = b.dr("WrT_bf", [D, D], BF16)
    wk = b.dr("WkT_bf", [D, D], BF16)
    wv = b.dr("WvT_bf", [D, D], BF16)

    def shift_build(dst, mu_name, t2, sub):
        ve0 = ve.v
        with nc.Block(f"l1f_sh{tag}_{sub}") as blk:
            @blk.vector
            def _(e):
                v = ve0
                for c in range(DC):
                    e.tensor_scalar(out=t2[:, c:c + 1], in0=vcol(b, mu_name, c),
                                    scalar1=-1.0, scalar2=1.0, op0=ALU.mult, op1=ALU.add
                                    ).then_inc(ve.h, 1)
                    v += 1
                e.wait_ge(ve.h, v)
                for c in range(DC):
                    e.tensor_scalar_mul(dst[:, c * T:(c + 1) * T],
                                        xnT[:, c * T:(c + 1) * T],
                                        vcol(b, mu_name, c)).then_inc(ve.h, 1)
                    v += 1
                    e.wait_ge(ve.h, v)
                    e.scalar_tensor_tensor(out=dst[:, c * T + 1:(c + 1) * T],
                                           in0=xnT[:, c * T:(c + 1) * T - 1],
                                           scalar=t2[:, c:c + 1],
                                           in1=dst[:, c * T + 1:(c + 1) * T],
                                           op0=ALU.mult, op1=ALU.add).then_inc(ve.h, 1)
                    v += 1
        ve.v = ve0 + 3 * DC

    with nc.sbuf_tensor(f"l1f_a{tag}", [128, DC * T], BF16) as xa, \
         nc.sbuf_tensor(f"l1f_k{tag}", [128, DC * T], BF16) as kact, \
         nc.sbuf_tensor(f"l1f_r{tag}", [128, T], BF16) as rch, \
         nc.sbuf_tensor(f"l1f_w{tag}", [128, 2 * D], BF16) as wsl, \
         nc.sbuf_tensor(f"l1f_t{tag}", [128, 2 * 512], F32) as tmpf, \
         nc.sbuf_tensor(f"l1f_c{tag}", [128, 2 * DC], F32) as omc, \
         nc.psum_tensor(f"l1f_p0{tag}", [128, 512], F32) as q0, \
         nc.psum_tensor(f"l1f_p1{tag}", [128, 512], F32) as q1:
        qs = [q0, q1]
        shift_build(xa, "muk_T", omc, "k")
        dsl = [b.s["dA"], b.s["dB"]]
        d0 = [s.v for s in dsl]
        pe0, ve0, ac0 = pe.v, ve.v, ac.v
        with nc.Block(f"l1f_k{tag}") as blk:
            @blk.sync
            def _(e):
                for o in range(DC):
                    if o >= 2:
                        e.wait_ge(pe.h, pe0 + (o - 1) * NB)
                    e.dma_start(out=wsl[:, (o % 2) * D:(o % 2) * D + D].rearrange("p (c j) -> p c j", c=DC),
                                in_=wk[:, o * 128:(o + 1) * 128].rearrange("(c p) j -> p c j", p=128)
                                ).then_inc(dsl[o % 2].h, 16)

            @blk.tensor
            def _(e):
                for o in range(DC):
                    e.wait_ge(dsl[o % 2].h, d0[o % 2] + 16 * (o // 2 + 1))
                    for blki in range(NB):
                        j = o * NB + blki
                        if j >= 2:
                            e.wait_ge(ve.h, ve0 + j - 1)
                        for c in range(DC):
                            mm = e.matmul(qs[j % 2][:],
                                          wsl[:, (o % 2) * D + c * 128:(o % 2) * D + c * 128 + 128],
                                          xa[:, c * T + blki * 512: c * T + blki * 512 + 512],
                                          start=(c == 0), stop=(c == DC - 1))
                            if c == DC - 1:
                                mm.then_inc(pe.h, 1)

            @blk.vector
            def _(e):
                for j in range(DC * NB):
                    e.wait_ge(pe.h, pe0 + j + 1)
                    if j >= 2:
                        e.wait_ge(ac.h, ac0 + j - 1)
                    e.tensor_scalar_max(tmpf[:, (j % 2) * 512:(j % 2) * 512 + 512],
                                        qs[j % 2][:], 0.0).then_inc(ve.h, 1)

            @blk.scalar
            def _(e):
                for o in range(DC):
                    for blki in range(NB):
                        j = o * NB + blki
                        e.wait_ge(ve.h, ve0 + j + 1)
                        e.activation(out=kact[:, o * T + blki * 512: o * T + blki * 512 + 512],
                                     in_=tmpf[:, (j % 2) * 512:(j % 2) * 512 + 512],
                                     func=AF.Square).then_inc(ac.h, 1)
        pe.v = pe0 + DC * NB
        ve.v = ve0 + DC * NB
        ac.v = ac0 + DC * NB
        for i, s in enumerate(dsl):
            s.v = d0[i] + 16 * (DC // 2)
        shift_build(xa, "mur_T", omc, "r")
        pe0, ve0, ac0, po0 = pe.v, ve.v, ac.v, po.v
        d0 = [s.v for s in dsl]
        with nc.psum_tensor(f"l1f_p2{tag}", [128, 512], F32) as q2, \
             nc.psum_tensor(f"l1f_p3{tag}", [128, 512], F32) as q3, \
             nc.Block(f"l1f_rv{tag}") as blk:
            q_r = [q0, q1]
            q_v = [q2, q3]

            @blk.sync
            def _(e):
                for o in range(DC):
                    if o >= 1:
                        e.wait_ge(pe.h, pe0 + o * 2 * NB)
                    e.dma_start(out=wsl[:, 0:D].rearrange("p (c j) -> p c j", c=DC),
                                in_=wr[:, o * 128:(o + 1) * 128].rearrange("(c p) j -> p c j", p=128)
                                ).then_inc(dsl[0].h, 16)
                    e.dma_start(out=wsl[:, D:2 * D].rearrange("p (c j) -> p c j", c=DC),
                                in_=wv[:, o * 128:(o + 1) * 128].rearrange("(c p) j -> p c j", p=128)
                                ).then_inc(dsl[1].h, 16)

            @blk.tensor
            def _(e):
                for o in range(DC):
                    e.wait_ge(dsl[0].h, d0[0] + 16 * (o + 1))
                    e.wait_ge(dsl[1].h, d0[1] + 16 * (o + 1))
                    for blki in range(NB):
                        jr = o * NB + blki
                        if jr >= 2:
                            e.wait_ge(ac.h, ac0 + jr - 1)
                        for c in range(DC):
                            mm = e.matmul(q_r[blki % 2][:], wsl[:, c * 128:c * 128 + 128],
                                          xa[:, c * T + blki * 512: c * T + blki * 512 + 512],
                                          start=(c == 0), stop=(c == DC - 1))
                            if c == DC - 1:
                                mm.then_inc(pe.h, 1)
                        jv = jr
                        if jv >= 2:
                            e.wait_ge(ve.h, ve0 + jv - 1)
                        for c in range(DC):
                            mm = e.matmul(q_v[blki % 2][:], wsl[:, D + c * 128:D + c * 128 + 128],
                                          kact[:, c * T + blki * 512: c * T + blki * 512 + 512],
                                          start=(c == 0), stop=(c == DC - 1))
                            if c == DC - 1:
                                mm.then_inc(pe.h, 1)

            @blk.scalar
            def _(e):
                for o in range(DC):
                    for blki in range(NB):
                        j = o * NB + blki
                        e.wait_ge(pe.h, pe0 + 2 * j + 1)
                        if o >= 1:
                            e.wait_ge(ve.h, ve0 + j - NB + 1)
                        e.activation(out=rch[:, blki * 512:(blki + 1) * 512],
                                     in_=q_r[blki % 2][:], func=AF.Sigmoid).then_inc(ac.h, 1)

            @blk.vector
            def _(e):
                for o in range(DC):
                    for blki in range(NB):
                        j = o * NB + blki
                        e.wait_ge(pe.h, pe0 + 2 * j + 2)
                        e.wait_ge(ac.h, ac0 + j + 1)
                        if j >= 2:
                            e.wait_ge(po.h, po0 + j - 1)
                        e.tensor_mul(tmpf[:, (j % 2) * 512:(j % 2) * 512 + 512],
                                     rch[:, blki * 512:(blki + 1) * 512],
                                     q_v[blki % 2][:]).then_inc(ve.h, 1)

            @blk.gpsimd
            def _(e):
                for o in range(DC):
                    for blki in range(NB):
                        j = o * NB + blki
                        e.wait_ge(ve.h, ve0 + j + 1)
                        xsl_ = xT[:, o * T + blki * 512: o * T + blki * 512 + 512]
                        e.tensor_add(xsl_, xsl_, tmpf[:, (j % 2) * 512:(j % 2) * 512 + 512]
                                     ).then_inc(po.h, 1)
        pe.v = pe0 + 2 * DC * NB
        ac.v = ac0 + DC * NB
        ve.v = ve0 + DC * NB
        po.v = po0 + DC * NB
        for i, s in enumerate(dsl):
            s.v = d0[i] + 16 * DC


ENG_SEM = {"tensor": "pe", "vector": "ve", "scalar": "ac", "gpsimd": "po"}


class ES:
    def __init__(self, b: Bld):
        self.b = b
        self.evs = {k: [] for k in ["sync", "tensor", "vector", "scalar", "gpsimd"]}

    def ev(self, eng, waits, fn, sem=None, inc=None):
        b = self.b
        if sem is None:
            sem = ENG_SEM[eng]
        if inc is None:
            inc = 16 if eng == "sync" else 1
        s = b.s[sem]
        s.v += inc
        h = (s.h, s.v)
        self.evs[eng].append((list(waits), fn, s.h, inc))
        return h

    def emit(self, name):
        nc = self.b.nc
        dma_final = {}
        for waits, fn, semh, inc in self.evs["sync"]:
            dma_final[id(semh)] = (semh, None)
        for waits, fn, semh, inc in self.evs["gpsimd"]:
            if inc >= 16:
                dma_final[id(semh)] = (semh, None)
        finals = {}
        for nm in ["dA", "dB", "dC", "dD", "dE", "dF", "dH", "dI", "dw", "dg", "ds"]:
            s = self.b.s[nm]
            if id(s.h) in dma_final:
                finals[nm] = (s.h, s.v)
        with nc.Block(name) as blk:
            need_gp_final = any(nm in ("dH", "dI", "dg") for nm in finals)
            for eng in ["sync", "tensor", "vector", "scalar", "gpsimd"]:
                evl = self.evs[eng]
                if not evl and not (eng == "sync" and finals) and not (eng == "gpsimd" and need_gp_final):
                    continue

                def body(e, evl=evl, eng=eng):
                    known = {}
                    for waits, fn, semh, inc in evl:
                        best = {}
                        for (h, v) in waits:
                            k = id(h)
                            if v > best.get(k, (None, -1))[1]:
                                best[k] = (h, v)
                        for k, (h, v) in best.items():
                            if v > known.get(k, -1):
                                e.wait_ge(h, v)
                                known[k] = v
                        last = fn(e)
                        assert last is not None, "event fn must return its last instruction"
                        last.then_inc(semh, inc)
                    if eng == "sync":
                        for nm, (h, v) in finals.items():
                            if nm not in ("dH", "dI", "dg"):
                                e.wait_ge(h, v)
                    elif eng == "gpsimd":
                        for nm, (h, v) in finals.items():
                            if nm in ("dH", "dI", "dg"):
                                e.wait_ge(h, v)
                getattr(blk, eng)(body)


def emit_l1attn(b: Bld, tag=""):
    nc = b.nc
    xT, xnT = b.sb["xT"], b.sb["xnT"]
    identb = b.sb["identb"]
    ones_bf, onesr_f32, onesr_bf = b.sb["ones_bf"], b.sb["onesr_f32"], b.sb["onesr_bf"]
    triTT = b.sb["triTT"]
    neg30r_bf = b.sb["neg30r_bf"]
    wq = b.dr("wqkv_bf", [D, 3 * D], BF16)
    wp = b.dr("wproj_bf", [D, D], BF16)

    lrow = b.sb["f_row"]
    zctx = ExitStack()
    qk = zctx.enter_context(nc.sbuf_tensor(f"laqk{tag}", [128, 8 * T], BF16))
    zT = qk
    with nc.sbuf_tensor(f"lao{tag}", [128, DC * T], BF16) as aoT, \
         nc.sbuf_tensor(f"lbr{tag}", [1, 64], F32) as brow, \
         nc.sbuf_tensor(f"lbc{tag}", [128, 16], F32) as nBbc, \
         nc.sbuf_tensor(f"lwsl{tag}", [128, 2 * D], BF16) as wsl, \
         nc.sbuf_tensor(f"lET{tag}", [128, 2 * 512], BF16) as ET, \
         nc.sbuf_tensor(f"lbcs{tag}", [128, 512], F32) as bcs:
        with nc.sbuf_tensor(f"lav{tag}", [128, 16 * 512], BF16) as vtok:
            qT = qk[:, 0:4 * T]
            kT = qk[:, 4 * T:8 * T]
            for g in range(2):
                sqt = aoT[:, 4 * T * g: 4 * T * g + 2 * T]
                es = ES(b)
                with nc.psum_tensor(f"lap0{tag}{g}", [128, 512], F32) as pq0, \
                     nc.psum_tensor(f"lap1{tag}{g}", [128, 512], F32) as pq1:
                    pqs = [pq0, pq1]
                    wload = [None] * 10
                    mmdone = {}
                    evdone = {}
                    for j in range(8):
                        col = (0 if j < 4 else D) + g * 512 + (j % 4) * 128
                        deps = [mmdone[(j - 2, NB - 1)]] if j >= 2 else []

                        def ld(e, j=j, col=col):
                            return e.dma_start(
                                out=wsl[:, (j % 2) * D:(j % 2) * D + D].rearrange("p (c j) -> p c j", c=DC),
                                in_=wq[:, col:col + 128].rearrange("(c p) j -> p c j", p=128))
                        wload[j] = es.ev("sync", deps, ld, sem="dA" if j % 2 == 0 else "dB", inc=16)
                        for blki in range(NB):
                            jj = j * NB + blki
                            deps2 = [wload[j]]
                            if jj >= 2:
                                deps2.append(evdone[jj - 2])

                            def mm(e, j=j, blki=blki, jj=jj):
                                r = None
                                for c in range(DC):
                                    r = e.matmul(pqs[jj % 2][:],
                                                 wsl[:, (j % 2) * D + c * 128:(j % 2) * D + c * 128 + 128],
                                                 xnT[:, c * T + blki * 512: c * T + blki * 512 + 512],
                                                 start=(c == 0), stop=(c == DC - 1))
                                return r
                            mmdone[(j, blki)] = es.ev("tensor", deps2, mm)

                            def ev_(e, j=j, blki=blki, jj=jj):
                                dstq = qT if j < 4 else kT
                                return e.tensor_copy(
                                    out=dstq[:, (j % 4) * T + blki * 512:(j % 4) * T + blki * 512 + 512],
                                    in_=pqs[jj % 2][:])
                            evdone[jj] = es.ev("vector", [mmdone[(j, blki)]], ev_)
                    for hv in range(2):
                        depsv = [mmdone[(7, NB - 1)]] if hv == 0 else [evdone[32 + 16 * hv - 1]]

                        def ldv(e, hv=hv):
                            return e.dma_start(
                                out=wsl[:, 0:2 * D].rearrange("p (c j) -> p c j", c=DC),
                                in_=wq[:, 2 * D + g * 512 + hv * 256:2 * D + g * 512 + hv * 256 + 256
                                       ].rearrange("(c p) j -> p c j", p=128))
                        wload[8 + hv] = es.ev("sync", depsv, ldv, sem="dA" if hv == 0 else "dB", inc=16)
                        for ti in range(NT):
                            jj = 32 + 16 * hv + ti
                            deps3 = [wload[8 + hv]]
                            if jj >= 2:
                                deps3.append(evdone[jj - 2])

                            def mmv(e, ti=ti, jj=jj):
                                r = None
                                for c in range(DC):
                                    r = e.matmul(pqs[jj % 2][0:128, 0:256],
                                                 xnT[:, c * T + ti * 128: c * T + (ti + 1) * 128],
                                                 wsl[:, c * 256:c * 256 + 256],
                                                 start=(c == 0), stop=(c == DC - 1))
                                return r
                            h_mm = es.ev("tensor", deps3, mmv)

                            def evv(e, ti=ti, jj=jj, hv=hv):
                                return e.tensor_copy(out=vtok[:, ti * 512 + hv * 256:ti * 512 + hv * 256 + 256],
                                                     in_=pqs[jj % 2][0:128, 0:256])
                            evdone[jj] = es.ev("vector", [h_mm], evv)
                    es.emit(f"laqkv{tag}{g}")

                es = ES(b)
                with nc.psum_tensor(f"lapB{tag}{g}", [1, 512], F32) as pB0, \
                     nc.psum_tensor(f"lapBb{tag}{g}", [1, 512], F32) as pB1:
                    pBs = [pB0, pB1]
                    sqdone = {}
                    grpdone = {}
                    red = {}
                    for j in range(8):
                        deps = [] if j < 2 else [grpdone[2 * (j - 2) + 1]]

                        def sqf(e, j=j):
                            srcq = qT if j < 4 else kT
                            return e.activation(out=sqt[:, (j % 2) * T:(j % 2) * T + T],
                                                in_=srcq[:, (j % 4) * T:(j % 4) * T + T], func=AF.Square)
                        sqdone[j] = es.ev("scalar", deps, sqf)
                        for half in range(2):
                            gi = 2 * j + half
                            col = (0 if j < 4 else 16) + 8 * g + (j % 4) * 2 + half
                            deps2 = [sqdone[j]]
                            if gi >= 2:
                                deps2.append(red[gi - 2])

                            def mmb(e, j=j, half=half, gi=gi):
                                r = None
                                for blki in range(NB):
                                    r = e.matmul(pBs[gi % 2][:],
                                                 ones_bf[half * 64:half * 64 + 64, 0:1],
                                                 sqt[half * 64:half * 64 + 64,
                                                     (j % 2) * T + blki * 512:(j % 2) * T + blki * 512 + 512],
                                                 start=(blki == 0), stop=(blki == NB - 1))
                                return r
                            grpdone[gi] = es.ev("tensor", deps2, mmb)

                            def redf(e, gi=gi, col=col):
                                return e.reduce_sum(out=brow[:, col:col + 1],
                                                    in_=pBs[gi % 2][:], axis=AXX)
                            red[gi] = es.ev("vector", [grpdone[gi]], redf)
                    es.emit(f"laB{tag}{g}")
                es = ES(b)
                with nc.psum_tensor(f"lapN{tag}{g}", [128, 8], F32) as pN:
                    q0c, k0c, n0c, s0c = 8 * g, 16 + 8 * g, 32 + 8 * g, 48 + 8 * g
                    h1 = es.ev("vector", [], lambda e: e.tensor_mul(
                        brow[:, n0c:n0c + 8], brow[:, q0c:q0c + 8], brow[:, k0c:k0c + 8]))
                    h2 = es.ev("scalar", [h1], lambda e: e.activation(
                        out=brow[:, s0c:s0c + 8], in_=brow[:, n0c:n0c + 8], func=AF.Sqrt))
                    h3 = es.ev("vector", [h2], lambda e: e.tensor_scalar_mul(
                        brow[:, n0c:n0c + 8], brow[:, s0c:s0c + 8], -1.0 / (2.0 * T)))
                    h4 = es.ev("tensor", [h3], lambda e: e.matmul(
                        pN[:], onesr_f32[:, 0:128], brow[:, n0c:n0c + 8], start=True, stop=True))
                    h5 = es.ev("scalar", [h4], lambda e: e.copy(out=nBbc[:, 8 * g:8 * g + 8], in_=pN[:]))
                    es.emit(f"laNB{tag}{g}")
                es = ES(b)
                with nc.psum_tensor(f"laS0{tag}{g}", [128, 512], F32) as pS0, \
                     nc.psum_tensor(f"laS1{tag}{g}", [128, 512], F32) as pS1, \
                     nc.psum_tensor(f"laO{tag}{g}", [64, 512], F32) as pO, \
                     nc.psum_tensor(f"laL{tag}{g}", [1, 512], F32) as pL, \
                     nc.psum_tensor(f"laNb{tag}{g}", [64, 512], F32) as pNb:
                    pSs = [pS0, pS1]
                    normdone = {}
                    exp_hist = []
                    for hl in range(8):
                        h = 8 * g + hl
                        qc = hl // 2
                        h2 = hl % 2
                        for tqb in range(NB):
                            ntsj = 4 * tqb + 4
                            last = None
                            for tsj in range(ntsj):
                                deps = []
                                if len(exp_hist) >= 2:
                                    deps.append(exp_hist[-2])

                                def mmS(e, qc=qc, h2=h2, tqb=tqb, tsj=tsj):
                                    c0 = 128 * tsj - 512 * tqb
                                    has_mask = 0 <= c0 <= 384
                                    r = e.matmul(pSs[tsj % 2][:],
                                                 kT[h2 * 64:h2 * 64 + 64, qc * T + tsj * 128: qc * T + (tsj + 1) * 128],
                                                 qT[h2 * 64:h2 * 64 + 64, qc * T + tqb * 512: qc * T + tqb * 512 + 512],
                                                 start=True, stop=not has_mask)
                                    if has_mask:
                                        if c0 > 0:
                                            r = e.matmul(pSs[tsj % 2][:, 0:c0], neg30r_bf[:],
                                                         onesr_bf[:, 0:c0], start=False, stop=False)
                                        r = e.matmul(pSs[tsj % 2][:, c0:c0 + 128], triTT[:], identb[:],
                                                     start=False, stop=True)
                                    return r
                                hS = es.ev("tensor", deps, mmS)

                                def exf(e, h=h, tsj=tsj):
                                    return e.activation(out=ET[:, (tsj % 2) * 512:(tsj % 2) * 512 + 512],
                                                        in_=pSs[tsj % 2][:], func=AF.Exp, scale=0.125,
                                                        bias=nBbc[:, h:h + 1])
                                eh = es.ev("scalar", [hS], exf)
                                exp_hist.append(eh)

                                dep2 = [eh]
                                if (h, tqb - 1) in normdone:
                                    dep2.append(normdone[(h, tqb - 1)])
                                elif tqb == 0 and (h - 1, NB - 1) in normdone:
                                    dep2.append(normdone[(h - 1, NB - 1)])

                                def mmOL(e, hl=hl, tqb=tqb, tsj=tsj):
                                    e.matmul(pO[:], vtok[:, tsj * 512 + 64 * hl: tsj * 512 + 64 * hl + 64],
                                             ET[:, (tsj % 2) * 512:(tsj % 2) * 512 + 512],
                                             start=(tsj == 0), stop=(tsj == 4 * tqb + 3))
                                    return e.matmul(pL[:], ones_bf[:, 0:1],
                                                    ET[:, (tsj % 2) * 512:(tsj % 2) * 512 + 512],
                                                    start=(tsj == 0), stop=(tsj == 4 * tqb + 3))
                                last = es.ev("tensor", dep2, mmOL)
                            hln0 = es.ev("scalar", [last], lambda e: e.activation(
                                out=lrow[:, 0:512], in_=pL[:], func=AF.Ln))
                            hln = es.ev("scalar", [hln0], lambda e: e.activation(
                                out=lrow[:, 512:1024], in_=lrow[:, 0:512], func=AF.Exp, scale=-1.0))
                            hbc = es.ev("tensor", [hln], lambda e: e.matmul(
                                pNb[:], onesr_f32[:, 0:64], lrow[:, 512:1024], start=True, stop=True))
                            hcp = es.ev("scalar", [hbc], lambda e: e.copy(out=bcs[0:64, 0:512], in_=pNb[:]))

                            def nrm(e, h2=h2, hl=hl, g=g, tqb=tqb):
                                return e.tensor_mul(
                                    aoT[h2 * 64:h2 * 64 + 64, (hl // 2 + 4 * g) * T + tqb * 512:
                                        (hl // 2 + 4 * g) * T + tqb * 512 + 512],
                                    pO[:], bcs[0:64, 0:512])
                            normdone[(h, tqb)] = es.ev("vector", [hcp, last], nrm)
                es.emit(f"lafl{tag}{g}")
        es = ES(b)
        with nc.psum_tensor(f"laP0{tag}", [128, 512], F32) as pP0, \
             nc.psum_tensor(f"laP1{tag}", [128, 512], F32) as pP1:
            pPs = [pP0, pP1]
            wld = {}
            mmd = {}
            evd = {}
            for o in range(DC):
                deps = [mmd[(o - 2, NB - 1)]] if o >= 2 else []

                def ldp(e, o=o):
                    return e.dma_start(
                        out=wsl[:, (o % 2) * D:(o % 2) * D + D].rearrange("p (c j) -> p c j", c=DC),
                        in_=wp[:, o * 128:(o + 1) * 128].rearrange("(c p) j -> p c j", p=128))
                wld[o] = es.ev("sync", deps, ldp, sem="dA" if o % 2 == 0 else "dB", inc=16)
                for blki in range(NB):
                    jj = o * NB + blki
                    deps2 = [wld[o]]
                    if jj >= 2:
                        deps2.append(evd[jj - 2])

                    def mmp(e, o=o, blki=blki, jj=jj):
                        r = None
                        for c in range(DC):
                            r = e.matmul(pPs[jj % 2][:],
                                         wsl[:, (o % 2) * D + c * 128:(o % 2) * D + c * 128 + 128],
                                         aoT[:, c * T + blki * 512: c * T + blki * 512 + 512],
                                         start=(c == 0), stop=(c == DC - 1))
                        return r
                    mmd[(o, blki)] = es.ev("tensor", deps2, mmp)

                    def evp(e, o=o, blki=blki, jj=jj):
                        return e.activation(out=zT[:, o * T + blki * 512: o * T + blki * 512 + 512],
                                            in_=pPs[jj % 2][:], func=AF.Identity,
                                            bias=vcol(b, "bproj_T", o))
                    hz = es.ev("scalar", [mmd[(o, blki)]], evp)

                    def addxn(e, o=o, blki=blki):
                        sl = slice(o * T + blki * 512, o * T + blki * 512 + 512)
                        return e.tensor_add(zT[:, sl], zT[:, sl], xnT[:, sl])
                    evd[jj] = es.ev("vector", [hz], addxn)
            es.emit(f"laproj{tag}")
    emit_ln(b, zT, None, False, False, gamma="lngam_T", beta="lnbet_T",
            add_into=xT, tag=f"li{tag}")
    zctx.close()


def emit_seprep(b: Bld):
    nc = b.nc
    identb = b.sb["identb"]
    seF = b.ein("se_flat")
    seT = b.dr("seT_bf", [D, NSLOT], BF16)
    es = ES(b)
    with nc.sbuf_tensor("sp_in", [128, 2 * D], F32) as xin, \
         nc.sbuf_tensor("sp_sq", [128, D], F32) as sqv, \
         nc.sbuf_tensor("sp_c", [128, 8], F32) as cols, \
         nc.sbuf_tensor("sp_nb", [128, 2 * D], BF16) as nb, \
         nc.sbuf_tensor("sp_st", [128, 2 * D], BF16) as strip, \
         nc.psum_tensor("sp_p0", [128, 128], BF16) as tp0, \
         nc.psum_tensor("sp_p1", [128, 128], BF16) as tp1:
        tps = [tp0, tp1]
        lddone = {}
        nrmdone = {}
        trdone = {}
        evdone = {}
        stdone = {}
        for i in range(NSLOT // 128):
            sl = (i % 2) * D

            def ld(e, i=i, sl=sl):
                return e.dma_start(out=xin[:, sl:sl + D], in_=seF[i * 128:(i + 1) * 128, :])
            deps = [nrmdone[i - 2]] if i >= 2 else []
            lddone[i] = es.ev("sync", deps, ld, sem="dA" if i % 2 == 0 else "dB", inc=16)

            def sq(e, i=i, sl=sl):
                return e.activation(out=sqv[:], in_=xin[:, sl:sl + D], func=AF.Square)
            hsq = es.ev("scalar", [lddone[i]] + ([nrmdone[i - 1]] if i >= 1 else []), sq)

            def red(e, i=i):
                return e.reduce_sum(out=cols[:, 0:1], in_=sqv[:], axis=AXX)
            hred = es.ev("vector", [hsq], red)
            hsq2 = es.ev("scalar", [hred], lambda e: e.activation(out=cols[:, 1:2], in_=cols[:, 0:1], func=AF.Sqrt))
            hmx = es.ev("vector", [hsq2], lambda e: e.tensor_scalar_max(cols[:, 2:3], cols[:, 1:2], 1e-12))
            hrc = es.ev("vector", [hmx], lambda e: e.reciprocal(out=cols[:, 3:4], in_=cols[:, 2:3]))
            hsc = es.ev("vector", [hrc], lambda e: e.tensor_scalar_mul(cols[:, 4:5], cols[:, 3:4], float(np.sqrt(D))))

            def nrm(e, i=i, sl=sl):
                return e.tensor_scalar_mul(nb[:, sl:sl + D], xin[:, sl:sl + D], cols[:, 4:5])
            nrmdone[i] = es.ev("vector", [hsc], nrm)
            for c in range(DC):
                k = i * DC + c
                deps = [nrmdone[i]]
                if k >= 2:
                    deps.append(evdone[k - 2])

                def tr(e, i=i, c=c, k=k, sl=sl):
                    return e.transpose(tps[k % 2][:], nb[:, sl + c * 128:sl + c * 128 + 128], identb[:])
                trdone[k] = es.ev("tensor", deps, tr)

                def ev_(e, i=i, c=c, k=k, sl=sl):
                    return e.tensor_scalar_mul(strip[:, sl + c * 128:sl + c * 128 + 128],
                                               tps[k % 2][:], vcol(b, "gamma_slot_T", c))
                deps = [trdone[k]]
                if i >= 2:
                    deps.append(stdone[i - 2])
                evdone[k] = es.ev("vector", deps, ev_)

            def st(e, i=i, sl=sl):
                return e.dma_start(
                    out=seT.rearrange("(c p) n -> p c n", p=128)[:, :, i * 128:(i + 1) * 128],
                    in_=strip[:, sl:sl + D].rearrange("p (c j) -> p c j", c=DC))
            stdone[i] = es.ev("sync", [evdone[i * DC + DC - 1]], st,
                              sem="dE" if i % 2 == 0 else "dF", inc=16)
        es.emit("seprep")


def emit_l0ffn(b: Bld, tag=""):
    nc = b.nc
    xT, xnT = b.sb["xT"], b.sb["xnT"]
    identb = b.sb["identb"]
    ones_bf, onesr_f32 = b.sb["ones_bf"], b.sb["onesr_f32"]
    f_r = b.sb["f_row"][:]
    seT = b.dr("seT_bf", [D, NSLOT], BF16)
    w1d = b.dr("w1f_bf", [D, HIDF], BF16)
    w2d = b.dr("w2f_bf", [HIDF, D], BF16)
    Lscr = b.dr("l0_Lscr", [T, NSLOT], F32)
    Escr = b.dr("l0_Escr", [NSLOT, T], BF16)
    Pscr = b.dr("l0_proc", [NSLOT, D], BF16)
    RSscr = b.dr("l0_rs", [1, T + 64], F32)

    es = ES(b)
    with nc.sbuf_tensor(f"l0fa{tag}", [128, DC * T], BF16) as xtok:
        actx = ExitStack()
        pbs = [actx.enter_context(nc.psum_tensor(f"l0fp{i}{tag}", [128, 512], F32)) for i in range(2)]
        tqs = [actx.enter_context(nc.psum_tensor(f"l0ft{i}{tag}", [128, 128], BF16)) for i in range(4)]
        bcev = {}
        cpev = {}
        xnrev = {}
        trev = {}
        tevv = {}
        with nc.sbuf_tensor(f"l0fbc{tag}", [128, 2 * 512], F32) as fbc:
            for blki in range(NB):
                deps = [cpev[blki - 2]] if blki >= 2 else []

                def bc(e, blki=blki):
                    return e.matmul(pbs[blki % 2][:], onesr_f32[:, 0:128],
                                    f_r[:, blki * 512:(blki + 1) * 512], start=True, stop=True)
                bcev[blki] = es.ev("tensor", deps, bc)
                deps = [bcev[blki]]
                if blki >= 2:
                    deps.append(xnrev[(blki - 2, DC - 1)])

                def cp(e, blki=blki):
                    return e.copy(out=fbc[:, (blki % 2) * 512:(blki % 2) * 512 + 512],
                                  in_=pbs[blki % 2][:])
                cpev[blki] = es.ev("scalar", deps, cp)
                for c in range(DC):
                    def xr1(e, blki=blki, c=c):
                        sl = slice(c * T + blki * 512, c * T + blki * 512 + 512)
                        return e.tensor_mul(xnT[:, sl], xnT[:, sl],
                                            fbc[:, (blki % 2) * 512:(blki % 2) * 512 + 512])
                    h1 = es.ev("vector", [cpev[blki]], xr1)

                    def xr2(e, blki=blki, c=c):
                        sl = slice(c * T + blki * 512, c * T + blki * 512 + 512)
                        return e.tensor_scalar_mul(xnT[:, sl], xnT[:, sl], vcol(b, "gamma_x_T", c))
                    xnrev[(blki, c)] = es.ev("vector", [h1], xr2)
            for i in range(NT):
                blki = i // 4
                for c in range(DC):
                    k = i * DC + c
                    deps = [xnrev[(blki, c)]]
                    if k >= 4:
                        deps.append(tevv[k - 4])

                    def tr(e, i=i, c=c, k=k):
                        return e.transpose(tqs[k % 4][:], xnT[:, c * T + i * 128: c * T + (i + 1) * 128],
                                           identb[:])
                    trev[k] = es.ev("tensor", [d for d in deps], tr)

                    def tv(e, i=i, c=c, k=k):
                        return e.tensor_copy(out=xtok[:, i * D + c * 128: i * D + c * 128 + 128],
                                             in_=tqs[k % 4][:])
                    tevv[k] = es.ev("vector", [trev[k]], tv)
            es.emit(f"l0fA{tag}")
        actx.close()

        es = ES(b)
        NCH = 8
        with nc.sbuf_tensor(f"l0fse{tag}", [128, 4096], BF16) as sesl, \
             nc.sbuf_tensor(f"l0fls{tag}", [128, 2 * 512], F32) as lsl, \
             nc.sbuf_tensor(f"l0fmc{tag}", [128, 64], F32) as mc, \
             nc.psum_tensor(f"l0fL0{tag}", [128, 512], F32) as pL0, \
             nc.psum_tensor(f"l0fL1{tag}", [128, 512], F32) as pL1, \
             nc.psum_tensor(f"l0fT0{tag}", [128, 128], F32) as pt0, \
             nc.psum_tensor(f"l0fT1{tag}", [128, 128], F32) as pt1:
            pLs = [pL0, pL1]
            pts = [pt0, pt1]
            ldse = {}
            mmL = {}
            evL = {}
            stL = {}
            redm = {}
            trE = {}
            redc = {}
            mxc = {}
            ini = es.ev("vector", [], lambda e: e.memset(mc[:], -1e30))
            for nch in range(NCH):
                deps = [mmL[(nch - 1, NT - 1)]] if nch >= 1 else []

                def lse(e, nch=nch):
                    return e.dma_start(
                        out=sesl[:].rearrange("p (c j) -> p c j", c=DC),
                        in_=seT[:, nch * 512:(nch + 1) * 512].rearrange("(c p) j -> p c j", p=128))
                ldse[nch] = es.ev("sync", deps, lse, sem="dA" if nch % 2 == 0 else "dB", inc=16)
                for i in range(NT):
                    j = nch * NT + i
                    deps = [ldse[nch], ini]
                    if j >= 2:
                        deps.append(evL[j - 2])

                    def mm(e, nch=nch, i=i, j=j):
                        r = None
                        for c in range(DC):
                            r = e.matmul(pLs[j % 2][:],
                                         xnT[:, c * T + i * 128: c * T + (i + 1) * 128],
                                         sesl[:, c * 512:(c + 1) * 512],
                                         start=(c == 0), stop=(c == DC - 1))
                        return r
                    mmL[(nch, i)] = es.ev("tensor", [d for d in deps], mm)

                    def rm1(e, nch=nch, i=i, j=j):
                        return e.reduce_max(out=mc[:, 48:49], in_=pLs[j % 2][:], axis=AXX)
                    rmh = es.ev("vector", [mmL[(nch, i)]] + ([redm[j - 1]] if j >= 1 else []), rm1)

                    def rm2(e, i=i):
                        return e.tensor_max(mc[:, i:i + 1], mc[:, i:i + 1], mc[:, 48:49])
                    redm[j] = es.ev("vector", [rmh], rm2)

                    def evc(e, nch=nch, i=i, j=j):
                        return e.activation(out=lsl[:, (j % 2) * 512:(j % 2) * 512 + 512],
                                            in_=pLs[j % 2][:], func=AF.Identity)
                    deps = [redm[j]]
                    if j >= 2:
                        deps.append(stL[j - 2])
                        deps.append(trE[(j - 2, 3)])
                    evL[j] = es.ev("scalar", [d for d in deps], evc)

                    def st(e, nch=nch, i=i, j=j):
                        return e.dma_start(
                            out=Lscr[i * 128:(i + 1) * 128, nch * 512:(nch + 1) * 512],
                            in_=lsl[:, (j % 2) * 512:(j % 2) * 512 + 512])
                    stL[j] = es.ev("sync", [evL[j]], st, sem="dE" if j % 2 == 0 else "dF", inc=16)
                    for sub in range(4):
                        kk = j * 4 + sub
                        deps = [evL[j]]
                        if kk >= 2:
                            deps.append(redc[kk - 2])

                        def tr(e, j=j, sub=sub, kk=kk):
                            return e.transpose(pts[kk % 2][:],
                                               lsl[:, (j % 2) * 512 + sub * 128:(j % 2) * 512 + sub * 128 + 128],
                                               b.sb["identf"][:])
                        trE[(j, sub)] = es.ev("tensor", [d for d in deps], tr)

                        def rc(e, nch=nch, sub=sub, kk=kk):
                            return e.reduce_max(out=mc[:, 48 + 2 + (kk % 2):48 + 3 + (kk % 2)],
                                                in_=pts[kk % 2][:], axis=AXX)
                        rdeps = [trE[(j, sub)]]
                        if kk >= 2:
                            rdeps.append(mxc[kk - 2])
                        redc[kk] = es.ev("vector", rdeps, rc)

                        def mx(e, nch=nch, sub=sub, kk=kk):
                            col = 16 + nch * 4 + sub
                            return e.tensor_max(mc[:, col:col + 1], mc[:, col:col + 1],
                                                mc[:, 48 + 2 + (kk % 2):48 + 3 + (kk % 2)])
                        mxc[kk] = es.ev("vector", [redc[kk]] + ([mxc[kk - 1]] if kk >= 1 else []), mx)
            def negm(e):
                return e.tensor_scalar_mul(mc[:, 0:16], mc[:, 0:16], -1.0)
            fin2 = es.ev("vector", [redm[NCH * NT - 1], mxc[NCH * NT * 4 - 1]], negm)
            with nc.psum_tensor(f"l0fTc{tag}", [128, 128], F32) as ptc:
                cmx = b.sbt("l0_cmx", [32, 128], F32)
                trc = es.ev("tensor", [fin2], lambda e: e.transpose(
                    ptc[0:32, :], mc[:, 16:48], b.sb["identf"][:]))
                evc2 = es.ev("vector", [trc], lambda e: e.tensor_copy(out=cmx[:], in_=ptc[0:32, :]))
                stcmh = es.ev("sync", [evc2], lambda e: e.dma_start(
                    out=b.dr("l0_cmrow", [1, NSLOT], F32).rearrange("o (c j) -> c j", c=32),
                    in_=cmx[:]), sem="dE", inc=16)
            es.emit(f"l0fB{tag}")

            mcP = b.sbt("l0_mc", [128, 64], F32)
            es = ES(b)
            hcpy = es.ev("vector", [], lambda e: e.tensor_copy(out=mcP[:], in_=mc[:]))
            es.ev("vector", [hcpy], lambda e: e.memset(mcP[:, 16:32], 0.0))
            es.emit(f"l0fBc{tag}")

        EdStrip = xnT[:, 0:16 * 512]
        hcache = xnT[:, 0:32 * 512]
        cmrow_d = b.dr("l0_cmrow", [1, NSLOT], F32)
        es = ES(b)
        octx = ExitStack()
        if True:
            cmbc = octx.enter_context(nc.sbuf_tensor(f"l0cmb{tag}", [128, NSLOT], F32))
            pctx = ExitStack()
            cmr = pctx.enter_context(nc.sbuf_tensor(f"l0cmr{tag}", [1, NSLOT], F32))
            pc0 = pctx.enter_context(nc.psum_tensor(f"l0pc0{tag}", [128, 512], F32))
            pc1 = pctx.enter_context(nc.psum_tensor(f"l0pc1{tag}", [128, 512], F32))
            ld = es.ev("sync", [], lambda e: e.dma_start(out=cmr[:], in_=cmrow_d),
                       sem="dA", inc=16)
            prev = {}
            for nch2 in range(8):
                deps = [ld] + ([prev[nch2 - 2]] if nch2 >= 2 else [])

                def bcc(e, nch2=nch2):
                    return e.matmul(pc0[:] if nch2 % 2 == 0 else pc1[:], onesr_f32[:, 0:128],
                                    cmr[:, nch2 * 512:(nch2 + 1) * 512], start=True, stop=True)
                h1 = es.ev("tensor", deps, bcc)

                def cpc(e, nch2=nch2):
                    return e.copy(out=cmbc[:, nch2 * 512:(nch2 + 1) * 512],
                                  in_=pc0[:] if nch2 % 2 == 0 else pc1[:])
                prev[nch2] = es.ev("scalar", [h1], cpc)
            es.emit(f"l0fCbc{tag}")
            pctx.close()

            lbuf = octx.enter_context(nc.sbuf_tensor(f"l0led{tag}", [128, 4 * 512], F32))
            ebuf = octx.enter_context(nc.sbuf_tensor(f"l0esl{tag}", [128, 2 * 512], BF16))
            epc = octx.enter_context(nc.sbuf_tensor(f"l0epc{tag}", [128, 2 * 512], BF16))
            slostr = octx.enter_context(nc.sbuf_tensor(f"l0slo{tag}", [128, DC * 512], BF16))
            w1s = octx.enter_context(nc.sbuf_tensor(f"l0w1s{tag}", [128, 2 * D], BF16))
            w2s = octx.enter_context(nc.sbuf_tensor(f"l0w2s{tag}", [128, 2 * D], BF16))
            rrow = octx.enter_context(nc.sbuf_tensor(f"l0rr{tag}", [1, 2 * 512], F32))
            bcs = octx.enter_context(nc.sbuf_tensor(f"l0bcs{tag}", [128, 512], F32))
            if True:
                for et in range(8):
                    es = ES(b)
                    cctx = ExitStack()
                    if True:
                        psl = [cctx.enter_context(nc.psum_tensor(f"l0s{i}{tag}{et}", [128, 512], F32))
                               for i in range(6)]
                        ps0, ps1 = psl[0], psl[1]
                        pcs = cctx.enter_context(nc.psum_tensor(f"l0cs{tag}{et}", [128, 512], F32))
                        ptE = cctx.enter_context(nc.psum_tensor(f"l0tE{tag}{et}", [128, 128], BF16))
                        ldL = {}
                        edv = {}
                        eds = {}
                        mmS = {}
                        esub = {}
                        etr = {}
                        eev = {}
                        est = {}
                        rsum = {}
                        radd = {}
                        for i in range(NT):
                            deps = [eds[i - 2]] if i >= 2 else []

                            def lL(e, i=i, et=et):
                                return e.dma_start(out=lbuf[:, (i % 2) * 512:(i % 2) * 512 + 512],
                                                   in_=Lscr[i * 128:(i + 1) * 128, et * 512:(et + 1) * 512])
                            ldL[i] = es.ev("sync", deps, lL, sem="dA" if i % 2 == 0 else "dB", inc=16)
                            deps = [ldL[i]]
                            if i >= 1:
                                deps.append(edv[i - 1])

                            def sub(e, i=i, et=et):
                                return e.tensor_sub(lbuf[:, 2 * 512 + (i % 2) * 512:2 * 512 + (i % 2) * 512 + 512],
                                                    lbuf[:, (i % 2) * 512:(i % 2) * 512 + 512],
                                                    cmbc[:, et * 512:(et + 1) * 512])
                            hsub = es.ev("vector", deps, sub)

                            def edx(e, i=i):
                                return e.activation(out=EdStrip[:, i * 512:(i + 1) * 512],
                                                    in_=lbuf[:, 2 * 512 + (i % 2) * 512:2 * 512 + (i % 2) * 512 + 512],
                                                    func=AF.Exp)
                            edv[i] = es.ev("scalar", [hsub], edx)
                            deps = [ldL[i]]
                            if i >= 2:
                                deps.append(etr[(i - 2, 3)])
                                deps.append(rsum[i - 2])

                            def ex2(e, i=i):
                                return e.activation(out=ebuf[:, (i % 2) * 512:(i % 2) * 512 + 512],
                                                    in_=lbuf[:, (i % 2) * 512:(i % 2) * 512 + 512],
                                                    func=AF.Exp, bias=mcP[:, i:i + 1])
                            eds[i] = es.ev("scalar", deps, ex2)

                            def rs1(e, i=i):
                                return e.reduce_sum(out=mcP[:, 48:49], in_=ebuf[:, (i % 2) * 512:(i % 2) * 512 + 512],
                                                    axis=AXX)
                            rdeps = [eds[i]]
                            if i >= 1:
                                rdeps.append(radd[i - 1])
                            rsum[i] = es.ev("vector", rdeps, rs1)

                            def rs2(e, i=i):
                                return e.tensor_add(mcP[:, 16 + i:17 + i], mcP[:, 16 + i:17 + i], mcP[:, 48:49])
                            radd[i] = es.ev("gpsimd", [rsum[i]], rs2)
                            for sub2 in range(4):
                                kk = i * 4 + sub2
                                deps = [eds[i]]
                                if kk >= 1:
                                    deps.append(eev[kk - 1])

                                def trE2(e, i=i, sub2=sub2):
                                    return e.transpose(ptE[:],
                                                       ebuf[:, (i % 2) * 512 + sub2 * 128:(i % 2) * 512 + sub2 * 128 + 128],
                                                       identb[:])
                                etr[(i, sub2)] = es.ev("tensor", deps, trE2)

                                def evE(e, i=i, sub2=sub2, kk=kk):
                                    return e.tensor_copy(out=epc[:, (i % 2) * 512 + sub2 * 128:
                                                                 (i % 2) * 512 + sub2 * 128 + 128],
                                                         in_=ptE[:])
                                deps = [etr[(i, sub2)]]
                                if i >= 2:
                                    deps.append(est[i - 2])
                                eev[kk] = es.ev("vector", deps, evE)

                            def stE(e, i=i, et=et):
                                return e.dma_start(
                                    out=Escr[et * 512:(et + 1) * 512, i * 128:(i + 1) * 128
                                             ].rearrange("(s p) j -> p s j", p=128),
                                    in_=epc[:, (i % 2) * 512:(i % 2) * 512 + 512
                                            ].rearrange("p (s j) -> p s j", s=4))
                            est[i] = es.ev("sync", [eev[i * 4 + 3]], stE,
                                           sem="dE" if i % 2 == 0 else "dF", inc=16)
                            deps = [edv[i]]

                            def mmp1(e, i=i):
                                for j in range(6):
                                    e.matmul(psl[j][:], xtok[:, i * D + j * 128: i * D + j * 128 + 128],
                                             EdStrip[:, i * 512:(i + 1) * 512],
                                             start=(i == 0), stop=(i == NT - 1))
                                return e.matmul(pcs[0:1, :], ones_bf[:, 0:1],
                                                EdStrip[:, i * 512:(i + 1) * 512],
                                                start=(i == 0), stop=(i == NT - 1))
                            mmS[i] = es.ev("tensor", deps, mmp1)
                        lastm = mmS[NT - 1]
                        hl1 = es.ev("scalar", [lastm], lambda e: e.activation(
                            out=rrow[:, 0:512], in_=pcs[0:1, :], func=AF.Ln))
                        hl2 = es.ev("scalar", [hl1], lambda e: e.activation(
                            out=rrow[:, 512:1024], in_=rrow[:, 0:512], func=AF.Exp, scale=-1.0))
                        hbc2 = es.ev("tensor", [hl2], lambda e: e.matmul(
                            pcs[:], onesr_f32[:, 0:128], rrow[:, 512:1024], start=True, stop=True))
                        hcp2 = es.ev("scalar", [hbc2], lambda e: e.copy(out=bcs[:], in_=pcs[:]))
                        sev = {}
                        for j in range(6):
                            def sv(e, j=j):
                                return e.tensor_mul(slostr[:, j * 512:(j + 1) * 512], psl[j][:], bcs[:])
                            sev[j] = es.ev("vector", [hcp2], sv)
                        mm2 = {}
                        for i in range(NT):
                            deps = [sev[0], sev[1]] if i == 0 else [mm2[i - 1]]

                            def mmp2(e, i=i):
                                e.matmul(ps0[:], xtok[:, i * D + 6 * 128: i * D + 6 * 128 + 128],
                                         EdStrip[:, i * 512:(i + 1) * 512],
                                         start=(i == 0), stop=(i == NT - 1))
                                return e.matmul(ps1[:], xtok[:, i * D + 7 * 128: i * D + 7 * 128 + 128],
                                                EdStrip[:, i * 512:(i + 1) * 512],
                                                start=(i == 0), stop=(i == NT - 1))
                            mm2[i] = es.ev("tensor", deps, mmp2)
                        for j in range(2):
                            def sv2(e, j=j):
                                return e.tensor_mul(slostr[:, (6 + j) * 512:(7 + j) * 512],
                                                    (ps0 if j == 0 else ps1)[:], bcs[:])
                            sev[6 + j] = es.ev("vector", [mm2[NT - 1]], sv2)
                        es.emit(f"l0fC{tag}{et}")
                    cctx.close()

                    es = ES(b)
                    cctx = ExitStack()
                    if True:
                        ph = cctx.enter_context(nc.psum_tensor(f"l0h{tag}{et}", [128, 512], F32))
                        prs = [cctx.enter_context(nc.psum_tensor(f"l0pr{i}{tag}{et}", [128, 512], F32))
                               for i in range(6)]
                        pr0, pr1 = prs[0], prs[1]
                        ldw1 = {}
                        ldw2 = {}
                        mmh = {}
                        gel = {}
                        mpr = {}
                        for hc in range(32):
                            deps = [mmh[hc - 2]] if hc >= 2 else []

                            def lw1(e, hc=hc):
                                return e.dma_start(
                                    out=w1s[:, (hc % 2) * D:(hc % 2) * D + D].rearrange("p (c j) -> p c j", c=DC),
                                    in_=w1d[:, hc * 128:(hc + 1) * 128].rearrange("(c p) j -> p c j", p=128))
                            ldw1[hc] = es.ev("sync", deps, lw1, sem="dA" if hc % 2 == 0 else "dB", inc=16)
                            deps = [mpr[hc - 2]] if hc >= 2 else []

                            def lw2(e, hc=hc):
                                return e.dma_start(out=w2s[:, (hc % 2) * D:(hc % 2) * D + D],
                                                   in_=w2d[hc * 128:(hc + 1) * 128, :])
                            ldw2[hc] = es.ev("sync", deps, lw2, sem="dC" if hc % 2 == 0 else "dD", inc=16)

                            def mh(e, hc=hc):
                                r = None
                                for c in range(DC):
                                    r = e.matmul(ph[:], w1s[:, (hc % 2) * D + c * 128:(hc % 2) * D + c * 128 + 128],
                                                 slostr[:, c * 512:(c + 1) * 512],
                                                 start=(c == 0), stop=(c == DC - 1))
                                return r
                            deps = [ldw1[hc]]
                            if hc >= 1:
                                deps.append(gel[hc - 1])
                            mmh[hc] = es.ev("tensor", deps, mh)

                            def gl(e, hc=hc):
                                return e.activation(out=hcache[:, hc * 512:(hc + 1) * 512],
                                                    in_=ph[:], func=AF.Gelu)
                            gel[hc] = es.ev("scalar", [mmh[hc]], gl)

                            def mp(e, hc=hc):
                                r = None
                                for sub3 in range(3):
                                    for dh in range(2):
                                        r = e.matmul(prs[sub3 * 2 + dh][:],
                                                     hcache[:, hc * 512 + sub3 * 128: hc * 512 + sub3 * 128 + 128],
                                                     w2s[:, (hc % 2) * D + dh * 512:(hc % 2) * D + dh * 512 + 512],
                                                     start=(hc == 0), stop=(hc == 31))
                                return r
                            mpr[hc] = es.ev("tensor", [gel[hc], ldw2[hc]], mp)
                        pstr = {}
                        pst = {}
                        for sub3 in range(3):
                            def pv(e, sub3=sub3):
                                e.tensor_copy(out=epc[:, 0:512], in_=prs[sub3 * 2][:])
                                return e.tensor_copy(out=epc[:, 512:1024], in_=prs[sub3 * 2 + 1][:])
                            deps = [mpr[31]]
                            if sub3 >= 1:
                                deps.append(pst[sub3 - 1])
                            pstr[sub3] = es.ev("vector", deps, pv, inc=2)

                            def pw(e, sub3=sub3, et=et):
                                return e.dma_start(
                                    out=Pscr[et * 512 + sub3 * 128: et * 512 + (sub3 + 1) * 128, :],
                                    in_=epc[:, 0:1024])
                            pst[sub3] = es.ev("sync", [pstr[sub3]], pw, sem="dE", inc=16)
                        mm3 = {}
                        lw2b = {}
                        for hc in range(32):
                            deps = [pst[0]] if hc == 0 else [mm3[hc - 1]]

                            def lw2c(e, hc=hc):
                                return e.dma_start(out=w2s[:, (hc % 2) * D:(hc % 2) * D + D],
                                                   in_=w2d[hc * 128:(hc + 1) * 128, :])
                            lw2b[hc] = es.ev("sync", deps + ([lw2b[hc - 2]] if hc >= 2 else []), lw2c,
                                             sem="dC" if hc % 2 == 0 else "dD", inc=16)

                            def m3(e, hc=hc):
                                e.matmul(pr0[:], hcache[:, hc * 512 + 3 * 128: hc * 512 + 4 * 128],
                                         w2s[:, (hc % 2) * D: (hc % 2) * D + 512],
                                         start=(hc == 0), stop=(hc == 31))
                                return e.matmul(pr1[:], hcache[:, hc * 512 + 3 * 128: hc * 512 + 4 * 128],
                                                w2s[:, (hc % 2) * D + 512: (hc % 2) * D + 1024],
                                                start=(hc == 0), stop=(hc == 31))
                            mm3[hc] = es.ev("tensor", [lw2b[hc]], m3)

                        def pv3(e):
                            e.tensor_copy(out=epc[:, 0:512], in_=pr0[:])
                            return e.tensor_copy(out=epc[:, 512:1024], in_=pr1[:])
                        h3v = es.ev("vector", [mm3[31], pst[2]], pv3, inc=2)

                        def pw3(e, et=et):
                            return e.dma_start(out=Pscr[et * 512 + 3 * 128: et * 512 + 4 * 128, :],
                                               in_=epc[:, 0:1024])
                        es.ev("sync", [h3v], pw3, sem="dE", inc=16)
                        es.emit(f"l0fD{tag}{et}")
                    cctx.close()
            octx.close()

        es = ES(b)
        with nc.sbuf_tensor(f"l0rsx{tag}", [16, 128], F32) as rsx, \
             nc.sbuf_tensor(f"l0rsr{tag}", [1, T], F32) as rsr, \
             nc.sbuf_tensor(f"l0rb{tag}", [128, 4 * 512], F32) as rsbc:
            petx = ExitStack()
            pE0 = petx.enter_context(nc.psum_tensor(f"l0pe0{tag}", [128, 128], F32))
            h1 = es.ev("vector", [], lambda e: e.reciprocal(out=mcP[:, 48:64], in_=mcP[:, 16:32]))
            h2 = es.ev("tensor", [h1], lambda e: e.transpose(pE0[0:16, :], mcP[:, 48:64], b.sb["identf"][:]))
            h3 = es.ev("vector", [h2], lambda e: e.tensor_copy(out=rsx[:], in_=pE0[0:16, :]))
            h4 = es.ev("sync", [h3], lambda e: e.dma_start(
                out=RSscr[:, 0:T].rearrange("o (c j) -> c j", c=16), in_=rsx[:]), sem="dE", inc=16)
            h5 = es.ev("sync", [h4], lambda e: e.dma_start(out=rsr[:], in_=RSscr[:, 0:T]), sem="dF", inc=16)
            prevq = {}
            for q in range(NB):
                def bcq(e, q=q):
                    return e.matmul(pE0[:] if False else pE0[:, 0:128], onesr_f32[:, 0:128],
                                    rsr[:, q * 512:q * 512 + 128], start=True, stop=True)
                for pp in range(4):
                    def bcp(e, q=q, pp=pp):
                        return e.matmul(pE0[:, 0:128], onesr_f32[:, 0:128],
                                        rsr[:, q * 512 + pp * 128:q * 512 + pp * 128 + 128],
                                        start=True, stop=True)
                    deps = [h5] + ([prevq[(q, pp - 1)]] if pp >= 1 else ([prevq[(q - 1, 3)]] if q >= 1 else []))
                    hb = es.ev("tensor", deps, bcp)

                    def cpq(e, q=q, pp=pp):
                        return e.copy(out=rsbc[:, q * 512 + pp * 128:q * 512 + pp * 128 + 128],
                                      in_=pE0[:, 0:128])
                    prevq[(q, pp)] = es.ev("scalar", [hb], cpq)
            es.emit(f"l0fEbc{tag}")
            petx.close()

            es = ES(b)
            ectx = ExitStack()
            if True:
                psl2 = ectx.enter_context(nc.sbuf_tensor(f"l0pp{tag}", [128, 2 * D], BF16))
                esl2 = ectx.enter_context(nc.sbuf_tensor(f"l0ee{tag}", [128, 2 * 512], BF16))
                tmq = ectx.enter_context(nc.sbuf_tensor(f"l0tm{tag}", [128, 2 * 512], F32))
                qs2 = [ectx.enter_context(nc.psum_tensor(f"l0q{i}{tag}", [128, 512], F32))
                       for i in range(8)]
                ldp = {}
                lde = {}
                mmq = {}
                evq = {}
                adq = {}
                for q in range(NB):
                    for nt2 in range(32):
                        jj = q * 32 + nt2
                        deps = [mmq[jj - 2]] if jj >= 2 else []

                        def lp(e, nt2=nt2, jj=jj):
                            return e.dma_start(out=psl2[:, (jj % 2) * D:(jj % 2) * D + D],
                                               in_=Pscr[nt2 * 128:(nt2 + 1) * 128, :])
                        ldp[jj] = es.ev("sync", deps, lp, sem="dA" if jj % 2 == 0 else "dB", inc=16)

                        def le_(e, q=q, nt2=nt2, jj=jj):
                            return e.dma_start(out=esl2[:, (jj % 2) * 512:(jj % 2) * 512 + 512],
                                               in_=Escr[nt2 * 128:(nt2 + 1) * 128, q * 512:(q + 1) * 512])
                        lde[jj] = es.ev("sync", deps, le_, sem="dC" if jj % 2 == 0 else "dD", inc=16)

                        def mq(e, q=q, nt2=nt2, jj=jj):
                            r = None
                            for c in range(DC):
                                r = e.matmul(qs2[c][:], psl2[:, (jj % 2) * D + c * 128:(jj % 2) * D + c * 128 + 128],
                                             esl2[:, (jj % 2) * 512:(jj % 2) * 512 + 512],
                                             start=(nt2 == 0), stop=(nt2 == 31))
                            return r
                        deps2 = [ldp[jj], lde[jj]]
                        if nt2 == 0 and q >= 1:
                            deps2.append(adq[(q - 1, DC - 1)])
                        mmq[jj] = es.ev("tensor", deps2, mq)
                    for c in range(DC):
                        def eq(e, q=q, c=c):
                            return e.tensor_mul(tmq[:, (c % 2) * 512:(c % 2) * 512 + 512],
                                                qs2[c][:], rsbc[:, q * 512:(q + 1) * 512])
                        deps = [mmq[q * 32 + 31]]
                        if c >= 2:
                            deps.append(adq[(q, c - 2)])
                        evq[(q, c)] = es.ev("vector", deps, eq)

                        def aq(e, q=q, c=c):
                            sl = slice(c * T + q * 512, c * T + q * 512 + 512)
                            return e.tensor_add(xT[:, sl], xT[:, sl], tmq[:, (c % 2) * 512:(c % 2) * 512 + 512])
                        adq[(q, c)] = es.ev("gpsimd", [evq[(q, c)]], aq)
                es.emit(f"l0fE{tag}")
            ectx.close()
    return


def emit_l0attn_prologue(b: Bld):
    nc = b.nc
    pald = b.dr("pal_bf", [D, 256], BF16)
    wod = b.dr("Wo_bf", [D, D], BF16)
    wop = b.sbt("wop", [128, 2 * 1024], BF16)
    w1b = b.sbt("w1b", [17, HID], BF16)
    w2b = b.sbt("w2b", [HID, HID], BF16)
    w3b = b.sbt("w3b", [HID, 4], BF16)
    bcol = b.sbt("bcol", [HID, 4], F32)
    es = ES(b)
    with nc.sbuf_tensor("wopsl", [128, 2 * 640], BF16) as wsl, \
         nc.sbuf_tensor("wopf", [HID, 3 * HID], F32) as wf, \
         nc.psum_tensor("wopp0", [128, 512], F32) as p0, \
         nc.psum_tensor("wopp1", [128, 512], F32) as p1:
        ps = [p0, p1]
        l1 = es.ev("sync", [], lambda e: e.dma_start(out=wf[0:17, 0:HID], in_=b.ein("W1")),
                   sem="dC", inc=16)
        c1 = es.ev("vector", [l1], lambda e: e.tensor_copy(out=w1b[:], in_=wf[0:17, 0:HID]))
        l2 = es.ev("sync", [c1], lambda e: e.dma_start(out=wf[:, 0:HID], in_=b.ein("W2")), sem="dC", inc=16)
        c2 = es.ev("vector", [l2], lambda e: e.tensor_copy(out=w2b[:], in_=wf[:, 0:HID]))
        l3 = es.ev("sync", [c2], lambda e: e.dma_start(out=wf[:, 0:4], in_=b.ein("W3")), sem="dC", inc=16)
        c3 = es.ev("vector", [l3], lambda e: e.tensor_copy(out=w3b[:], in_=wf[:, 0:4]))
        hb1 = es.ev("sync", [c3], lambda e: e.dma_start(out=bcol[:, 0:1], in_=b.ein("b1")), sem="dC", inc=16)
        hb2 = es.ev("sync", [], lambda e: e.dma_start(out=bcol[:, 1:2], in_=b.ein("b2")), sem="dD", inc=16)
        hb3 = es.ev("sync", [], lambda e: e.dma_start(out=bcol[0:4, 2:3], in_=b.ein("b3")), sem="dE", inc=16)
        ldv = {}
        mmv = {}
        evv = {}
        ldw = {}
        for jj in range(4):
            cc, dh = jj // 2, jj % 2
            for c in range(DC):
                k = jj * DC + c
                deps = [mmv[k - 2]] if k >= 2 else []

                def ldp_(e, cc=cc, c=c, k=k):
                    return e.dma_start(out=wsl[:, (k % 2) * 640:(k % 2) * 640 + 128],
                                       in_=pald[c * 128:(c + 1) * 128, cc * 128:(cc + 1) * 128])
                ldv[k] = es.ev("sync", deps, ldp_, sem="dA" if k % 2 == 0 else "dB", inc=16)

                def ldw_(e, dh=dh, c=c, k=k):
                    return e.dma_start(out=wsl[:, (k % 2) * 640 + 128:(k % 2) * 640 + 640],
                                       in_=wod[c * 128:(c + 1) * 128, dh * 512:(dh + 1) * 512])
                wdeps = list(deps)
                if k == 0:
                    wdeps.append(hb3)
                elif k == 1:
                    wdeps.append(hb2)
                ldw[k] = es.ev("sync", wdeps, ldw_, sem="dE" if k % 2 == 0 else "dF", inc=16)

                def mm(e, jj=jj, c=c, k=k):
                    return e.matmul(ps[jj % 2][:], wsl[:, (k % 2) * 640:(k % 2) * 640 + 128],
                                    wsl[:, (k % 2) * 640 + 128:(k % 2) * 640 + 640],
                                    start=(c == 0), stop=(c == DC - 1))
                deps = [ldv[k], ldw[k]]
                if jj >= 2 and c == 0:
                    deps.append(evv[jj - 2])
                mmv[k] = es.ev("tensor", deps, mm)

            def ev_(e, cc=cc, dh=dh, jj=jj):
                return e.tensor_copy(out=wop[:, cc * 1024 + dh * 512: cc * 1024 + dh * 512 + 512],
                                     in_=ps[jj % 2][:])
            evv[jj] = es.ev("vector", [mmv[jj * DC + DC - 1]], ev_)
        es.emit("woprep")


def emit_l0attn(b: Bld, tag=""):
    nc = b.nc
    xT, xnT = b.sb["xT"], b.sb["xnT"]
    identb, identf = b.sb["identb"], b.sb["identf"]
    ones_bf, onesr_f32, onesr_bf = b.sb["ones_bf"], b.sb["onesr_f32"], b.sb["onesr_bf"]
    tri_bf, neg30r_bf = b.sb["tri_bf"], b.sb["neg30r_bf"]
    iota_col = b.sb["iota_col"]
    wid = b.dr("Wi_bf", [D, D], BF16)
    wpd = b.dr("Wp_bf", [D, D], BF16)
    PHAT = b.dr("l0a_phat", [2 * T, D], BF16)
    TKI = b.dr("l0a_tki", [T * 16], F32)
    KEEP = b.dr("l0a_keep", [T * 16], F32)
    DELTA = b.dr("l0a_delta", [T * 16], BF16)
    GRM = b.dr("l0a_grm", [256 * 128, 128], BF16)
    Z3 = b.dr("l0a_z3", [4, T * 16], BF16)
    ropec = b.ein("c_ropecos")
    ropes = b.ein("c_ropesin")

    itctx = ExitStack()
    ITb = itctx.enter_context(nc.sbuf_tensor(f"l0aI{tag}", [128, DC * T], BF16))
    PTb = itctx.enter_context(nc.sbuf_tensor(f"l0aP{tag}", [128, DC * T], BF16))

    for mi, (wsrc, dstT) in enumerate([(wid, ITb), (wpd, PTb)]):
        es = ES(b)
        with nc.sbuf_tensor(f"l0aw{tag}{mi}", [128, 2 * D], BF16) as wsl, \
             nc.sbuf_tensor(f"l0acs{tag}{mi}", [128, 4 * 512], BF16) as cssl, \
             nc.sbuf_tensor(f"l0at{tag}{mi}", [128, 2 * 512], F32) as tbuf, \
             nc.psum_tensor(f"l0ap0{tag}{mi}", [128, 512], F32) as q0, \
             nc.psum_tensor(f"l0ap1{tag}{mi}", [128, 512], F32) as q1:
            qs = [q0, q1]
            wld = {}
            mmd = {}
            rpd = {}
            csd = {}
            for p in range(4):
                for half in range(2):
                    o = p + 4 * half
                    jj = p * 2 + half
                    deps = [mmd[(p - 1, half)]] if p >= 1 else []

                    def ld(e, o=o, jj=jj, wsrc=wsrc):
                        return e.dma_start(
                            out=wsl[:, (jj % 2) * D:(jj % 2) * D + D].rearrange("p (c j) -> p c j", c=DC),
                            in_=wsrc[:, o * 128:(o + 1) * 128].rearrange("(c p) j -> p c j", p=128))
                    wld[(p, half)] = es.ev("sync", deps, ld, sem="dA" if jj % 2 == 0 else "dB", inc=16)
                for blk in range(NB):
                    if blk >= 2:
                        deps = [rpd[(p, blk - 2)]]
                    elif p >= 1:
                        deps = [rpd[(p - 1, blk + 2)]]
                    else:
                        deps = []

                    def ldco(e, p=p, blk=blk):
                        return e.dma_start(out=cssl[:, (blk % 2) * 1024:(blk % 2) * 1024 + 512],
                                           in_=ropec[p * 128:(p + 1) * 128, blk * 512:(blk + 1) * 512])
                    csd1 = es.ev("sync", deps, ldco, sem="dC" if blk % 2 == 0 else "dD", inc=16)

                    def ldsi(e, p=p, blk=blk):
                        return e.dma_start(out=cssl[:, (blk % 2) * 1024 + 512:(blk % 2) * 1024 + 1024],
                                           in_=ropes[p * 128:(p + 1) * 128, blk * 512:(blk + 1) * 512])
                    csd2 = es.ev("sync", list(deps), ldsi, sem="dE" if blk % 2 == 0 else "dF", inc=16)
                    csd[(p, blk)] = csd2
                    for half in range(2):
                        o = p + 4 * half

                        def mm(e, o=o, half=half, blk=blk, jj=p * 2 + half):
                            r = None
                            for c in range(DC):
                                r = e.matmul(qs[half][:],
                                             wsl[:, (jj % 2) * D + c * 128:(jj % 2) * D + c * 128 + 128],
                                             xnT[:, c * T + blk * 512: c * T + blk * 512 + 512],
                                             start=(c == 0), stop=(c == DC - 1))
                            return r
                        deps = [wld[(p, half)]]
                        if blk == 0 and p >= 1:
                            deps.append(rpd[(p - 1, NB - 1)])
                        elif blk >= 1:
                            deps.append(rpd[(p, blk - 1)])
                        mmd[(p, half, blk)] = es.ev("tensor", deps, mm)
                    dep2 = [mmd[(p, 0, blk)], mmd[(p, 1, blk)], csd1, csd2]

                    def rp1(e, p=p, blk=blk):
                        return e.tensor_mul(tbuf[:, 0:512], q0[:],
                                            cssl[:, (blk % 2) * 1024:(blk % 2) * 1024 + 512])
                    h1 = es.ev("vector", dep2, rp1)

                    def rp2(e, p=p, blk=blk):
                        return e.tensor_mul(tbuf[:, 512:1024], q1[:],
                                            cssl[:, (blk % 2) * 1024 + 512:(blk % 2) * 1024 + 1024])
                    h2 = es.ev("gpsimd", [h1], rp2) if False else es.ev("vector", [h1], rp2)

                    def rp3(e, p=p, blk=blk, dstT=dstT):
                        return e.scalar_tensor_tensor(
                            out=dstT[:, p * T + blk * 512: p * T + blk * 512 + 512],
                            in0=tbuf[:, 512:1024], scalar=-1.0, in1=tbuf[:, 0:512],
                            op0=ALU.mult, op1=ALU.add)
                    h3 = es.ev("vector", [h2], rp3)

                    def rp4(e, p=p, blk=blk):
                        return e.tensor_mul(tbuf[:, 0:512], q0[:],
                                            cssl[:, (blk % 2) * 1024 + 512:(blk % 2) * 1024 + 1024])
                    h4 = es.ev("vector", [h3], rp4)

                    def rp5(e, p=p, blk=blk):
                        return e.tensor_mul(tbuf[:, 512:1024], q1[:],
                                            cssl[:, (blk % 2) * 1024:(blk % 2) * 1024 + 512])
                    h5 = es.ev("vector", [h4], rp5)

                    def rp6(e, p=p, blk=blk, dstT=dstT):
                        return e.tensor_add(
                            dstT[:, (p + 4) * T + blk * 512: (p + 4) * T + blk * 512 + 512],
                            tbuf[:, 0:512], tbuf[:, 512:1024])
                    rpd[(p, blk)] = es.ev("vector", [h5], rp6)
                mmd[(p, 0)] = mmd[(p, 0, NB - 1)]
                mmd[(p, 1)] = mmd[(p, 1, NB - 1)]
            es.emit(f"l0aproj{tag}{mi}")

        es = ES(b)
        with nc.sbuf_tensor(f"l0an{tag}{mi}", [128, 1024], BF16) as sqb, \
             nc.sbuf_tensor(f"l0anr{tag}{mi}", [1, T], F32) as nrows, \
             nc.sbuf_tensor(f"l0abc{tag}{mi}", [128, 160], F32) as nbc, \
             nc.sbuf_tensor(f"l0aht{tag}{mi}", [128, 2 * 1024], BF16) as htok, \
             nc.psum_tensor(f"l0an0{tag}{mi}", [1, 512], F32) as pn0, \
             nc.psum_tensor(f"l0an1{tag}{mi}", [1, 512], F32) as pn1, \
             nc.psum_tensor(f"l0anb{tag}{mi}", [128, 512], F32) as pnb, \
             nc.psum_tensor(f"l0att{tag}{mi}", [128, 128], BF16) as ptt0, \
             nc.psum_tensor(f"l0att2{tag}{mi}", [128, 128], BF16) as ptt1:
            pns = [pn0, pn1]
            ptts = [ptt0, ptt1]
            sqd = {}
            mmn = {}
            evn = {}
            for blk in range(NB):
                for c in range(DC):
                    k = blk * DC + c
                    deps = [mmn[k - 2]] if k >= 2 else []

                    def sq(e, blk=blk, c=c, k=k, dstT=dstT):
                        return e.activation(out=sqb[:, (k % 2) * 512:(k % 2) * 512 + 512],
                                            in_=dstT[:, c * T + blk * 512: c * T + blk * 512 + 512],
                                            func=AF.Square)
                    sqd[k] = es.ev("scalar", deps, sq)

                    def mmq(e, blk=blk, c=c, k=k):
                        return e.matmul(pns[blk % 2][0:1, :], ones_bf[:, 0:1],
                                        sqb[:, (k % 2) * 512:(k % 2) * 512 + 512],
                                        start=(c == 0), stop=(c == DC - 1))
                    mmn[k] = es.ev("tensor", [sqd[k]] + ([evn[blk - 2]] if blk >= 2 and c == 0 else []), mmq)
                deps = [mmn[blk * DC + DC - 1]]
                if blk >= 1:
                    deps.append(deps[0])

                def evq(e, blk=blk):
                    return e.tensor_copy(out=nrows[:, blk * 512:(blk + 1) * 512], in_=pns[blk % 2][0:1, :])
                evn[blk] = es.ev("vector", deps, evq)
            hq = es.ev("scalar", [evn[NB - 1]], lambda e: e.activation(
                out=nrows[:], in_=nrows[:], func=AF.Sqrt))
            hm = es.ev("vector", [hq], lambda e: e.tensor_scalar_max(nrows[:], nrows[:], 1e-12))
            hln = es.ev("scalar", [hm], lambda e: e.activation(
                out=nrows[:], in_=nrows[:], func=AF.Ln))
            hex = es.ev("scalar", [hln], lambda e: e.activation(
                out=nrows[:], in_=nrows[:], func=AF.Exp, scale=-1.0))
            RSb = b.dr("l0a_rnb", [1, T], F32)
            hst = es.ev("sync", [hex], lambda e: e.dma_start(out=RSb, in_=nrows[:]),
                        sem="dE", inc=16)
            hld = es.ev("sync", [hst], lambda e: e.dma_start(
                out=nbc[0:16, 0:128], in_=RSb.rearrange("o (i p) -> i p", i=16)), sem="dF", inc=16)
            htr = es.ev("tensor", [hld], lambda e: e.transpose(
                pnb[0:128, 0:16], nbc[0:16, 0:128], identf[0:16, 0:16]))
            hrc = es.ev("vector", [htr], lambda e: e.tensor_copy(out=nbc[:, 128:128 + 16],
                                                                 in_=pnb[0:128, 0:16]))
            trd = {}
            tvd = {}
            std = {}
            for i in range(NT):
                for c in range(DC):
                    k = i * DC + c
                    deps = [hrc]
                    if k >= 2:
                        deps.append(tvd[k - 2])

                    def trh(e, i=i, c=c, k=k, dstT=dstT):
                        return e.transpose(ptts[k % 2][:],
                                           dstT[:, c * T + i * 128: c * T + (i + 1) * 128], identb[:])
                    trd[k] = es.ev("tensor", deps, trh)
                    deps = [trd[k]]
                    if i >= 2:
                        deps.append(std[i - 2])

                    def tvh(e, i=i, c=c, k=k):
                        return e.tensor_scalar_mul(
                            htok[:, (i % 2) * 1024 + c * 128:(i % 2) * 1024 + c * 128 + 128],
                            ptts[k % 2][:], nbc[:, 128 + i:128 + i + 1])
                    tvd[k] = es.ev("vector", deps, tvh)

                def sth(e, i=i, mi=mi):
                    return e.dma_start(out=PHAT[mi * T + i * 128: mi * T + (i + 1) * 128, :],
                                       in_=htok[:, (i % 2) * 1024:(i % 2) * 1024 + 1024])
                std[i] = es.ev("sync", [tvd[i * DC + DC - 1]], sth,
                               sem="dE" if i % 2 == 0 else "dF", inc=16)
            es.emit(f"l0anorm{tag}{mi}")

    es = ES(b)
    with nc.sbuf_tensor(f"l0aL{tag}", [128, T], F32) as Lrow, \
         nc.sbuf_tensor(f"l0atv{tag}", [128, 16], F32) as tv, \
         nc.sbuf_tensor(f"l0ati{tag}", [128, 16], U32) as tiu, \
         nc.sbuf_tensor(f"l0atf{tag}", [128, 4 * 16], F32) as tf, \
         nc.sbuf_tensor(f"l0adl{tag}", [128, 16], BF16) as dlt, \
         nc.sbuf_tensor(f"l0atc{tag}", [128, 2], F32) as tcc, \
         nc.psum_tensor(f"l0aq0{tag}", [128, 512], F32) as q0, \
         nc.psum_tensor(f"l0aq1{tag}", [128, 512], F32) as q1:
        qs = [q0, q1]
        prev_done = None
        for q in range(NT):
            nblk = q // 4 + 1
            evd = {}
            for tsb in range(nblk):
                boundary = (tsb == nblk - 1)

                deps = []
                if prev_done is not None and tsb < 2:
                    deps.extend(prev_done)
                if tsb >= 2:
                    deps.append(evd[tsb - 2])

                def mmq2(e, q=q, tsb=tsb, boundary=boundary):
                    r = None
                    for c in range(DC):
                        r = e.matmul(qs[tsb % 2][:],
                                     ITb[:, c * T + q * 128: c * T + (q + 1) * 128],
                                     PTb[:, c * T + tsb * 512: c * T + tsb * 512 + 512],
                                     start=(c == 0), stop=(c == DC - 1) and not boundary)
                    if boundary:
                        c0 = q * 128 - tsb * 512
                        r = e.matmul(qs[tsb % 2][:, c0:c0 + 128], tri_bf[:], identb[:],
                                     start=False, stop=(c0 + 128 >= 512))
                        if c0 + 128 < 512:
                            r = e.matmul(qs[tsb % 2][:, c0 + 128:512], neg30r_bf[:],
                                         onesr_bf[:, 0:512 - c0 - 128], start=False, stop=True)
                    return r
                hmm = es.ev("tensor", deps, mmq2)

                def evc(e, q=q, tsb=tsb):
                    return e.activation(out=Lrow[:, tsb * 512:(tsb + 1) * 512],
                                        in_=qs[tsb % 2][:], func=AF.Identity)
                evd[tsb] = es.ev("scalar", [hmm], evc)
            w = nblk * 512
            hdep = [evd[t2] for t2 in range(nblk)]

            ma = es.ev("vector", hdep, lambda e, w=w: e.max(out=tv[:, 0:8], in_=Lrow[:, 0:w]))
            ia = es.ev("vector", [ma], lambda e, w=w: e.max_index(
                out=tiu[:, 0:8], in_max=tv[:, 0:8], in_values=Lrow[:, 0:w]))
            rp = es.ev("vector", [ia], lambda e, w=w: e.match_replace(
                out=Lrow[:, 0:w], in_to_replace=tv[:, 0:8], in_values=Lrow[:, 0:w], imm_value=-1e30))
            mb = es.ev("vector", [rp], lambda e, w=w: e.max(out=tv[:, 8:16], in_=Lrow[:, 0:w]))
            h2 = es.ev("vector", [mb], lambda e, w=w: e.max_index(
                out=tiu[:, 8:16], in_max=tv[:, 8:16], in_values=Lrow[:, 0:w]))

            def kef(e):
                return e.tensor_single_scalar(out=tf[:, 16:32], in_=tv[:], scalar=-1e29, op=ALU.is_gt)
            h3 = es.ev("vector", [h2], kef)

            def idxf(e):
                return e.tensor_copy(out=tf[:, 0:16], in_=tiu[:])
            h4 = es.ev("vector", [h2], idxf)
            h5 = es.ev("vector", [h4], lambda e, q=q: e.tensor_scalar_add(tcc[:, 0:1], iota_col[:], float(128 * q)))
            h6 = es.ev("vector", [h5], lambda e: e.tensor_scalar_add(tf[:, 15:16], tcc[:, 0:1], float(T)))
            h7 = es.ev("vector", [h6], lambda e: e.tensor_scalar(
                out=tf[:, 32:48], in0=tf[:, 0:16], scalar1=tcc[:, 0:1], scalar2=-1.0 / T,
                op0=ALU.subtract, op1=ALU.mult))
            h8 = es.ev("vector", [h7], lambda e: e.tensor_scalar_max(tf[:, 32:48], tf[:, 32:48], 0.0))
            h9 = es.ev("vector", [h8, h3], lambda e: e.tensor_mul(dlt[:], tf[:, 32:48], tf[:, 16:32]))
            s1 = es.ev("sync", [h6], lambda e, q=q: e.dma_start(
                out=TKI.rearrange("(t k) -> t k", k=16)[q * 128:(q + 1) * 128, :], in_=tf[:, 0:16]),
                sem="dE", inc=16)
            s2 = es.ev("sync", [h3], lambda e, q=q: e.dma_start(
                out=KEEP.rearrange("(t k) -> t k", k=16)[q * 128:(q + 1) * 128, :], in_=tf[:, 16:32]),
                sem="dF", inc=16)
            s3 = es.ev("sync", [h9], lambda e, q=q: e.dma_start(
                out=DELTA.rearrange("(t k) -> t k", k=16)[q * 128:(q + 1) * 128, :], in_=dlt[:]),
                sem="dC", inc=16)
            prev_done = [h9, s1, s2, s3]
        es.emit(f"l0atopk{tag}")
    itctx.close()

    es = ES(b)
    with nc.sbuf_tensor(f"l0agx{tag}", [128, 256], U32) as gidxT, \
         nc.sbuf_tensor(f"l0akp{tag}", [128, 256], F32) as keepT, \
         nc.sbuf_tensor(f"l0atm{tag}", [128, 2 * 128], F32) as tmf, \
         nc.psum_tensor(f"l0agp{tag}", [128, 128], F32) as pg0:
        h3b = None
        for r in range(2):
            h1 = es.ev("sync", ([h3b] if h3b else []), lambda e, r=r: e.dma_start(
                out=tmf[:, 0:128] if r == 0 else tmf[:, 128:256],
                in_=TKI.rearrange("(r p j) -> r p j", r=2, p=128)[r]),
                sem="dA" if r == 0 else "dB", inc=16)
            h2 = es.ev("tensor", [h1], lambda e, r=r: e.transpose(
                pg0[:], tmf[:, r * 128:(r + 1) * 128], identf[:]))
            h3 = es.ev("vector", [h2], lambda e, r=r: e.tensor_copy(
                out=gidxT[:, r * 128:(r + 1) * 128], in_=pg0[:]))
            h1b = es.ev("sync", [h3], lambda e, r=r: e.dma_start(
                out=tmf[:, 0:128] if r == 0 else tmf[:, 128:256],
                in_=KEEP.rearrange("(r p j) -> r p j", r=2, p=128)[r]),
                sem="dA" if r == 0 else "dB", inc=16)
            h2b = es.ev("tensor", [h1b], lambda e, r=r: e.transpose(
                pg0[:], tmf[:, r * 128:(r + 1) * 128], identf[:]))
            h3b = es.ev("vector", [h2b], lambda e, r=r: e.tensor_copy(
                out=keepT[:, r * 128:(r + 1) * 128], in_=pg0[:]))
        es.emit(f"l0agidx{tag}")

        es = ES(b)
        with nc.sbuf_tensor(f"l0axg{tag}", [128, 2 * D], BF16) as Xg, \
             nc.sbuf_tensor(f"l0axt{tag}", [128, 2 * D], BF16) as XgT, \
             nc.sbuf_tensor(f"l0ags{tag}", [128, 8 * 128], BF16) as Gstr, \
             nc.psum_tensor(f"l0agt0{tag}", [128, 128], BF16) as pt0, \
             nc.psum_tensor(f"l0agt1{tag}", [128, 128], BF16) as pt1, \
             nc.psum_tensor(f"l0agt2{tag}", [128, 128], BF16) as pt2, \
             nc.psum_tensor(f"l0agt3{tag}", [128, 128], BF16) as pt3, \
             nc.psum_tensor(f"l0agg0{tag}", [128, 128], F32) as pG0, \
             nc.psum_tensor(f"l0agg1{tag}", [128, 128], F32) as pG1:
            pts = [pt0, pt1, pt2, pt3]
            pGs = [pG0, pG1]
            gld = {}
            zrd = {}
            trd = {}
            tvd = {}
            mmg = {}
            evg = {}
            stg = {}
            for g in range(256):
                deps = [trd[(g - 2, DC - 1)]] if g >= 2 else []

                def gl(e, g=g):
                    return e.indirect_dma_start(
                        Xg[:, (g % 2) * D:(g % 2) * D + D], None, PHAT,
                        bass.IndirectOffsetOnAxis(gidxT[:, g:g + 1], 0))
                gld[g] = es.ev("gpsimd", deps, gl, sem="dH" if g % 2 == 0 else "dI", inc=16)
                zdep = [gld[g]]
                if g < 2:
                    def zr(e, g=g):
                        return e.tensor_scalar_mul(Xg[:, (g % 2) * D:(g % 2) * D + D],
                                                   Xg[:, (g % 2) * D:(g % 2) * D + D],
                                                   keepT[:, g:g + 1])
                    zrd[g] = es.ev("vector", zdep, zr)
                    zdep = [zrd[g]]
                for c in range(DC):
                    k = g * DC + c
                    deps = list(zdep)
                    if k >= 4:
                        deps.append(tvd[k - 4])

                    def tr(e, g=g, c=c, k=k):
                        return e.transpose(pts[k % 4][:],
                                           Xg[:, (g % 2) * D + c * 128:(g % 2) * D + c * 128 + 128],
                                           identb[:])
                    trd[(g, c)] = es.ev("tensor", deps, tr)

                    def tvf(e, g=g, c=c, k=k):
                        return e.tensor_copy(out=XgT[:, (g % 2) * D + c * 128:(g % 2) * D + c * 128 + 128],
                                             in_=pts[k % 4][:])

                    def tvf_sc(e, g=g, c=c, k=k):
                        return e.copy(out=XgT[:, (g % 2) * D + c * 128:(g % 2) * D + c * 128 + 128],
                                      in_=pts[k % 4][:])
                    deps = [trd[(g, c)]]
                    if c == 0 and g >= 2:
                        deps.append(mmg[g - 2])
                    if k % 2 == 0:
                        tvd[k] = es.ev("vector", deps, tvf)
                    else:
                        tvd[k] = es.ev("scalar", deps, tvf_sc)

                def mg(e, g=g):
                    r = None
                    for c in range(DC):
                        r = e.matmul(pGs[g % 2][:],
                                     XgT[:, (g % 2) * D + c * 128:(g % 2) * D + c * 128 + 128],
                                     XgT[:, (g % 2) * D + c * 128:(g % 2) * D + c * 128 + 128],
                                     start=(c == 0), stop=(c == DC - 1))
                    return r
                deps = [tvd[g * DC + DC - 1], tvd[g * DC + DC - 2]]
                if g >= 2:
                    deps.append(evg[g - 2])
                mmg[g] = es.ev("tensor", deps, mg)

                def evc(e, g=g):
                    return e.tensor_scalar(out=Gstr[:, (g % 8) * 128:(g % 8) * 128 + 128],
                                           in0=pGs[g % 2][:], scalar1=1.0, scalar2=-1.0,
                                           op0=ALU.min, op1=ALU.max)
                deps = [mmg[g]]
                if g >= 8:
                    deps.append(stg[g // 8 - 1])
                evg[g] = es.ev("vector", deps, evc)
                if g % 8 == 7:
                    def st(e, g=g):
                        g0 = g - 7
                        return e.dma_start(
                            out=GRM[g0 * 128:(g0 + 8) * 128, :].rearrange("(gi p) j -> p gi j", p=128),
                            in_=Gstr[:].rearrange("p (gi j) -> p gi j", gi=8))
                    stg[g // 8] = es.ev("sync", [evg[g]], st,
                                        sem="dE" if (g // 8) % 2 == 0 else "dF", inc=16)
            es.emit(f"l0agram{tag}")

    es = ES(b)
    with nc.sbuf_tensor(f"l0ar{tag}", [17, 4096], BF16) as relT, \
         nc.sbuf_tensor(f"l0ah1{tag}", [HID, 4096], BF16) as h1b_, \
         nc.sbuf_tensor(f"l0ah2{tag}", [HID, 4096], BF16) as h2b_, \
         nc.sbuf_tensor(f"l0az{tag}", [4, 4096], BF16) as z3s, \
         nc.psum_tensor(f"l0am0{tag}", [64, 512], F32) as pm0, \
         nc.psum_tensor(f"l0am1{tag}", [64, 512], F32) as pm1, \
         nc.psum_tensor(f"l0az0{tag}", [4, 512], F32) as pz0, \
         nc.psum_tensor(f"l0az1{tag}", [4, 512], F32) as pz1:
        pms = [pm0, pm1]
        pzs = [pz0, pz1]
        w1b, w2b, w3b, bcol = b.sb["w1b"], b.sb["w2b"], b.sb["w3b"], b.sb["bcol"]
        prev_ch_done = None
        for ch in range(8):
            g0 = ch * 32

            def ldrel(e, ch=ch, g0=g0):
                return e.dma_start(
                    out=relT[0:16, :].rearrange("j (g i k) -> j g i k", g=32, i=8),
                    in_=GRM.rearrange("(g i j) k -> j g i k", g=256, i=8)[:, g0:g0 + 32, :, 0:16])
            hra = es.ev("sync", ([prev_ch_done] if prev_ch_done else []), ldrel, sem="dA", inc=16)

            def ldd(e, ch=ch):
                return e.dma_start(out=relT[16:17, :],
                                   in_=DELTA[ch * 4096:(ch + 1) * 4096].rearrange("(o n) -> o n", o=1))
            hrb = es.ev("sync", [], ldd, sem="dB", inc=16)
            hr = hrb
            hr_all = [hra, hrb]
            ev1 = {}
            ev2 = {}
            ev3 = {}
            for sb in range(8):
                def m1(e, sb=sb):
                    return e.matmul(pms[sb % 2][:], w1b[:], relT[:, sb * 512:(sb + 1) * 512],
                                    start=True, stop=True)
                deps = [hra, hrb] + ([ev1[sb - 2]] if sb >= 2 else [])
                hm1 = es.ev("tensor", deps, m1)

                def g1(e, sb=sb):
                    return e.activation(out=h1b_[:, sb * 512:(sb + 1) * 512], in_=pms[sb % 2][:],
                                        func=AF.Gelu, bias=bcol[0:HID, 0:1])
                ev1[sb] = es.ev("scalar", [hm1], g1)
            for sb in range(8):
                def m2(e, sb=sb):
                    return e.matmul(pms[sb % 2][:], w2b[:], h1b_[:, sb * 512:(sb + 1) * 512],
                                    start=True, stop=True)
                deps = [ev1[sb]] + ([ev2[sb - 2]] if sb >= 2 else [ev1[7]])
                hm2 = es.ev("tensor", deps, m2)

                def g2(e, sb=sb):
                    return e.activation(out=h2b_[:, sb * 512:(sb + 1) * 512], in_=pms[sb % 2][:],
                                        func=AF.Gelu, bias=bcol[0:HID, 1:2])
                ev2[sb] = es.ev("scalar", [hm2], g2)
            for sb in range(8):
                def m3(e, sb=sb):
                    return e.matmul(pzs[sb % 2][:], w3b[:], h2b_[:, sb * 512:(sb + 1) * 512],
                                    start=True, stop=True)
                deps = [ev2[sb]] + ([ev3[sb - 2]] if sb >= 2 else [ev2[7]])
                hm3 = es.ev("tensor", deps, m3)

                def g3a(e, sb=sb):
                    return e.activation(out=z3s[0:4, sb * 512:(sb + 1) * 512], in_=pzs[sb % 2][:],
                                        func=AF.Identity, bias=bcol[0:4, 2:3])
                hz = es.ev("scalar", [hm3], g3a)

                def g3b(e, sb=sb):
                    return e.activation(out=z3s[0:2, sb * 512:(sb + 1) * 512],
                                        in_=z3s[0:2, sb * 512:(sb + 1) * 512], func=AF.Tanh)
                ev3[sb] = es.ev("scalar", [hz], g3b)

            def stz(e, ch=ch):
                return e.dma_start(out=Z3[:, ch * 4096:(ch + 1) * 4096], in_=z3s[:])
            prev_ch_done = es.ev("sync", [ev3[7]], stz, sem="dE", inc=16)
        es.emit(f"l0amlp{tag}")

    es = ES(b)
    wop = b.sb["wop"]
    hat_tile = b.sb["hat_tile"]
    with nc.sbuf_tensor(f"l0azt{tag}", [128, 2 * 80], BF16) as zt, \
         nc.sbuf_tensor(f"l0akp2{tag}", [128, 2 * 16], F32) as kp2, \
         nc.sbuf_tensor(f"l0aw{tag}", [128, 2 * 48], F32) as wsc, \
         nc.sbuf_tensor(f"l0ahat{tag}", [128, 2 * 3 * 256], BF16) as hats, \
         nc.sbuf_tensor(f"l0aA{tag}", [128, 2 * 4 * 256], F32) as Aac, \
         nc.sbuf_tensor(f"l0aAb{tag}", [128, 2 * 256], BF16) as Abf, \
         nc.psum_tensor(f"l0ay0{tag}", [128, 128], F32) as py0, \
         nc.psum_tensor(f"l0ay1{tag}", [128, 128], F32) as py1, \
         nc.psum_tensor(f"l0ay2{tag}", [128, 128], F32) as py2, \
         nc.psum_tensor(f"l0ay3{tag}", [128, 128], F32) as py3, \
         nc.psum_tensor(f"l0aat{tag}", [128, 128], BF16) as pat0, \
         nc.psum_tensor(f"l0aat2{tag}", [128, 128], BF16) as pat1:
        pys4 = [[py0, py1], [py2, py3]]
        pats = [pat0, pat1]
        hatrow = 2 * 3 * 256
        Arow = 2 * 4 * 256

        def bc_inner(buf, off, rowlen):
            return bass.AP(buf, off, [[rowlen, 128], [1, 16], [0, 16]])

        def bc_outer(buf, off, rowlen):
            return bass.AP(buf, off, [[rowlen, 128], [0, 16], [1, 16]])

        def v3(buf, off, rowlen):
            return bass.AP(buf, off, [[rowlen, 128], [16, 16], [1, 16]])
        prevAq = {}
        for q in range(NT):
            p = q % 2
            zo, wo, ho, ao, bo = p * 80, p * 48, p * 768, p * 1024, p * 256
            dep0 = [prevAq[q - 2]] if q >= 2 else []

            def ldz(e, q=q, zo=zo):
                return e.dma_start(
                    out=zt[:, zo:zo + 48].rearrange("p (r k) -> p r k", r=3),
                    in_=Z3[0:3, :].rearrange("r (t k) -> t r k", k=16)[q * 128:(q + 1) * 128, :, :])
            hz = es.ev("sync", dep0, ldz, sem="dA" if p == 0 else "dB", inc=16)
            hk = es.ev("sync", dep0, lambda e, q=q, p=p: e.dma_start(
                out=kp2[:, p * 16:p * 16 + 16],
                in_=KEEP.rearrange("(t k) -> t k", k=16)[q * 128:(q + 1) * 128, :]),
                sem="dC" if p == 0 else "dD", inc=16)
            kps = kp2[:, p * 16:p * 16 + 16]
            h1 = es.ev("vector", [hz, hk], lambda e, wo=wo, zo=zo, kps=kps: e.tensor_mul(
                wsc[:, wo + 32:wo + 48], zt[:, zo + 32:zo + 48], kps))
            h2 = es.ev("vector", [h1], lambda e, wo=wo, kps=kps: e.tensor_scalar(
                out=wsc[:, wo + 16:wo + 32], in0=kps, scalar1=1e30, scalar2=-1e30,
                op0=ALU.mult, op1=ALU.add))
            h3 = es.ev("vector", [h2], lambda e, wo=wo: e.tensor_add(
                wsc[:, wo + 32:wo + 48], wsc[:, wo + 32:wo + 48], wsc[:, wo + 16:wo + 32]))
            h4 = es.ev("vector", [h3], lambda e, wo=wo: e.memset(wsc[:, wo + 47:wo + 48], -1e30))
            h5 = es.ev("vector", [h4], lambda e, wo=wo: e.reduce_max(
                out=wsc[:, wo + 1:wo + 2], in_=wsc[:, wo + 32:wo + 48], axis=AXX))
            h6 = es.ev("vector", [h5], lambda e, wo=wo: e.tensor_scalar_mul(
                wsc[:, wo + 2:wo + 3], wsc[:, wo + 1:wo + 2], -1.0))
            h7 = es.ev("scalar", [h6], lambda e, wo=wo: e.activation(
                out=wsc[:, wo + 16:wo + 32], in_=wsc[:, wo + 32:wo + 48],
                func=AF.Exp, bias=wsc[:, wo + 2:wo + 3]))
            h8 = es.ev("vector", [h7], lambda e, wo=wo: e.reduce_sum(
                out=wsc[:, wo + 3:wo + 4], in_=wsc[:, wo + 16:wo + 32], axis=AXX))
            h9 = es.ev("vector", [h8], lambda e, wo=wo: e.reciprocal(
                out=wsc[:, wo + 4:wo + 5], in_=wsc[:, wo + 3:wo + 4]))
            hw = es.ev("vector", [h9], lambda e, wo=wo: e.tensor_scalar_mul(
                wsc[:, wo + 16:wo + 32], wsc[:, wo + 16:wo + 32], wsc[:, wo + 4:wo + 5]))
            hf = {}
            for ax in range(2):
                a1 = es.ev("scalar", [hz] + ([hf[0]] if ax == 1 else []),
                           lambda e, ax=ax, zo=zo: e.activation(
                               out=zt[:, zo + 64:zo + 80], in_=zt[:, zo + ax * 16:zo + (ax + 1) * 16],
                               func=AF.Identity, scale=7.5, bias=konst(b, 7.5)))
                a2 = es.ev("vector", [a1], lambda e, zo=zo: e.tensor_scalar(
                    out=zt[:, zo + 64:zo + 80], in0=zt[:, zo + 64:zo + 80], scalar1=15.0, scalar2=0.0,
                    op0=ALU.min, op1=ALU.max))
                a3 = es.ev("vector", [a2], lambda e, ax=ax, zo=zo, ho=ho: e.tensor_tensor(
                    out=v3(hats, ho + ax * 256, hatrow), in0=bc_inner(zt, zo + 64, 2 * 80),
                    in1=v3(hat_tile, 0, 256), op=ALU.subtract))
                a4 = es.ev("scalar", [a3], lambda e, ax=ax, ho=ho: e.activation(
                    out=hats[:, ho + ax * 256:ho + (ax + 1) * 256],
                    in_=hats[:, ho + ax * 256:ho + (ax + 1) * 256], func=AF.Abs))
                a5 = es.ev("scalar", [a4], lambda e, ax=ax, ho=ho: e.activation(
                    out=hats[:, ho + ax * 256:ho + (ax + 1) * 256],
                    in_=hats[:, ho + ax * 256:ho + (ax + 1) * 256],
                    func=AF.Relu, scale=-1.0, bias=konst(b, 1.0)))
                hf[ax] = a5
            hwy = es.ev("vector", [hf[1], hw], lambda e, ho=ho, wo=wo: e.tensor_tensor(
                out=v3(hats, ho + 512, hatrow), in0=v3(hats, ho + 256, hatrow),
                in1=bc_inner(wsc, wo + 16, 2 * 48), op=ALU.mult))
            depA = dep0

            def mulk(e, k2, dst, ho=ho):
                return e.tensor_tensor(
                    out=bass.AP(Aac, dst, [[Arow, 128], [16, 16], [1, 16]]),
                    in0=bc_inner(hats, ho + 512 + k2 * 16, hatrow),
                    in1=bc_outer(hats, ho + 0 + k2 * 16, hatrow), op=ALU.mult)
            hE = es.ev("vector", [hwy, hf[0]] + depA, lambda e, mulk=mulk, ao=ao: mulk(e, 0, ao + 0))
            hO = es.ev("gpsimd", [hwy, hf[0]] + depA, lambda e, mulk=mulk, ao=ao: mulk(e, 1, ao + 256))
            for k2 in range(2, 15):
                if k2 % 2 == 0:
                    hs = es.ev("vector", [hE], lambda e, k2=k2, mulk=mulk, ao=ao: mulk(e, k2, ao + 512))
                    hE = es.ev("vector", [hs], lambda e, ao=ao: e.tensor_add(
                        Aac[:, ao:ao + 256], Aac[:, ao:ao + 256], Aac[:, ao + 512:ao + 768]))
                else:
                    hs = es.ev("gpsimd", [hO], lambda e, k2=k2, mulk=mulk, ao=ao: mulk(e, k2, ao + 768))
                    hO = es.ev("gpsimd", [hs], lambda e, ao=ao: e.tensor_add(
                        Aac[:, ao + 256:ao + 512], Aac[:, ao + 256:ao + 512],
                        Aac[:, ao + 768:ao + 1024]))
            hA = es.ev("vector", [hE, hO], lambda e, ao=ao: e.tensor_add(
                Aac[:, ao:ao + 256], Aac[:, ao:ao + 256], Aac[:, ao + 256:ao + 512]))
            hcst = es.ev("vector", [hA], lambda e, bo=bo, ao=ao: e.tensor_copy(
                out=Abf[:, bo:bo + 256], in_=Aac[:, ao:ao + 256]))
            trA = {}
            for cc in range(2):
                t1 = es.ev("tensor", [hcst] if cc == 0 else [trA[0]],
                           lambda e, cc=cc, bo=bo, p=p: e.transpose(
                               pats[p][:], Abf[:, bo + cc * 128:bo + (cc + 1) * 128], identb[:]))
                trA[cc] = es.ev("vector", [t1], lambda e, cc=cc, ho=ho, p=p: e.tensor_copy(
                    out=hats[:, ho + 512 + cc * 128:ho + 512 + cc * 128 + 128], in_=pats[p][:]))
            prevy = {}
            for c in range(DC):
                def ym2(e, c=c, ho=ho, p=p):
                    r = None
                    for cc in range(2):
                        r = e.matmul(pys4[p][c % 2][:],
                                     wop[:, cc * 1024 + c * 128: cc * 1024 + c * 128 + 128],
                                     hats[:, ho + 512 + cc * 128:ho + 512 + cc * 128 + 128],
                                     start=(cc == 0), stop=(cc == 1))
                    return r
                deps = [trA[1]]
                if c >= 2:
                    deps.append(prevy[c - 2])
                elif q >= 2:
                    deps.append(prevAq[q - 2])
                hy = es.ev("tensor", deps, ym2)

                def ya(e, c=c, q=q, p=p):
                    sl = slice(c * T + q * 128, c * T + (q + 1) * 128)
                    return e.tensor_add(xT[:, sl], xT[:, sl], pys4[p][c % 2][:])
                prevy[c] = es.ev("vector", [hy], ya)
            prevAq[q] = prevy[DC - 1]
        es.emit(f"l0apal{tag}")


def build_stage1():
    b = Bld()
    emit_setup(b)
    emit_weight_prep(b)
    for seq in range(SPB):
        emit_load_x(b, seq, tag=f"s{seq}")
        emit_ln(b, b.sb["xT"], b.sb["xnT"], True, True, tag=f"s{seq}")
        emit_l1ffn(b, tag=f"s{seq}")
        emit_store_x(b, seq, tag=f"s{seq}")
    return b


def build_stage3():
    b = Bld()
    emit_setup(b)
    emit_weight_prep(b)
    emit_seprep(b)
    for seq in range(SPB):
        emit_load_x(b, seq, tag=f"s{seq}")
        emit_ln(b, b.sb["xT"], b.sb["xnT"], True, True, tag=f"s{seq}")
        emit_l0ffn(b, tag=f"s{seq}")
        emit_store_x(b, seq, tag=f"s{seq}")
    return b


def build_stage4():
    b = Bld()
    emit_setup(b)
    emit_weight_prep(b)
    emit_l0attn_prologue(b)
    for seq in range(SPB):
        emit_load_x(b, seq, tag=f"s{seq}")
        emit_ln(b, b.sb["xT"], b.sb["xnT"], True, False, tag=f"s{seq}")
        emit_l0attn(b, tag=f"s{seq}")
        emit_store_x(b, seq, tag=f"s{seq}")
    return b


def build_stage2():
    b = Bld()
    emit_setup(b)
    emit_weight_prep(b)
    for seq in range(SPB):
        emit_load_x(b, seq, tag=f"s{seq}")
        emit_ln(b, b.sb["xT"], b.sb["xnT"], True, False, tag=f"s{seq}")
        emit_l1attn(b, tag=f"s{seq}")
        emit_store_x(b, seq, tag=f"s{seq}")
    return b


def build_full():
    b = Bld()
    emit_setup(b)
    emit_weight_prep(b)
    emit_seprep(b)
    emit_l0attn_prologue(b)
    for seq in range(SPB):
        emit_load_x(b, seq, tag=f"s{seq}")
        emit_ln(b, b.sb["xT"], b.sb["xnT"], True, True, tag=f"al{seq}")
        emit_l0attn(b, tag=f"s{seq}")
        emit_l0ffn(b, tag=f"s{seq}")
        emit_ln(b, b.sb["xT"], b.sb["xnT"], True, False, tag=f"bl{seq}")
        emit_l1attn(b, tag=f"s{seq}")
        emit_l1ffn(b, tag=f"s{seq}")
        emit_store_x(b, seq, tag=f"s{seq}")
    return b


_CACHED = {}


def kernel(**inputs):
    b = build_full()
    in_maps = [host_inputs_for_core(inputs, c) for c in range(8)]
    res = run_bass_kernel_spmd(b.nc, in_maps, core_ids=list(range(8)))
    out = np.concatenate([r["y_tok"].reshape(SPB, T, D) for r in res.results], axis=0)
    return out


# version 05
# speedup vs baseline: 1.2311x; 1.1540x over previous
import numpy as np
import ml_dtypes
from contextlib import ExitStack
import concourse.bass as bass
import concourse.mybir as mybir
from concourse.bass_utils import run_bass_kernel_spmd

F32 = mybir.dt.float32
BF16 = mybir.dt.bfloat16
U32 = mybir.dt.uint32
AF = mybir.ActivationFunctionType
ALU = mybir.AluOpType
AXX = mybir.AxisListType.X
BF = ml_dtypes.bfloat16

B, T, D = 16, 2048, 1024
SPB = 2
DC = D // 128
NT = T // 128
NB = T // 512
KMAX = 15
HID = 64
NSLOT = 4096
HIDF = 4096
NHEAD, HD = 16, 64
LN_EPS = 1e-5


class Sem:
    __slots__ = ("h", "v")

    def __init__(self, h):
        self.h = h
        self.v = 0


class Bld:
    def __init__(self):
        self.es = ExitStack()
        nc = bass.Bass("TRN2", target_bir_lowering=False)
        self.nc = nc
        self.s = {}
        for name in ["dw", "dg", "ds", "pe", "ve", "ac", "po", "mA", "mB", "mC",
                     "dA", "dB", "dC", "dD", "dE", "dF", "dH", "dI"]:
            self.s[name] = Sem(self.es.enter_context(nc.semaphore(name)))
        self.inp = {}
        self.out = {}
        self.dram = {}
        self.sb = {}
        self.vecidx = {}
        self.kvals = {}

    def ein(self, name):
        if name not in self.inp:
            shape, dtype = INPUT_SPECS[name]
            self.inp[name] = self.nc.dram_tensor(name, list(shape), dtype, kind="ExternalInput").ap()
        return self.inp[name]

    def eout(self, name, shape, dtype):
        if name not in self.out:
            self.out[name] = self.nc.dram_tensor(name, list(shape), dtype, kind="ExternalOutput").ap()
        return self.out[name]

    def dr(self, name, shape, dtype):
        if name not in self.dram:
            self.dram[name] = self.nc.dram_tensor(name, list(shape), dtype, kind="Internal")
        return self.dram[name].ap()

    def sbt(self, name, shape=None, dtype=None):
        if name not in self.sb:
            self.sb[name] = self.es.enter_context(self.nc.sbuf_tensor(name, list(shape), dtype))
        return self.sb[name]


def host_constants():
    c = {}
    c["c_ident_bf"] = np.eye(128, dtype=BF)
    c["c_ident_f32"] = np.eye(128, dtype=np.float32)
    half = D // 2
    inv_freq = 1.0 / 10000.0 ** (np.arange(half, dtype=np.float64) / half)
    ang = inv_freq[:, None] * np.arange(T, dtype=np.float64)[None, :]
    c["c_ropecos"] = np.cos(ang).astype(BF)
    c["c_ropesin"] = np.sin(ang).astype(BF)
    c["c_iota_col"] = np.arange(128, dtype=np.float32)[:, None]
    tri = np.where(np.arange(128)[:, None] <= np.arange(128)[None, :], 0.0, -1e30).astype(np.float32)
    c["c_tri_bf"] = np.ascontiguousarray(tri).astype(BF)
    c["c_hat_tile"] = np.broadcast_to(np.tile(np.arange(16, dtype=np.float32), 16)[None, :],
                                      (128, 256)).copy()
    c["c_triTT"] = np.ascontiguousarray(tri.T).astype(BF)
    c["c_neg30r_bf"] = np.full((1, 128), -1e30, dtype=BF)
    c["c_ones_bf"] = np.ones((128, 1), dtype=BF)
    c["c_ones_f32"] = np.ones((128, 1), dtype=np.float32)
    c["c_onesr_bf"] = np.ones((1, 512), dtype=BF)
    c["c_onesr_f32"] = np.ones((1, 512), dtype=np.float32)
    return c


VEC_NAMES = ["gamma_x_T", "gamma_slot_T", "bproj_T", "lngam_T", "lnbet_T", "mur_T", "muk_T"]

INPUT_SPECS = {
    "x_tok": ([SPB * T, D], F32),
    "Wi": ([D, D], F32), "Wp": ([D, D], F32),
    "pal": ([D, 256], F32),
    "Wo": ([D, D], F32),
    "W1": ([17, HID], F32), "b1": ([HID, 1], F32),
    "W2": ([HID, HID], F32), "b2": ([HID, 1], F32),
    "W3": ([HID, 4], F32), "b3": ([4, 1], F32),
    "se_flat": ([NSLOT, D], F32),
    "w1f": ([D, HIDF], F32), "w2f": ([HIDF, D], F32),
    "wqkv": ([D, 3 * D], F32), "wproj": ([D, D], F32),
    "WrT": ([D, D], F32), "WkT": ([D, D], F32), "WvT": ([D, D], F32),
    "c_ident_bf": ([128, 128], BF16),
    "c_ident_f32": ([128, 128], F32),
    "c_ropecos": ([512, T], BF16),
    "c_ropesin": ([512, T], BF16),
    "c_iota_col": ([128, 1], F32),
    "c_tri_bf": ([128, 128], BF16),
    "c_hat_tile": ([128, 256], F32),
    "c_triTT": ([128, 128], BF16),
    "c_neg30r_bf": ([1, 128], BF16),
    "c_ones_bf": ([128, 1], BF16),
    "c_ones_f32": ([128, 1], F32),
    "c_onesr_bf": ([1, 512], BF16),
    "c_onesr_f32": ([1, 512], F32),
}
for _nm in VEC_NAMES:
    INPUT_SPECS[_nm] = ([128, DC], F32)


def host_inputs_for_core(inputs, core):
    f32 = np.float32
    m = {}
    x = np.asarray(inputs["L0_attn_x"], f32)
    m["x_tok"] = np.ascontiguousarray(x[core * SPB:(core + 1) * SPB].reshape(SPB * T, D))
    m["Wi"] = np.ascontiguousarray(np.asarray(inputs["L0_attn_Wi"], f32))
    m["Wp"] = np.ascontiguousarray(np.asarray(inputs["L0_attn_Wp"], f32))
    m["pal"] = np.ascontiguousarray(np.asarray(inputs["L0_attn_palette"], f32).reshape(D, 256))
    m["Wo"] = np.ascontiguousarray(np.asarray(inputs["L0_attn_Wo"], f32)[0])
    m["W1"] = np.ascontiguousarray(np.asarray(inputs["L0_attn_W1"], f32))
    m["b1"] = np.ascontiguousarray(np.asarray(inputs["L0_attn_b1"], f32)[:, None])
    m["W2"] = np.ascontiguousarray(np.asarray(inputs["L0_attn_W2"], f32))
    m["b2"] = np.ascontiguousarray(np.asarray(inputs["L0_attn_b2"], f32)[:, None])
    wm = np.asarray(inputs["L0_attn_Wm"], f32)
    m["W3"] = np.ascontiguousarray(np.concatenate(
        [np.asarray(inputs["L0_attn_Wc"], f32), wm, wm], axis=1))
    bm = np.asarray(inputs["L0_attn_bm"], f32)
    m["b3"] = np.ascontiguousarray(np.concatenate(
        [np.asarray(inputs["L0_attn_bc"], f32), bm, bm])[:, None])
    m["se_flat"] = np.ascontiguousarray(np.asarray(inputs["L0_ffn_slot_embeds"], f32).reshape(NSLOT, D))
    m["w1f"] = np.ascontiguousarray(np.asarray(inputs["L0_ffn_w1"], f32))
    m["w2f"] = np.ascontiguousarray(np.asarray(inputs["L0_ffn_w2"], f32))
    m["wqkv"] = np.ascontiguousarray(np.asarray(inputs["L1_attn_w_qkv"], f32))
    m["wproj"] = np.ascontiguousarray(np.asarray(inputs["L1_attn_w_proj"], f32))
    m["WrT"] = np.ascontiguousarray(np.asarray(inputs["L1_ffn_W_r"], f32).T)
    m["WkT"] = np.ascontiguousarray(np.asarray(inputs["L1_ffn_W_k"], f32).T)
    m["WvT"] = np.ascontiguousarray(np.asarray(inputs["L1_ffn_W_v"], f32).T)

    def vt(v):
        return np.ascontiguousarray(np.asarray(v, f32).reshape(DC, 128).T)
    m["gamma_x_T"] = vt(inputs["L0_ffn_gamma_x"])
    m["gamma_slot_T"] = vt(inputs["L0_ffn_gamma_slot"])
    m["bproj_T"] = vt(inputs["L1_attn_b_proj"])
    m["lngam_T"] = vt(inputs["L1_attn_ln_gamma"])
    m["lnbet_T"] = vt(inputs["L1_attn_ln_beta"])
    m["mur_T"] = vt(inputs["L1_ffn_mu_r"])
    m["muk_T"] = vt(inputs["L1_ffn_mu_k"])
    m.update(host_constants())
    return m


KONST_VALS = [LN_EPS, 1.0, 0.0, -1e30, 7.5]


def emit_setup(b: Bld):
    nc = b.nc
    dw = b.s["dw"]
    vt = b.sbt("vecs", [128, DC * len(VEC_NAMES)], F32)
    b.vecidx = {nm: i for i, nm in enumerate(VEC_NAMES)}
    kp = b.sbt("kpool", [128, len(KONST_VALS)], F32)
    b.kvals = {v: i for i, v in enumerate(KONST_VALS)}
    consts = [("c_ident_bf", "identb"), ("c_ident_f32", "identf"),
              ("c_iota_col", "iota_col"),
              ("c_triTT", "triTT"), ("c_neg30r_bf", "neg30r_bf"),
              ("c_tri_bf", "tri_bf"), ("c_hat_tile", "hat_tile"),
              ("c_ones_bf", "ones_bf"), ("c_ones_f32", "ones_f32"),
              ("c_onesr_bf", "onesr_bf"), ("c_onesr_f32", "onesr_f32")]
    dw0 = dw.v
    n = 0
    with nc.Block("setup") as blk:
        @blk.vector
        def _(e):
            for v, i in b.kvals.items():
                e.memset(kp[:, i:i + 1], float(v))

        @blk.sync
        def _(e):
            nonlocal n
            for i, nm in enumerate(VEC_NAMES):
                e.dma_start(out=vt[:, i * DC:(i + 1) * DC], in_=b.ein(nm)).then_inc(dw.h, 16)
                n += 1
            for nm, sb in consts:
                shape, dtype = INPUT_SPECS[nm]
                t = b.sbt(sb, shape, dtype)
                e.dma_start(out=t[:], in_=b.ein(nm)).then_inc(dw.h, 16)
                n += 1
            e.wait_ge(dw.h, dw0 + 16 * n)
    dw.v = dw0 + 16 * n
    b.sbt("xT", [128, DC * T], F32)
    b.sbt("xnT", [128, DC * T], BF16)
    b.sbt("f_row", [1, T], F32)
    b.sbt("l0_mc", [128, 64], F32)
    b.sbt("l0_cmx", [32, 128], F32)


def vcol(b: Bld, name, c):
    i = b.vecidx[name]
    return b.sb["vecs"][:, i * DC + c: i * DC + c + 1]


def konst(b: Bld, val):
    i = b.kvals[val]
    return b.sb["kpool"][:, i:i + 1]


def emit_weight_prep(b: Bld):
    nc = b.nc
    dg = b.s["dg"]
    casts = [("Wi", [D, D]), ("Wp", [D, D]), ("w1f", [D, HIDF]), ("w2f", [HIDF, D]),
             ("wqkv", [D, 3 * D]), ("wproj", [D, D]),
             ("WrT", [D, D]), ("WkT", [D, D]), ("WvT", [D, D]),
             ("pal", [D, 256]), ("Wo", [D, D])]
    dg0 = dg.v
    with nc.Block("wprep") as blk:
        @blk.gpsimd
        def _(g):
            for nm, shp in casts:
                d = b.dr(nm + "_bf", shp, BF16)
                g.dma_start(out=d, in_=b.ein(nm)).then_inc(dg.h, 16)
            g.wait_ge(dg.h, dg0 + 16 * len(casts))
    dg.v = dg0 + 16 * len(casts)


def emit_load_x(b: Bld, seq, tag=""):
    nc = b.nc
    pe, ve = b.s["pe"], b.s["ve"]
    dsl = [b.s["dA"], b.s["dB"]]
    d0 = [s.v for s in dsl]
    xT, identf = b.sb["xT"], b.sb["identf"]
    xtok = b.ein("x_tok")
    pe0, ve0 = pe.v, ve.v
    with nc.sbuf_tensor(f"lxt{tag}", [128, 2 * D], F32) as tile, \
         nc.psum_tensor(f"lxp0{tag}", [128, 128], F32) as p0, \
         nc.psum_tensor(f"lxp1{tag}", [128, 128], F32) as p1, \
         nc.psum_tensor(f"lxp2{tag}", [128, 128], F32) as p2, \
         nc.psum_tensor(f"lxp3{tag}", [128, 128], F32) as p3, \
         nc.Block(f"loadx{tag}") as blk:
        pss = [p0, p1, p2, p3]

        @blk.sync
        def _(e):
            for i in range(NT):
                if i >= 2:
                    e.wait_ge(pe.h, pe0 + DC * (i - 1))
                e.dma_start(out=tile[:, (i % 2) * D:(i % 2) * D + D],
                            in_=xtok[seq * T + i * 128: seq * T + (i + 1) * 128, :]
                            ).then_inc(dsl[i % 2].h, 16)

        @blk.tensor
        def _(e):
            for i in range(NT):
                e.wait_ge(dsl[i % 2].h, d0[i % 2] + 16 * (i // 2 + 1))
                buf = tile[:, (i % 2) * D:(i % 2) * D + D]
                for c in range(DC):
                    k = i * DC + c
                    if k >= 4:
                        e.wait_ge(ve.h, ve0 + k - 3)
                    e.transpose(pss[k % 4][:], buf[:, c * 128:(c + 1) * 128], identf[:]).then_inc(pe.h, 1)

        @blk.vector
        def _(e):
            for i in range(NT):
                for c in range(DC):
                    k = i * DC + c
                    e.wait_ge(pe.h, pe0 + k + 1)
                    e.tensor_copy(out=xT[:, c * T + i * 128: c * T + (i + 1) * 128],
                                  in_=pss[k % 4][:]).then_inc(ve.h, 1)
    for i, s in enumerate(dsl):
        s.v = d0[i] + 16 * (NT // 2)
    pe.v = pe0 + NT * DC
    ve.v = ve0 + NT * DC


def emit_store_x(b: Bld, seq, tag=""):
    nc = b.nc
    pe, ve = b.s["pe"], b.s["ve"]
    dsl = [b.s["dE"], b.s["dF"]]
    d0 = [s.v for s in dsl]
    xT, identf = b.sb["xT"], b.sb["identf"]
    ytok = b.eout("y_tok", [SPB * T, D], F32)
    pe0, ve0 = pe.v, ve.v
    with nc.sbuf_tensor(f"sxt{tag}", [128, 2 * D], F32) as tile, \
         nc.psum_tensor(f"sxp0{tag}", [128, 128], F32) as p0, \
         nc.psum_tensor(f"sxp1{tag}", [128, 128], F32) as p1, \
         nc.psum_tensor(f"sxp2{tag}", [128, 128], F32) as p2, \
         nc.psum_tensor(f"sxp3{tag}", [128, 128], F32) as p3, \
         nc.Block(f"storex{tag}") as blk:
        pss = [p0, p1, p2, p3]

        @blk.tensor
        def _(e):
            for i in range(NT):
                if i >= 2:
                    e.wait_ge(dsl[i % 2].h, d0[i % 2] + 16 * (i // 2))
                for c in range(DC):
                    k = i * DC + c
                    if k >= 4:
                        e.wait_ge(ve.h, ve0 + k - 3)
                    e.transpose(pss[k % 4][:], xT[:, c * T + i * 128: c * T + (i + 1) * 128],
                                identf[:]).then_inc(pe.h, 1)

        @blk.vector
        def _(e):
            for i in range(NT):
                buf = tile[:, (i % 2) * D:(i % 2) * D + D]
                for c in range(DC):
                    k = i * DC + c
                    e.wait_ge(pe.h, pe0 + k + 1)
                    e.tensor_copy(out=buf[:, c * 128:(c + 1) * 128], in_=pss[k % 4][:]).then_inc(ve.h, 1)

        @blk.sync
        def _(e):
            for i in range(NT):
                e.wait_ge(ve.h, ve0 + DC * (i + 1))
                e.dma_start(out=ytok[seq * T + i * 128: seq * T + (i + 1) * 128, :],
                            in_=tile[:, (i % 2) * D:(i % 2) * D + D]).then_inc(dsl[i % 2].h, 16)
            e.wait_ge(dsl[0].h, d0[0] + 16 * (NT // 2))
            e.wait_ge(dsl[1].h, d0[1] + 16 * (NT // 2))
    for i, s in enumerate(dsl):
        s.v = d0[i] + 16 * (NT // 2)
    pe.v = pe0 + NT * DC
    ve.v = ve0 + NT * DC


def emit_ln(b: Bld, src, dst, src_is_f32, with_f, gamma=None, beta=None,
            add_into=None, tag=""):
    nc = b.nc
    pe, ve, ac, po, mA = b.s["pe"], b.s["ve"], b.s["ac"], b.s["po"], b.s["mA"]
    ones_bf, onesr_f32 = b.sb["ones_bf"], b.sb["onesr_f32"]
    f_r = b.sb["f_row"][:]
    NK = NB * DC

    def xsl(c, blki):
        return src[:, c * T + blki * 512: c * T + blki * 512 + 512]

    with nc.sbuf_tensor(f"lnrw{tag}", [1, 4 * T], F32) as rows, \
         nc.sbuf_tensor(f"lnhi{tag}", [128, 4 * 512], BF16) as hi, \
         nc.sbuf_tensor(f"lnsq{tag}", [128, 4 * 512], BF16) as sq, \
         nc.psum_tensor(f"lnpA{tag}", [128, 512], F32) as pA, \
         nc.psum_tensor(f"lnpB{tag}", [128, 512], F32) as pB, \
         nc.psum_tensor(f"lnpC{tag}", [128, 512], F32) as pC, \
         nc.psum_tensor(f"lnpD{tag}", [128, 512], F32) as pD:
        ps_s = [pA, pB]
        ps_q = [pC, pD]
        mu_r, rs_r = rows[:, 0:T], rows[:, T:2 * T]
        sA_r, sB_r = rows[:, 2 * T:3 * T], rows[:, 3 * T:4 * T]
        s1_r, s2_r = sA_r, sB_r
        pe0, ve0, ac0, mA0 = pe.v, ve.v, ac.v, mA.v

        def hslot(k):
            return hi[:, (k % 4) * 512:(k % 4) * 512 + 512]

        def qslot(k):
            return sq[:, (k % 4) * 512:(k % 4) * 512 + 512]

        with nc.Block(f"lnstat{tag}") as blk:
            @blk.vector
            def _(e):
                for blki in range(NB):
                    for c in range(DC):
                        k = blki * DC + c
                        if k >= 4:
                            e.wait_ge(pe.h, pe0 + k - 3)
                        e.tensor_copy(out=hslot(k), in_=xsl(c, blki)).then_inc(ve.h, 1)
                    if blki >= 1:
                        e.wait_ge(pe.h, pe0 + DC * blki)
                        e.tensor_copy(out=s1_r[:, (blki - 1) * 512:blki * 512],
                                      in_=ps_s[(blki - 1) % 2][0:1, :])
                        e.tensor_copy(out=s2_r[:, (blki - 1) * 512:blki * 512],
                                      in_=ps_q[(blki - 1) % 2][0:1, :]).then_inc(mA.h, 1)
                e.wait_ge(pe.h, pe0 + DC * NB)
                e.tensor_copy(out=s1_r[:, (NB - 1) * 512:NB * 512], in_=ps_s[(NB - 1) % 2][0:1, :])
                e.tensor_copy(out=s2_r[:, (NB - 1) * 512:NB * 512],
                              in_=ps_q[(NB - 1) % 2][0:1, :]).then_inc(mA.h, 1)

            @blk.scalar
            def _(e):
                for k in range(NK):
                    e.wait_ge(ve.h, ve0 + k + 1)
                    if k >= 4:
                        e.wait_ge(pe.h, pe0 + k - 3)
                    e.activation(out=qslot(k), in_=hslot(k), func=AF.Square).then_inc(ac.h, 1)

            @blk.tensor
            def _(e):
                for blki in range(NB):
                    if blki >= 2:
                        e.wait_ge(mA.h, mA0 + blki - 1)
                    for c in range(DC):
                        k = blki * DC + c
                        e.wait_ge(ac.h, ac0 + k + 1)
                        e.matmul(ps_s[blki % 2][0:1, :], ones_bf[:, 0:1], hslot(k),
                                 start=(c == 0), stop=(c == DC - 1))
                        e.matmul(ps_q[blki % 2][0:1, :], ones_bf[:, 0:1], qslot(k),
                                 start=(c == 0), stop=(c == DC - 1)).then_inc(pe.h, 1)
        pe.v = pe0 + NK
        ve.v = ve0 + NK
        ac.v = ac0 + NK
        mA.v = mA0 + NB

        ve0, ac0 = ve.v, ac.v
        with nc.Block(f"lnrow{tag}") as blk:
            @blk.vector
            def _(e):
                v = ve0

                def step(instr):
                    nonlocal v
                    v += 1
                    instr.then_inc(ve.h, 1)
                    e.wait_ge(ve.h, v)
                step(e.tensor_scalar_mul(mu_r, sA_r, 1.0 / D))
                step(e.tensor_scalar_mul(sA_r, sB_r, 1.0 / D))
                step(e.tensor_mul(sB_r, mu_r, mu_r))
                step(e.tensor_sub(sA_r, sA_r, sB_r))
                step(e.tensor_scalar_max(sA_r, sA_r, 1e-30))
                if with_f:
                    e.wait_ge(ac.h, ac0 + 2)
                    step(e.tensor_sub(sA_r, sB_r, f_r))

            @blk.scalar
            def _(e):
                a = ac0

                def step(instr):
                    nonlocal a
                    a += 1
                    instr.then_inc(ac.h, 1)
                    e.wait_ge(ac.h, a)
                e.wait_ge(ve.h, ve0 + 5)
                step(e.activation(out=sB_r, in_=sA_r, func=AF.Ln, bias=konst(b, LN_EPS)[0:1, :]))
                if with_f:
                    step(e.activation(out=f_r, in_=sA_r, func=AF.Ln))
                step(e.activation(out=rs_r, in_=sB_r, func=AF.Exp, scale=-0.5))
                if with_f:
                    e.wait_ge(ve.h, ve0 + 6)
                    step(e.activation(out=f_r, in_=sA_r, func=AF.Exp, scale=0.5))
        ve.v = ve0 + (6 if with_f else 5)
        ac.v = ac0 + (4 if with_f else 2)

        pe0, ac0, po0, mA0 = pe.v, ac.v, b.s["po"].v, mA.v
        INCPS = 3 if gamma is not None else 1
        bcm = [pA, pB]
        bcr = [pC, pD]
        with nc.sbuf_tensor(f"lnbc{tag}", [128, 2 * 1024], F32) as bc, \
             nc.sbuf_tensor(f"lntmp{tag}", [128, 4 * 512], F32) as tmp, \
             nc.Block(f"lnnorm{tag}") as blk:
            @blk.tensor
            def _(e):
                for blki in range(NB):
                    if blki >= 2:
                        e.wait_ge(ac.h, ac0 + blki - 1)
                    e.matmul(bcm[blki % 2][:], onesr_f32[:, 0:128],
                             mu_r[:, blki * 512:(blki + 1) * 512], start=True, stop=True)
                    e.matmul(bcr[blki % 2][:], onesr_f32[:, 0:128],
                             rs_r[:, blki * 512:(blki + 1) * 512], start=True, stop=True).then_inc(pe.h, 1)

            @blk.scalar
            def _(e):
                for blki in range(NB):
                    e.wait_ge(pe.h, pe0 + blki + 1)
                    if blki >= 2:
                        e.wait_ge(mA.h, mA0 + INCPS * (blki - 1) * DC)
                    s = (blki % 2) * 1024
                    e.copy(out=bc[:, s:s + 512], in_=bcm[blki % 2][:])
                    e.copy(out=bc[:, s + 512:s + 1024], in_=bcr[blki % 2][:]).then_inc(ac.h, 1)

            @blk.gpsimd
            def _(e):
                po = b.s["po"]
                for blki in range(NB):
                    e.wait_ge(ac.h, ac0 + blki + 1)
                    s = (blki % 2) * 1024
                    for c in range(DC):
                        k = blki * DC + c
                        if k >= 4:
                            e.wait_ge(mA.h, mA0 + INCPS * (k - 3))
                        e.tensor_sub(tmp[:, (k % 4) * 512:(k % 4) * 512 + 512],
                                     xsl(c, blki), bc[:, s:s + 512]).then_inc(po.h, 1)

            @blk.vector
            def _(e):
                po = b.s["po"]
                for blki in range(NB):
                    s = (blki % 2) * 1024
                    for c in range(DC):
                        k = blki * DC + c
                        e.wait_ge(po.h, po0 + k + 1)
                        tsl = tmp[:, (k % 4) * 512:(k % 4) * 512 + 512]
                        if gamma is None:
                            dsl_ = dst[:, c * T + blki * 512: c * T + blki * 512 + 512]
                            e.tensor_mul(dsl_, tsl, bc[:, s + 512:s + 1024]).then_inc(mA.h, 1)
                        else:
                            e.tensor_mul(tsl, tsl, bc[:, s + 512:s + 1024]).then_inc(mA.h, 1)
                            e.wait_ge(mA.h, mA0 + 3 * k + 1)
                            if add_into is None:
                                dsl_ = dst[:, c * T + blki * 512: c * T + blki * 512 + 512]
                                e.tensor_scalar(out=dsl_, in0=tsl, scalar1=vcol(b, gamma, c),
                                                scalar2=vcol(b, beta, c), op0=ALU.mult,
                                                op1=ALU.add).then_inc(mA.h, 2)
                            else:
                                e.tensor_scalar(out=tsl, in0=tsl, scalar1=vcol(b, gamma, c),
                                                scalar2=vcol(b, beta, c), op0=ALU.mult,
                                                op1=ALU.add).then_inc(mA.h, 1)
                                e.wait_ge(mA.h, mA0 + 3 * k + 2)
                                asl = add_into[:, c * T + blki * 512: c * T + blki * 512 + 512]
                                e.tensor_add(asl, asl, tsl).then_inc(mA.h, 1)
        pe.v = pe0 + NB
        ac.v = ac0 + NB
        b.s["po"].v = po0 + NK
        mA.v = mA0 + INCPS * NK


def emit_l1ffn(b: Bld, tag=""):
    nc = b.nc
    pe, ve, ac, po = b.s["pe"], b.s["ve"], b.s["ac"], b.s["po"]
    xT, xnT = b.sb["xT"], b.sb["xnT"]
    wr = b.dr("WrT_bf", [D, D], BF16)
    wk = b.dr("WkT_bf", [D, D], BF16)
    wv = b.dr("WvT_bf", [D, D], BF16)

    def shift_build(dst, mu_name, t2, sub):
        ve0 = ve.v
        with nc.Block(f"l1f_sh{tag}_{sub}") as blk:
            @blk.vector
            def _(e):
                v = ve0
                for c in range(DC):
                    e.tensor_scalar(out=t2[:, c:c + 1], in0=vcol(b, mu_name, c),
                                    scalar1=-1.0, scalar2=1.0, op0=ALU.mult, op1=ALU.add
                                    ).then_inc(ve.h, 1)
                    v += 1
                e.wait_ge(ve.h, v)
                for c in range(DC):
                    e.tensor_scalar_mul(dst[:, c * T:(c + 1) * T],
                                        xnT[:, c * T:(c + 1) * T],
                                        vcol(b, mu_name, c)).then_inc(ve.h, 1)
                    v += 1
                    e.wait_ge(ve.h, v)
                    e.scalar_tensor_tensor(out=dst[:, c * T + 1:(c + 1) * T],
                                           in0=xnT[:, c * T:(c + 1) * T - 1],
                                           scalar=t2[:, c:c + 1],
                                           in1=dst[:, c * T + 1:(c + 1) * T],
                                           op0=ALU.mult, op1=ALU.add).then_inc(ve.h, 1)
                    v += 1
        ve.v = ve0 + 3 * DC

    with nc.sbuf_tensor(f"l1f_a{tag}", [128, DC * T], BF16) as xa, \
         nc.sbuf_tensor(f"l1f_k{tag}", [128, DC * T], BF16) as kact, \
         nc.sbuf_tensor(f"l1f_r{tag}", [128, T], BF16) as rch, \
         nc.sbuf_tensor(f"l1f_w{tag}", [128, 2 * D], BF16) as wsl, \
         nc.sbuf_tensor(f"l1f_t{tag}", [128, 2 * 512], F32) as tmpf, \
         nc.sbuf_tensor(f"l1f_c{tag}", [128, 2 * DC], F32) as omc, \
         nc.psum_tensor(f"l1f_p0{tag}", [128, 512], F32) as q0, \
         nc.psum_tensor(f"l1f_p1{tag}", [128, 512], F32) as q1:
        qs = [q0, q1]
        shift_build(xa, "muk_T", omc, "k")
        dsl = [b.s["dA"], b.s["dB"]]
        d0 = [s.v for s in dsl]
        pe0, ve0, ac0 = pe.v, ve.v, ac.v
        with nc.Block(f"l1f_k{tag}") as blk:
            @blk.sync
            def _(e):
                for o in range(DC):
                    if o >= 2:
                        e.wait_ge(pe.h, pe0 + (o - 1) * NB)
                    e.dma_start(out=wsl[:, (o % 2) * D:(o % 2) * D + D].rearrange("p (c j) -> p c j", c=DC),
                                in_=wk[:, o * 128:(o + 1) * 128].rearrange("(c p) j -> p c j", p=128)
                                ).then_inc(dsl[o % 2].h, 16)

            @blk.tensor
            def _(e):
                for o in range(DC):
                    e.wait_ge(dsl[o % 2].h, d0[o % 2] + 16 * (o // 2 + 1))
                    for blki in range(NB):
                        j = o * NB + blki
                        if j >= 2:
                            e.wait_ge(ve.h, ve0 + j - 1)
                        for c in range(DC):
                            mm = e.matmul(qs[j % 2][:],
                                          wsl[:, (o % 2) * D + c * 128:(o % 2) * D + c * 128 + 128],
                                          xa[:, c * T + blki * 512: c * T + blki * 512 + 512],
                                          start=(c == 0), stop=(c == DC - 1))
                            if c == DC - 1:
                                mm.then_inc(pe.h, 1)

            @blk.vector
            def _(e):
                for j in range(DC * NB):
                    e.wait_ge(pe.h, pe0 + j + 1)
                    if j >= 2:
                        e.wait_ge(ac.h, ac0 + j - 1)
                    e.tensor_scalar_max(tmpf[:, (j % 2) * 512:(j % 2) * 512 + 512],
                                        qs[j % 2][:], 0.0).then_inc(ve.h, 1)

            @blk.scalar
            def _(e):
                for o in range(DC):
                    for blki in range(NB):
                        j = o * NB + blki
                        e.wait_ge(ve.h, ve0 + j + 1)
                        e.activation(out=kact[:, o * T + blki * 512: o * T + blki * 512 + 512],
                                     in_=tmpf[:, (j % 2) * 512:(j % 2) * 512 + 512],
                                     func=AF.Square).then_inc(ac.h, 1)
        pe.v = pe0 + DC * NB
        ve.v = ve0 + DC * NB
        ac.v = ac0 + DC * NB
        for i, s in enumerate(dsl):
            s.v = d0[i] + 16 * (DC // 2)
        shift_build(xa, "mur_T", omc, "r")
        pe0, ve0, ac0, po0 = pe.v, ve.v, ac.v, po.v
        d0 = [s.v for s in dsl]
        with nc.psum_tensor(f"l1f_p2{tag}", [128, 512], F32) as q2, \
             nc.psum_tensor(f"l1f_p3{tag}", [128, 512], F32) as q3, \
             nc.Block(f"l1f_rv{tag}") as blk:
            q_r = [q0, q1]
            q_v = [q2, q3]

            @blk.sync
            def _(e):
                for o in range(DC):
                    if o >= 1:
                        e.wait_ge(pe.h, pe0 + o * 2 * NB)
                    e.dma_start(out=wsl[:, 0:D].rearrange("p (c j) -> p c j", c=DC),
                                in_=wr[:, o * 128:(o + 1) * 128].rearrange("(c p) j -> p c j", p=128)
                                ).then_inc(dsl[0].h, 16)
                    e.dma_start(out=wsl[:, D:2 * D].rearrange("p (c j) -> p c j", c=DC),
                                in_=wv[:, o * 128:(o + 1) * 128].rearrange("(c p) j -> p c j", p=128)
                                ).then_inc(dsl[1].h, 16)

            @blk.tensor
            def _(e):
                for o in range(DC):
                    e.wait_ge(dsl[0].h, d0[0] + 16 * (o + 1))
                    e.wait_ge(dsl[1].h, d0[1] + 16 * (o + 1))
                    for blki in range(NB):
                        jr = o * NB + blki
                        if jr >= 2:
                            e.wait_ge(ac.h, ac0 + jr - 1)
                        for c in range(DC):
                            mm = e.matmul(q_r[blki % 2][:], wsl[:, c * 128:c * 128 + 128],
                                          xa[:, c * T + blki * 512: c * T + blki * 512 + 512],
                                          start=(c == 0), stop=(c == DC - 1))
                            if c == DC - 1:
                                mm.then_inc(pe.h, 1)
                        jv = jr
                        if jv >= 2:
                            e.wait_ge(ve.h, ve0 + jv - 1)
                        for c in range(DC):
                            mm = e.matmul(q_v[blki % 2][:], wsl[:, D + c * 128:D + c * 128 + 128],
                                          kact[:, c * T + blki * 512: c * T + blki * 512 + 512],
                                          start=(c == 0), stop=(c == DC - 1))
                            if c == DC - 1:
                                mm.then_inc(pe.h, 1)

            @blk.scalar
            def _(e):
                for o in range(DC):
                    for blki in range(NB):
                        j = o * NB + blki
                        e.wait_ge(pe.h, pe0 + 2 * j + 1)
                        if o >= 1:
                            e.wait_ge(ve.h, ve0 + j - NB + 1)
                        e.activation(out=rch[:, blki * 512:(blki + 1) * 512],
                                     in_=q_r[blki % 2][:], func=AF.Sigmoid).then_inc(ac.h, 1)

            @blk.vector
            def _(e):
                for o in range(DC):
                    for blki in range(NB):
                        j = o * NB + blki
                        e.wait_ge(pe.h, pe0 + 2 * j + 2)
                        e.wait_ge(ac.h, ac0 + j + 1)
                        if j >= 2:
                            e.wait_ge(po.h, po0 + j - 1)
                        e.tensor_mul(tmpf[:, (j % 2) * 512:(j % 2) * 512 + 512],
                                     rch[:, blki * 512:(blki + 1) * 512],
                                     q_v[blki % 2][:]).then_inc(ve.h, 1)

            @blk.gpsimd
            def _(e):
                for o in range(DC):
                    for blki in range(NB):
                        j = o * NB + blki
                        e.wait_ge(ve.h, ve0 + j + 1)
                        xsl_ = xT[:, o * T + blki * 512: o * T + blki * 512 + 512]
                        e.tensor_add(xsl_, xsl_, tmpf[:, (j % 2) * 512:(j % 2) * 512 + 512]
                                     ).then_inc(po.h, 1)
        pe.v = pe0 + 2 * DC * NB
        ac.v = ac0 + DC * NB
        ve.v = ve0 + DC * NB
        po.v = po0 + DC * NB
        for i, s in enumerate(dsl):
            s.v = d0[i] + 16 * DC


ENG_SEM = {"tensor": "pe", "vector": "ve", "scalar": "ac", "gpsimd": "po"}


class ES:
    def __init__(self, b: Bld):
        self.b = b
        self.evs = {k: [] for k in ["sync", "tensor", "vector", "scalar", "gpsimd"]}

    def ev(self, eng, waits, fn, sem=None, inc=None):
        b = self.b
        if sem is None:
            sem = ENG_SEM[eng]
        if inc is None:
            inc = 16 if eng == "sync" else 1
        s = b.s[sem]
        s.v += inc
        h = (s.h, s.v)
        self.evs[eng].append((list(waits), fn, s.h, inc))
        return h

    def emit(self, name):
        nc = self.b.nc
        dma_final = {}
        for waits, fn, semh, inc in self.evs["sync"]:
            dma_final[id(semh)] = (semh, None)
        for waits, fn, semh, inc in self.evs["gpsimd"]:
            if inc >= 16:
                dma_final[id(semh)] = (semh, None)
        finals = {}
        for nm in ["dA", "dB", "dC", "dD", "dE", "dF", "dH", "dI", "dw", "dg", "ds"]:
            s = self.b.s[nm]
            if id(s.h) in dma_final:
                finals[nm] = (s.h, s.v)
        with nc.Block(name) as blk:
            need_gp_final = any(nm in ("dH", "dI", "dg") for nm in finals)
            for eng in ["sync", "tensor", "vector", "scalar", "gpsimd"]:
                evl = self.evs[eng]
                if not evl and not (eng == "sync" and finals) and not (eng == "gpsimd" and need_gp_final):
                    continue

                def body(e, evl=evl, eng=eng):
                    known = {}
                    for waits, fn, semh, inc in evl:
                        best = {}
                        for (h, v) in waits:
                            k = id(h)
                            if v > best.get(k, (None, -1))[1]:
                                best[k] = (h, v)
                        for k, (h, v) in best.items():
                            if v > known.get(k, -1):
                                e.wait_ge(h, v)
                                known[k] = v
                        last = fn(e)
                        assert last is not None, "event fn must return its last instruction"
                        last.then_inc(semh, inc)
                    if eng == "sync":
                        for nm, (h, v) in finals.items():
                            if nm not in ("dH", "dI", "dg"):
                                e.wait_ge(h, v)
                    elif eng == "gpsimd":
                        for nm, (h, v) in finals.items():
                            if nm in ("dH", "dI", "dg"):
                                e.wait_ge(h, v)
                getattr(blk, eng)(body)


def emit_l1attn(b: Bld, tag=""):
    nc = b.nc
    xT, xnT = b.sb["xT"], b.sb["xnT"]
    identb = b.sb["identb"]
    ones_bf, onesr_f32, onesr_bf = b.sb["ones_bf"], b.sb["onesr_f32"], b.sb["onesr_bf"]
    triTT = b.sb["triTT"]
    neg30r_bf = b.sb["neg30r_bf"]
    wq = b.dr("wqkv_bf", [D, 3 * D], BF16)
    wp = b.dr("wproj_bf", [D, D], BF16)

    lrow = b.sb["f_row"]
    zctx = ExitStack()
    qk = zctx.enter_context(nc.sbuf_tensor(f"laqk{tag}", [128, 8 * T], BF16))
    zT = qk
    with nc.sbuf_tensor(f"lao{tag}", [128, DC * T], BF16) as aoT, \
         nc.sbuf_tensor(f"lbr{tag}", [1, 64], F32) as brow, \
         nc.sbuf_tensor(f"lbc{tag}", [128, 16], F32) as nBbc, \
         nc.sbuf_tensor(f"lwsl{tag}", [128, 2 * D], BF16) as wsl, \
         nc.sbuf_tensor(f"lET{tag}", [128, 2 * 512], BF16) as ET, \
         nc.sbuf_tensor(f"lbcs{tag}", [128, 512], F32) as bcs:
        with nc.sbuf_tensor(f"lav{tag}", [128, 16 * 512], BF16) as vtok:
            qT = qk[:, 0:4 * T]
            kT = qk[:, 4 * T:8 * T]
            for g in range(2):
                sqt = aoT[:, 4 * T * g: 4 * T * g + 2 * T]
                es = ES(b)
                with nc.psum_tensor(f"lap0{tag}{g}", [128, 512], F32) as pq0, \
                     nc.psum_tensor(f"lap1{tag}{g}", [128, 512], F32) as pq1:
                    pqs = [pq0, pq1]
                    wload = [None] * 10
                    mmdone = {}
                    evdone = {}
                    for j in range(8):
                        col = (0 if j < 4 else D) + g * 512 + (j % 4) * 128
                        deps = [mmdone[(j - 2, NB - 1)]] if j >= 2 else []

                        def ld(e, j=j, col=col):
                            return e.dma_start(
                                out=wsl[:, (j % 2) * D:(j % 2) * D + D].rearrange("p (c j) -> p c j", c=DC),
                                in_=wq[:, col:col + 128].rearrange("(c p) j -> p c j", p=128))
                        wload[j] = es.ev("sync", deps, ld, sem="dA" if j % 2 == 0 else "dB", inc=16)
                        for blki in range(NB):
                            jj = j * NB + blki
                            deps2 = [wload[j]]
                            if jj >= 2:
                                deps2.append(evdone[jj - 2])

                            def mm(e, j=j, blki=blki, jj=jj):
                                r = None
                                for c in range(DC):
                                    r = e.matmul(pqs[jj % 2][:],
                                                 wsl[:, (j % 2) * D + c * 128:(j % 2) * D + c * 128 + 128],
                                                 xnT[:, c * T + blki * 512: c * T + blki * 512 + 512],
                                                 start=(c == 0), stop=(c == DC - 1))
                                return r
                            mmdone[(j, blki)] = es.ev("tensor", deps2, mm)

                            def ev_(e, j=j, blki=blki, jj=jj):
                                dstq = qT if j < 4 else kT
                                return e.tensor_copy(
                                    out=dstq[:, (j % 4) * T + blki * 512:(j % 4) * T + blki * 512 + 512],
                                    in_=pqs[jj % 2][:])
                            evdone[jj] = es.ev("vector", [mmdone[(j, blki)]], ev_)
                    for hv in range(2):
                        depsv = [mmdone[(7, NB - 1)]] if hv == 0 else [evdone[32 + 16 * hv - 1]]

                        def ldv(e, hv=hv):
                            return e.dma_start(
                                out=wsl[:, 0:2 * D].rearrange("p (c j) -> p c j", c=DC),
                                in_=wq[:, 2 * D + g * 512 + hv * 256:2 * D + g * 512 + hv * 256 + 256
                                       ].rearrange("(c p) j -> p c j", p=128))
                        wload[8 + hv] = es.ev("sync", depsv, ldv, sem="dA" if hv == 0 else "dB", inc=16)
                        for ti in range(NT):
                            jj = 32 + 16 * hv + ti
                            deps3 = [wload[8 + hv]]
                            if jj >= 2:
                                deps3.append(evdone[jj - 2])

                            def mmv(e, ti=ti, jj=jj):
                                r = None
                                for c in range(DC):
                                    r = e.matmul(pqs[jj % 2][0:128, 0:256],
                                                 xnT[:, c * T + ti * 128: c * T + (ti + 1) * 128],
                                                 wsl[:, c * 256:c * 256 + 256],
                                                 start=(c == 0), stop=(c == DC - 1))
                                return r
                            h_mm = es.ev("tensor", deps3, mmv)

                            def evv(e, ti=ti, jj=jj, hv=hv):
                                return e.tensor_copy(out=vtok[:, ti * 512 + hv * 256:ti * 512 + hv * 256 + 256],
                                                     in_=pqs[jj % 2][0:128, 0:256])
                            evdone[jj] = es.ev("vector", [h_mm], evv)
                    es.emit(f"laqkv{tag}{g}")

                es = ES(b)
                with nc.psum_tensor(f"lapB{tag}{g}", [1, 512], F32) as pB0, \
                     nc.psum_tensor(f"lapBb{tag}{g}", [1, 512], F32) as pB1:
                    pBs = [pB0, pB1]
                    sqdone = {}
                    grpdone = {}
                    red = {}
                    for j in range(8):
                        deps = [] if j < 2 else [grpdone[2 * (j - 2) + 1]]

                        def sqf(e, j=j):
                            srcq = qT if j < 4 else kT
                            return e.activation(out=sqt[:, (j % 2) * T:(j % 2) * T + T],
                                                in_=srcq[:, (j % 4) * T:(j % 4) * T + T], func=AF.Square)
                        sqdone[j] = es.ev("scalar", deps, sqf)
                        for half in range(2):
                            gi = 2 * j + half
                            col = (0 if j < 4 else 16) + 8 * g + (j % 4) * 2 + half
                            deps2 = [sqdone[j]]
                            if gi >= 2:
                                deps2.append(red[gi - 2])

                            def mmb(e, j=j, half=half, gi=gi):
                                r = None
                                for blki in range(NB):
                                    r = e.matmul(pBs[gi % 2][:],
                                                 ones_bf[half * 64:half * 64 + 64, 0:1],
                                                 sqt[half * 64:half * 64 + 64,
                                                     (j % 2) * T + blki * 512:(j % 2) * T + blki * 512 + 512],
                                                 start=(blki == 0), stop=(blki == NB - 1))
                                return r
                            grpdone[gi] = es.ev("tensor", deps2, mmb)

                            def redf(e, gi=gi, col=col):
                                return e.reduce_sum(out=brow[:, col:col + 1],
                                                    in_=pBs[gi % 2][:], axis=AXX)
                            red[gi] = es.ev("vector", [grpdone[gi]], redf)
                    es.emit(f"laB{tag}{g}")
                es = ES(b)
                with nc.psum_tensor(f"lapN{tag}{g}", [128, 8], F32) as pN:
                    q0c, k0c, n0c, s0c = 8 * g, 16 + 8 * g, 32 + 8 * g, 48 + 8 * g
                    h1 = es.ev("vector", [], lambda e: e.tensor_mul(
                        brow[:, n0c:n0c + 8], brow[:, q0c:q0c + 8], brow[:, k0c:k0c + 8]))
                    h2 = es.ev("scalar", [h1], lambda e: e.activation(
                        out=brow[:, s0c:s0c + 8], in_=brow[:, n0c:n0c + 8], func=AF.Sqrt))
                    h3 = es.ev("vector", [h2], lambda e: e.tensor_scalar_mul(
                        brow[:, n0c:n0c + 8], brow[:, s0c:s0c + 8], -1.0 / (2.0 * T)))
                    h4 = es.ev("tensor", [h3], lambda e: e.matmul(
                        pN[:], onesr_f32[:, 0:128], brow[:, n0c:n0c + 8], start=True, stop=True))
                    h5 = es.ev("scalar", [h4], lambda e: e.copy(out=nBbc[:, 8 * g:8 * g + 8], in_=pN[:]))
                    es.emit(f"laNB{tag}{g}")
                es = ES(b)
                with nc.psum_tensor(f"laS0{tag}{g}", [128, 512], F32) as pS0, \
                     nc.psum_tensor(f"laS1{tag}{g}", [128, 512], F32) as pS1, \
                     nc.psum_tensor(f"laO{tag}{g}", [64, 512], F32) as pO, \
                     nc.psum_tensor(f"laL{tag}{g}", [1, 512], F32) as pL, \
                     nc.psum_tensor(f"laNb{tag}{g}", [64, 512], F32) as pNb:
                    pSs = [pS0, pS1]
                    normdone = {}
                    exp_hist = []
                    for hl in range(8):
                        h = 8 * g + hl
                        qc = hl // 2
                        h2 = hl % 2
                        for tqb in range(NB):
                            ntsj = 4 * tqb + 4
                            last = None
                            for tsj in range(ntsj):
                                deps = []
                                if len(exp_hist) >= 2:
                                    deps.append(exp_hist[-2])

                                def mmS(e, qc=qc, h2=h2, tqb=tqb, tsj=tsj):
                                    c0 = 128 * tsj - 512 * tqb
                                    has_mask = 0 <= c0 <= 384
                                    r = e.matmul(pSs[tsj % 2][:],
                                                 kT[h2 * 64:h2 * 64 + 64, qc * T + tsj * 128: qc * T + (tsj + 1) * 128],
                                                 qT[h2 * 64:h2 * 64 + 64, qc * T + tqb * 512: qc * T + tqb * 512 + 512],
                                                 start=True, stop=not has_mask)
                                    if has_mask:
                                        if c0 > 0:
                                            r = e.matmul(pSs[tsj % 2][:, 0:c0], neg30r_bf[:],
                                                         onesr_bf[:, 0:c0], start=False, stop=False)
                                        r = e.matmul(pSs[tsj % 2][:, c0:c0 + 128], triTT[:], identb[:],
                                                     start=False, stop=True)
                                    return r
                                hS = es.ev("tensor", deps, mmS)

                                def exf(e, h=h, tsj=tsj):
                                    return e.activation(out=ET[:, (tsj % 2) * 512:(tsj % 2) * 512 + 512],
                                                        in_=pSs[tsj % 2][:], func=AF.Exp, scale=0.125,
                                                        bias=nBbc[:, h:h + 1])
                                eh = es.ev("scalar", [hS], exf)
                                exp_hist.append(eh)

                                dep2 = [eh]
                                if (h, tqb - 1) in normdone:
                                    dep2.append(normdone[(h, tqb - 1)])
                                elif tqb == 0 and (h - 1, NB - 1) in normdone:
                                    dep2.append(normdone[(h - 1, NB - 1)])

                                def mmOL(e, hl=hl, tqb=tqb, tsj=tsj):
                                    e.matmul(pO[:], vtok[:, tsj * 512 + 64 * hl: tsj * 512 + 64 * hl + 64],
                                             ET[:, (tsj % 2) * 512:(tsj % 2) * 512 + 512],
                                             start=(tsj == 0), stop=(tsj == 4 * tqb + 3))
                                    return e.matmul(pL[:], ones_bf[:, 0:1],
                                                    ET[:, (tsj % 2) * 512:(tsj % 2) * 512 + 512],
                                                    start=(tsj == 0), stop=(tsj == 4 * tqb + 3))
                                last = es.ev("tensor", dep2, mmOL)
                            hln0 = es.ev("scalar", [last], lambda e: e.activation(
                                out=lrow[:, 0:512], in_=pL[:], func=AF.Ln))
                            hln = es.ev("scalar", [hln0], lambda e: e.activation(
                                out=lrow[:, 512:1024], in_=lrow[:, 0:512], func=AF.Exp, scale=-1.0))
                            hbc = es.ev("tensor", [hln], lambda e: e.matmul(
                                pNb[:], onesr_f32[:, 0:64], lrow[:, 512:1024], start=True, stop=True))
                            hcp = es.ev("scalar", [hbc], lambda e: e.copy(out=bcs[0:64, 0:512], in_=pNb[:]))

                            def nrm(e, h2=h2, hl=hl, g=g, tqb=tqb):
                                return e.tensor_mul(
                                    aoT[h2 * 64:h2 * 64 + 64, (hl // 2 + 4 * g) * T + tqb * 512:
                                        (hl // 2 + 4 * g) * T + tqb * 512 + 512],
                                    pO[:], bcs[0:64, 0:512])
                            normdone[(h, tqb)] = es.ev("vector", [hcp, last], nrm)
                es.emit(f"lafl{tag}{g}")
        es = ES(b)
        with nc.psum_tensor(f"laP0{tag}", [128, 512], F32) as pP0, \
             nc.psum_tensor(f"laP1{tag}", [128, 512], F32) as pP1:
            pPs = [pP0, pP1]
            wld = {}
            mmd = {}
            evd = {}
            for o in range(DC):
                deps = [mmd[(o - 2, NB - 1)]] if o >= 2 else []

                def ldp(e, o=o):
                    return e.dma_start(
                        out=wsl[:, (o % 2) * D:(o % 2) * D + D].rearrange("p (c j) -> p c j", c=DC),
                        in_=wp[:, o * 128:(o + 1) * 128].rearrange("(c p) j -> p c j", p=128))
                wld[o] = es.ev("sync", deps, ldp, sem="dA" if o % 2 == 0 else "dB", inc=16)
                for blki in range(NB):
                    jj = o * NB + blki
                    deps2 = [wld[o]]
                    if jj >= 2:
                        deps2.append(evd[jj - 2])

                    def mmp(e, o=o, blki=blki, jj=jj):
                        r = None
                        for c in range(DC):
                            r = e.matmul(pPs[jj % 2][:],
                                         wsl[:, (o % 2) * D + c * 128:(o % 2) * D + c * 128 + 128],
                                         aoT[:, c * T + blki * 512: c * T + blki * 512 + 512],
                                         start=(c == 0), stop=(c == DC - 1))
                        return r
                    mmd[(o, blki)] = es.ev("tensor", deps2, mmp)

                    def evp(e, o=o, blki=blki, jj=jj):
                        return e.activation(out=zT[:, o * T + blki * 512: o * T + blki * 512 + 512],
                                            in_=pPs[jj % 2][:], func=AF.Identity,
                                            bias=vcol(b, "bproj_T", o))
                    hz = es.ev("scalar", [mmd[(o, blki)]], evp)

                    def addxn(e, o=o, blki=blki):
                        sl = slice(o * T + blki * 512, o * T + blki * 512 + 512)
                        return e.tensor_add(zT[:, sl], zT[:, sl], xnT[:, sl])
                    evd[jj] = es.ev("vector", [hz], addxn)
            es.emit(f"laproj{tag}")
    emit_ln(b, zT, None, False, False, gamma="lngam_T", beta="lnbet_T",
            add_into=xT, tag=f"li{tag}")
    zctx.close()


def emit_seprep(b: Bld):
    nc = b.nc
    identb = b.sb["identb"]
    seF = b.ein("se_flat")
    seT = b.dr("seT_bf", [D, NSLOT], BF16)
    es = ES(b)
    with nc.sbuf_tensor("sp_in", [128, 2 * D], F32) as xin, \
         nc.sbuf_tensor("sp_sq", [128, D], F32) as sqv, \
         nc.sbuf_tensor("sp_c", [128, 8], F32) as cols, \
         nc.sbuf_tensor("sp_nb", [128, 2 * D], BF16) as nb, \
         nc.sbuf_tensor("sp_st", [128, 2 * D], BF16) as strip, \
         nc.psum_tensor("sp_p0", [128, 128], BF16) as tp0, \
         nc.psum_tensor("sp_p1", [128, 128], BF16) as tp1:
        tps = [tp0, tp1]
        lddone = {}
        nrmdone = {}
        trdone = {}
        evdone = {}
        stdone = {}
        for i in range(NSLOT // 128):
            sl = (i % 2) * D

            def ld(e, i=i, sl=sl):
                return e.dma_start(out=xin[:, sl:sl + D], in_=seF[i * 128:(i + 1) * 128, :])
            deps = [nrmdone[i - 2]] if i >= 2 else []
            lddone[i] = es.ev("sync", deps, ld, sem="dA" if i % 2 == 0 else "dB", inc=16)

            def sq(e, i=i, sl=sl):
                return e.activation(out=sqv[:], in_=xin[:, sl:sl + D], func=AF.Square)
            hsq = es.ev("scalar", [lddone[i]] + ([nrmdone[i - 1]] if i >= 1 else []), sq)

            def red(e, i=i):
                return e.reduce_sum(out=cols[:, 0:1], in_=sqv[:], axis=AXX)
            hred = es.ev("vector", [hsq], red)
            hsq2 = es.ev("scalar", [hred], lambda e: e.activation(out=cols[:, 1:2], in_=cols[:, 0:1], func=AF.Sqrt))
            hmx = es.ev("vector", [hsq2], lambda e: e.tensor_scalar_max(cols[:, 2:3], cols[:, 1:2], 1e-12))
            hrc = es.ev("vector", [hmx], lambda e: e.reciprocal(out=cols[:, 3:4], in_=cols[:, 2:3]))
            hsc = es.ev("vector", [hrc], lambda e: e.tensor_scalar_mul(cols[:, 4:5], cols[:, 3:4], float(np.sqrt(D))))

            def nrm(e, i=i, sl=sl):
                return e.tensor_scalar_mul(nb[:, sl:sl + D], xin[:, sl:sl + D], cols[:, 4:5])
            nrmdone[i] = es.ev("vector", [hsc], nrm)
            for c in range(DC):
                k = i * DC + c
                deps = [nrmdone[i]]
                if k >= 2:
                    deps.append(evdone[k - 2])

                def tr(e, i=i, c=c, k=k, sl=sl):
                    return e.transpose(tps[k % 2][:], nb[:, sl + c * 128:sl + c * 128 + 128], identb[:])
                trdone[k] = es.ev("tensor", deps, tr)

                def ev_(e, i=i, c=c, k=k, sl=sl):
                    return e.tensor_scalar_mul(strip[:, sl + c * 128:sl + c * 128 + 128],
                                               tps[k % 2][:], vcol(b, "gamma_slot_T", c))
                deps = [trdone[k]]
                if i >= 2:
                    deps.append(stdone[i - 2])
                evdone[k] = es.ev("vector", deps, ev_)

            def st(e, i=i, sl=sl):
                return e.dma_start(
                    out=seT.rearrange("(c p) n -> p c n", p=128)[:, :, i * 128:(i + 1) * 128],
                    in_=strip[:, sl:sl + D].rearrange("p (c j) -> p c j", c=DC))
            stdone[i] = es.ev("sync", [evdone[i * DC + DC - 1]], st,
                              sem="dE" if i % 2 == 0 else "dF", inc=16)
        es.emit("seprep")


def emit_l0ffn(b: Bld, tag="", upto=99):
    nc = b.nc
    xT, xnT = b.sb["xT"], b.sb["xnT"]
    identb, identf = b.sb["identb"], b.sb["identf"]
    ones_bf, onesr_f32 = b.sb["ones_bf"], b.sb["onesr_f32"]
    f_r = b.sb["f_row"][:]
    seT = b.dr("seT_bf", [D, NSLOT], BF16)
    w1d = b.dr("w1f_bf", [D, HIDF], BF16)
    w2d = b.dr("w2f_bf", [HIDF, D], BF16)
    Lscr = b.dr("l0_Lscr", [T, NSLOT], F32)
    Escr = b.dr("l0_Escr", [NSLOT, T], BF16)
    RSscr = b.dr("l0_rs", [1, 2 * T], F32)
    cmrow_d = b.dr("l0_cmrow", [1, NSLOT], F32)
    mc = b.sbt("l0_mc", [128, 64], F32)
    cm2 = b.sbt("l0_cm2", [128, 36], F32)
    cmx32 = b.sbt("l0_cmx", [32, 128], F32)
    NCH = 8

    es = ES(b)
    actx = ExitStack()
    pbs = [actx.enter_context(nc.psum_tensor(f"l0fp{i}{tag}", [128, 512], F32)) for i in range(2)]
    with nc.sbuf_tensor(f"l0fbc{tag}", [128, 2 * 512], F32) as fbc:
        bcev = {}
        cpev = {}
        xnrev = {}
        for blki in range(NB):
            deps = [cpev[blki - 2]] if blki >= 2 else []

            def bc(e, blki=blki):
                return e.matmul(pbs[blki % 2][:], onesr_f32[:, 0:128],
                                f_r[:, blki * 512:(blki + 1) * 512], start=True, stop=True)
            bcev[blki] = es.ev("tensor", deps, bc)
            deps = [bcev[blki]]
            if blki >= 2:
                deps.append(xnrev[(blki - 2, DC - 1)])

            def cp(e, blki=blki):
                return e.copy(out=fbc[:, (blki % 2) * 512:(blki % 2) * 512 + 512],
                              in_=pbs[blki % 2][:])
            cpev[blki] = es.ev("scalar", deps, cp)
            for c in range(DC):
                def xr1(e, blki=blki, c=c):
                    sl = slice(c * T + blki * 512, c * T + blki * 512 + 512)
                    return e.tensor_mul(xnT[:, sl], xnT[:, sl],
                                        fbc[:, (blki % 2) * 512:(blki % 2) * 512 + 512])
                h1 = es.ev("vector", [cpev[blki]], xr1)

                def xr2(e, blki=blki, c=c):
                    sl = slice(c * T + blki * 512, c * T + blki * 512 + 512)
                    return e.tensor_scalar_mul(xnT[:, sl], xnT[:, sl], vcol(b, "gamma_x_T", c))
                xnrev[(blki, c)] = es.ev("vector", [h1], xr2)
        es.emit(f"l0fA{tag}")
    actx.close()
    if upto < 2:
        return

    sectx = ExitStack()
    seslL = [sectx.enter_context(nc.sbuf_tensor(f"l0se{n}{tag}", [128, 4096], BF16))
             for n in range(NCH)]
    if True:
        ldse = {}
        es = ES(b)
        b1ctx = ExitStack()
        pRow = [b1ctx.enter_context(nc.psum_tensor(f"l0pr{n}{tag}", [128, 512], F32))
                for n in range(NCH)]
        lsl = b1ctx.enter_context(nc.sbuf_tensor(f"l0lsl{tag}", [128, 2 * 512], F32))
        etil = b1ctx.enter_context(nc.sbuf_tensor(f"l0eti{tag}", [128, 2 * 512], F32))
        lse_sems = ["dA", "dB", "dC", "dD"]
        for nch in range(NCH):
            def lse(e, nch=nch):
                return e.dma_start(
                    out=seslL[nch][:].rearrange("p (c j) -> p c j", c=DC),
                    in_=seT[:, nch * 512:(nch + 1) * 512].rearrange("(c p) j -> p c j", p=128))
            deps = [ldse[nch - 4]] if nch >= 4 else []
            ldse[nch] = es.ev("sync", deps, lse, sem=lse_sems[nch % 4], inc=16)
        mmL = {}
        evL = {}
        stLj = {}
        rmx = {}
        exd = {}
        rsd = {}
        for i in range(NT):
            for nch in range(NCH):
                j = i * NCH + nch
                deps = [ldse[nch]]
                if i >= 1:
                    deps.append(exd[(i - 1, nch)])

                def mm(e, i=i, nch=nch, j=j):
                    r = None
                    for c in range(DC):
                        r = e.matmul(pRow[nch][:],
                                     xnT[:, c * T + i * 128: c * T + (i + 1) * 128],
                                     seslL[nch][:, c * 512:(c + 1) * 512],
                                     start=(c == 0), stop=(c == DC - 1))
                    return r
                mmL[j] = es.ev("tensor", deps, mm)
                deps = [mmL[j]]
                if j >= 2:
                    deps.append(stLj[j - 2])

                def evc(e, nch=nch, j=j):
                    return e.activation(out=lsl[:, (j % 2) * 512:(j % 2) * 512 + 512],
                                        in_=pRow[nch][:], func=AF.Identity)
                evL[j] = es.ev("scalar", deps, evc)

                def rm1(e, j=j):
                    return e.reduce_max(out=mc[:, 48:49],
                                        in_=lsl[:, (j % 2) * 512:(j % 2) * 512 + 512], axis=AXX)
                rdeps = [evL[j]]
                if j >= 1:
                    rdeps.append(rmx[j - 1])
                rmh = es.ev("vector", rdeps, rm1)
                if nch == 0:
                    def rm2(e, i=i):
                        return e.tensor_copy(out=mc[:, i:i + 1], in_=mc[:, 48:49])
                else:
                    def rm2(e, i=i):
                        return e.tensor_max(mc[:, i:i + 1], mc[:, i:i + 1], mc[:, 48:49])
                rmx[j] = es.ev("vector", [rmh], rm2)

                def st(e, i=i, nch=nch, j=j):
                    return e.dma_start(
                        out=Lscr[i * 128:(i + 1) * 128, nch * 512:(nch + 1) * 512],
                        in_=lsl[:, (j % 2) * 512:(j % 2) * 512 + 512])
                stLj[j] = es.ev("sync", [evL[j]], st, sem="dE" if j % 2 == 0 else "dF", inc=16)
            hneg = es.ev("vector", [rmx[i * NCH + NCH - 1]],
                         lambda e, i=i: e.tensor_scalar_mul(mc[:, i:i + 1], mc[:, i:i + 1], -1.0))
            for nch in range(NCH):
                deps = [hneg, evL[i * NCH + nch]]
                if nch >= 2:
                    deps.append(rsd[(i, nch - 2)])

                def ex(e, i=i, nch=nch):
                    return e.activation(out=etil[:, (nch % 2) * 512:(nch % 2) * 512 + 512],
                                        in_=pRow[nch][:], func=AF.Exp, bias=mc[:, i:i + 1])
                exd[(i, nch)] = es.ev("scalar", deps, ex)

                def rs1(e, i=i, nch=nch):
                    dst = mc[:, 16 + i:17 + i] if nch == 0 else mc[:, 49:50]
                    return e.reduce_sum(out=dst, in_=etil[:, (nch % 2) * 512:(nch % 2) * 512 + 512],
                                        axis=AXX)
                rdeps = [exd[(i, nch)]]
                if nch >= 1:
                    rdeps.append(rsd[(i, nch - 1)])
                hrs = es.ev("vector", rdeps, rs1)
                if nch == 0:
                    rsd[(i, nch)] = hrs
                else:
                    def rs2(e, i=i):
                        return e.tensor_add(mc[:, 16 + i:17 + i], mc[:, 16 + i:17 + i], mc[:, 49:50])
                    rsd[(i, nch)] = es.ev("vector", [hrs], rs2)
        last_hneg_v = rsd[(NT - 1, NCH - 1)]
        hrec = es.ev("vector", [last_hneg_v], lambda e: e.reciprocal(out=mc[:, 32:48],
                                                                    in_=mc[:, 16:32]))
        es.emit(f"l0fB1{tag}")
        b1ctx.close()
        if upto < 3:
            sectx.close()
            return

        es = ES(b)
        b2ctx = ExitStack()
        nmbc = b2ctx.enter_context(nc.sbuf_tensor(f"l0nbc{tag}", [128, T], F32))
        etst = b2ctx.enter_context(nc.sbuf_tensor(f"l0ets{tag}", [128, 2 * T], BF16))
        rawt = b2ctx.enter_context(nc.sbuf_tensor(f"l0raw{tag}", [128, 2 * 512], F32))
        r16 = b2ctx.enter_context(nc.sbuf_tensor(f"l0r16{tag}", [16, 128], F32))
        pT0 = b2ctx.enter_context(nc.psum_tensor(f"l0pT0{tag}", [128, 512], F32))
        pT1 = b2ctx.enter_context(nc.psum_tensor(f"l0pT1{tag}", [128, 512], F32))
        pB0 = b2ctx.enter_context(nc.psum_tensor(f"l0pB{tag}", [128, 512], F32))
        pTs = [pT0, pT1]
        row1 = f_r
        t1 = es.ev("tensor", [], lambda e: e.transpose(pB0[0:16, 0:128], mc[:, 0:16], identf[:]))
        t2 = es.ev("vector", [t1], lambda e: e.tensor_copy(out=r16[:], in_=pB0[0:16, 0:128]))
        t3 = es.ev("sync", [t2], lambda e: e.dma_start(
            out=RSscr[:, 0:T].rearrange("o (c j) -> c j", c=16), in_=r16[:]), sem="dC", inc=16)
        t4 = es.ev("sync", [t3], lambda e: e.dma_start(out=row1, in_=RSscr[:, 0:T]),
                   sem="dD", inc=16)
        bcd = {}
        for q4 in range(NB):
            deps = [t4] + ([bcd[q4 - 1]] if q4 >= 1 else [t2])

            def bcf(e, q4=q4):
                return e.matmul(pB0[:], onesr_f32[:, 0:128],
                                row1[:, q4 * 512:(q4 + 1) * 512], start=True, stop=True)
            hb = es.ev("tensor", deps, bcf)

            def cpf(e, q4=q4):
                return e.copy(out=nmbc[:, q4 * 512:(q4 + 1) * 512], in_=pB0[:])
            bcd[q4] = es.ev("scalar", [hb], cpf)
        u1 = es.ev("tensor", [bcd[NB - 1]], lambda e: e.transpose(
            pB0[0:16, 0:128], mc[:, 32:48], identf[:]))
        u2 = es.ev("vector", [u1, t3], lambda e: e.tensor_copy(out=r16[:], in_=pB0[0:16, 0:128]))
        u3 = es.ev("sync", [u2], lambda e: e.dma_start(
            out=RSscr[:, T:2 * T].rearrange("o (c j) -> c j", c=16), in_=r16[:]), sem="dE", inc=16)
        mmT = {}
        evTd = {}
        cmx = {}
        exT = {}
        adT = {}
        stT = {}
        for nch in range(NCH):
            for st4 in range(4):
                sti = nch * 4 + st4
                for tb in range(NB):
                    k = sti * NB + tb
                    deps = [ldse[nch], bcd[NB - 1]]
                    if k >= 2:
                        deps.append(evTd[k - 2])

                    def mT(e, nch=nch, st4=st4, k=k, tb=tb):
                        r = None
                        for c in range(DC):
                            base = c * 512 + st4 * 128
                            r = e.matmul(pTs[k % 2][:], seslL[nch][:, base:base + 128],
                                         xnT[:, c * T + tb * 512: c * T + tb * 512 + 512],
                                         start=(c == 0), stop=(c == DC - 1))
                        return r
                    mmT[k] = es.ev("tensor", deps, mT)

                    def evT(e, k=k):
                        return e.activation(out=rawt[:, (k % 2) * 512:(k % 2) * 512 + 512],
                                            in_=pTs[k % 2][:], func=AF.Identity)
                    edeps = [mmT[k]]
                    if k >= 2:
                        edeps.append(exT[k - 2])
                    evTd[k] = es.ev("scalar", edeps, evT)

                    def rc1(e, k=k):
                        return e.reduce_max(out=cm2[:, 32:33],
                                            in_=rawt[:, (k % 2) * 512:(k % 2) * 512 + 512], axis=AXX)
                    rdeps = [evTd[k]]
                    if k >= 1:
                        rdeps.append(cmx[k - 1])
                    hrc = es.ev("vector", rdeps, rc1)
                    if tb == 0:
                        def rc2(e, sti=sti):
                            return e.tensor_copy(out=cm2[:, sti:sti + 1], in_=cm2[:, 32:33])
                    else:
                        def rc2(e, sti=sti):
                            return e.tensor_max(cm2[:, sti:sti + 1], cm2[:, sti:sti + 1],
                                                cm2[:, 32:33])
                    cmx[k] = es.ev("vector", [hrc], rc2)

                    def aT(e, k=k, tb=tb):
                        sl = slice((k % 2) * 512, (k % 2) * 512 + 512)
                        return e.tensor_tensor(out=rawt[:, sl], in0=rawt[:, sl],
                                               in1=nmbc[:, tb * 512:(tb + 1) * 512], op=ALU.add)
                    adeps = [evTd[k], cmx[k]]
                    if k >= 2:
                        adeps.append(exT[k - 2])
                    adT[k] = es.ev("vector", adeps, aT)

                    xdeps = [adT[k]]
                    if tb == 0 and sti >= 2:
                        xdeps.append(stT[sti - 2])

                    def xf(e, k=k, sti=sti, tb=tb):
                        return e.activation(
                            out=etst[:, (sti % 2) * T + tb * 512:(sti % 2) * T + tb * 512 + 512],
                            in_=rawt[:, (k % 2) * 512:(k % 2) * 512 + 512], func=AF.Exp)
                    exT[k] = es.ev("scalar", xdeps, xf)

                def sT(e, nch=nch, st4=st4, sti=sti):
                    s0 = nch * 512 + st4 * 128
                    return e.dma_start(out=Escr[s0:s0 + 128, :],
                                       in_=etst[:, (sti % 2) * T:(sti % 2) * T + T])
                stT[sti] = es.ev("sync", [exT[sti * NB + NB - 1]], sT,
                                 sem="dA" if sti % 2 == 0 else "dB", inc=16)
        tc1 = es.ev("tensor", [cmx[NCH * 4 * NB - 1], u2], lambda e: e.transpose(
            pB0[0:32, 0:128], cm2[:, 0:32], identf[:]))
        tc2 = es.ev("vector", [tc1, u2], lambda e: e.tensor_copy(out=cmx32[:], in_=pB0[0:32, 0:128]))
        tc3 = es.ev("sync", [tc2], lambda e: e.dma_start(
            out=cmrow_d.rearrange("o (c j) -> c j", c=32), in_=cmx32[:]), sem="dF", inc=16)
        es.emit(f"l0fB2{tag}")
        b2ctx.close()
    sectx.close()
    if upto < 4:
        return

    octx = ExitStack()
    xtok = octx.enter_context(nc.sbuf_tensor(f"l0xtk{tag}", [128, NT * D], BF16))
    cmbc = octx.enter_context(nc.sbuf_tensor(f"l0cmb{tag}", [128, NSLOT], F32))
    bcT = octx.enter_context(nc.sbuf_tensor(f"l0bcT{tag}", [128, T], F32))
    slostr = octx.enter_context(nc.sbuf_tensor(f"l0slo{tag}", [128, DC * 512], BF16))
    procbuf = octx.enter_context(nc.sbuf_tensor(f"l0pcb{tag}", [128, 4 * D], BF16))

    es = ES(b)
    a2ctx = ExitStack()
    tqs = [a2ctx.enter_context(nc.psum_tensor(f"l0ft{i2}{tag}", [128, 128], BF16))
           for i2 in range(4)]
    trev = {}
    tevv = {}
    for i in range(NT):
        for c in range(DC):
            k = i * DC + c
            deps = []
            if k >= 4:
                deps.append(tevv[k - 4])

            def tr(e, i=i, c=c, k=k):
                return e.transpose(tqs[k % 4][:], xnT[:, c * T + i * 128: c * T + (i + 1) * 128],
                                   identb[:])
            trev[k] = es.ev("tensor", deps, tr)

            def tv(e, i=i, c=c, k=k):
                return e.tensor_copy(out=xtok[:, i * D + c * 128: i * D + c * 128 + 128],
                                     in_=tqs[k % 4][:])

            def tv_sc(e, i=i, c=c, k=k):
                return e.copy(out=xtok[:, i * D + c * 128: i * D + c * 128 + 128],
                              in_=tqs[k % 4][:])
            if k % 2 == 0:
                tevv[k] = es.ev("vector", [trev[k]], tv)
            else:
                tevv[k] = es.ev("scalar", [trev[k]], tv_sc)
    es.emit(f"l0fA2{tag}")
    a2ctx.close()
    if upto < 5:
        octx.close()
        return

    es = ES(b)
    cbctx = ExitStack()
    pc0 = cbctx.enter_context(nc.psum_tensor(f"l0pc0{tag}", [128, 512], F32))
    pc1 = cbctx.enter_context(nc.psum_tensor(f"l0pc1{tag}", [128, 512], F32))
    cmr = cbctx.enter_context(nc.sbuf_tensor(f"l0cmr{tag}", [1, NSLOT], F32))
    pcs2 = [pc0, pc1]
    ldc = es.ev("sync", [], lambda e: e.dma_start(out=cmr[:], in_=cmrow_d), sem="dA", inc=16)
    ldr = es.ev("sync", [], lambda e: e.dma_start(out=b.sb["f_row"][:], in_=RSscr[:, T:2 * T]),
                sem="dB", inc=16)
    prevc = {}
    for nch2 in range(8):
        deps = [ldc] + ([prevc[nch2 - 2]] if nch2 >= 2 else [])

        def bcc(e, nch2=nch2):
            return e.matmul(pcs2[nch2 % 2][:], onesr_f32[:, 0:128],
                            cmr[:, nch2 * 512:(nch2 + 1) * 512], start=True, stop=True)
        h1 = es.ev("tensor", deps, bcc)

        def cpc(e, nch2=nch2):
            return e.copy(out=cmbc[:, nch2 * 512:(nch2 + 1) * 512], in_=pcs2[nch2 % 2][:])
        prevc[nch2] = es.ev("scalar", [h1], cpc)
    prevr = {}
    for q4 in range(NB):
        deps = [ldr] + ([prevr[q4 - 2]] if q4 >= 2 else [prevc[7], prevc[6]])

        def bcr(e, q4=q4):
            return e.matmul(pcs2[q4 % 2][:], onesr_f32[:, 0:128],
                            b.sb["f_row"][:, q4 * 512:(q4 + 1) * 512], start=True, stop=True)
        h1 = es.ev("tensor", deps, bcr)

        def cpr(e, q4=q4):
            return e.copy(out=bcT[:, q4 * 512:(q4 + 1) * 512], in_=pcs2[q4 % 2][:])
        prevr[q4] = es.ev("scalar", [h1], cpr)
    es.emit(f"l0fCbc{tag}")
    cbctx.close()
    if upto < 6:
        octx.close()
        return

    EdStrip = xnT[:, 0:16 * 512]
    hcache = xnT[:, 0:32 * 512]
    for et in range(8):
        es = ES(b)
        cctx = ExitStack()
        psl = [cctx.enter_context(nc.psum_tensor(f"l0s{i2}{tag}{et}", [128, 512], F32))
               for i2 in range(6)]
        pcs = cctx.enter_context(nc.psum_tensor(f"l0cs{tag}{et}", [128, 512], F32))
        lbuf = cctx.enter_context(nc.sbuf_tensor(f"l0led{tag}{et}", [128, 4 * 512], F32))
        rrow = cctx.enter_context(nc.sbuf_tensor(f"l0rr{tag}{et}", [1, 2 * 512], F32))
        bcs = cctx.enter_context(nc.sbuf_tensor(f"l0bcs{tag}{et}", [128, 512], F32))
        ps0, ps1 = psl[0], psl[1]
        ldL = {}
        edv = {}
        mmS = {}
        for i in range(NT):
            deps = [edv[i - 2]] if i >= 2 else []

            def lL(e, i=i, et=et):
                return e.dma_start(out=lbuf[:, (i % 2) * 512:(i % 2) * 512 + 512],
                                   in_=Lscr[i * 128:(i + 1) * 128, et * 512:(et + 1) * 512])
            ldL[i] = es.ev("sync", deps, lL, sem="dA" if i % 2 == 0 else "dB", inc=16)
            deps = [ldL[i]]
            if i >= 1:
                deps.append(edv[i - 1])

            def sub(e, i=i, et=et):
                return e.tensor_sub(lbuf[:, 2 * 512 + (i % 2) * 512:2 * 512 + (i % 2) * 512 + 512],
                                    lbuf[:, (i % 2) * 512:(i % 2) * 512 + 512],
                                    cmbc[:, et * 512:(et + 1) * 512])
            hsub = es.ev("vector", deps, sub)

            def edx(e, i=i):
                return e.activation(out=EdStrip[:, i * 512:(i + 1) * 512],
                                    in_=lbuf[:, 2 * 512 + (i % 2) * 512:2 * 512 + (i % 2) * 512 + 512],
                                    func=AF.Exp)
            edv[i] = es.ev("scalar", [hsub], edx)

            def mmp1(e, i=i):
                for j in range(6):
                    e.matmul(psl[j][:], xtok[:, i * D + j * 128: i * D + j * 128 + 128],
                             EdStrip[:, i * 512:(i + 1) * 512],
                             start=(i == 0), stop=(i == NT - 1))
                return e.matmul(pcs[0:1, :], ones_bf[:, 0:1],
                                EdStrip[:, i * 512:(i + 1) * 512],
                                start=(i == 0), stop=(i == NT - 1))
            mmS[i] = es.ev("tensor", [edv[i]], mmp1)
        lastm = mmS[NT - 1]
        hl1 = es.ev("scalar", [lastm], lambda e: e.activation(
            out=rrow[:, 0:512], in_=pcs[0:1, :], func=AF.Ln))
        hl2 = es.ev("scalar", [hl1], lambda e: e.activation(
            out=rrow[:, 512:1024], in_=rrow[:, 0:512], func=AF.Exp, scale=-1.0))
        hbc2 = es.ev("tensor", [hl2], lambda e: e.matmul(
            pcs[:], onesr_f32[:, 0:128], rrow[:, 512:1024], start=True, stop=True))
        hcp2 = es.ev("scalar", [hbc2], lambda e: e.copy(out=bcs[:], in_=pcs[:]))
        sev = {}
        for j in range(6):
            def sv(e, j=j):
                return e.tensor_mul(slostr[:, j * 512:(j + 1) * 512], psl[j][:], bcs[:])
            sev[j] = es.ev("vector", [hcp2], sv)
        mm2 = {}
        for i in range(NT):
            deps = [sev[0], sev[1]] if i == 0 else [mm2[i - 1]]

            def mmp2(e, i=i):
                e.matmul(ps0[:], xtok[:, i * D + 6 * 128: i * D + 6 * 128 + 128],
                         EdStrip[:, i * 512:(i + 1) * 512],
                         start=(i == 0), stop=(i == NT - 1))
                return e.matmul(ps1[:], xtok[:, i * D + 7 * 128: i * D + 7 * 128 + 128],
                                EdStrip[:, i * 512:(i + 1) * 512],
                                start=(i == 0), stop=(i == NT - 1))
            mm2[i] = es.ev("tensor", deps, mmp2)
        for j in range(2):
            def sv2(e, j=j):
                return e.tensor_mul(slostr[:, (6 + j) * 512:(7 + j) * 512],
                                    (ps0 if j == 0 else ps1)[:], bcs[:])
            sev[6 + j] = es.ev("vector", [mm2[NT - 1]], sv2)
        es.emit(f"l0fC{tag}{et}")
        cctx.close()
        if upto < 7:
            octx.close()
            return

        es = ES(b)
        dctx = ExitStack()
        ph = [dctx.enter_context(nc.psum_tensor(f"l0h{i2}{tag}{et}", [128, 512], F32))
              for i2 in range(2)]
        prs = [dctx.enter_context(nc.psum_tensor(f"l0pp{i2}{tag}{et}", [128, 512], F32))
               for i2 in range(4)]
        w1s = dctx.enter_context(nc.sbuf_tensor(f"l0w1s{tag}{et}", [128, 2 * D], BF16))
        w2s = dctx.enter_context(nc.sbuf_tensor(f"l0w2s{tag}{et}", [128, 2 * 512], BF16))
        ldw1 = {}
        mmh = {}
        gel = {}
        evA = {}
        for hc in range(32):
            deps = [mmh[hc - 2]] if hc >= 2 else []

            def lw1(e, hc=hc):
                return e.dma_start(
                    out=w1s[:, (hc % 2) * D:(hc % 2) * D + D].rearrange("p (c j) -> p c j", c=DC),
                    in_=w1d[:, hc * 128:(hc + 1) * 128].rearrange("(c p) j -> p c j", p=128))
            ldw1[hc] = es.ev("sync", deps, lw1, sem="dC" if hc % 2 == 0 else "dD", inc=16)

            def mh(e, hc=hc):
                r = None
                for c in range(DC):
                    r = e.matmul(ph[hc % 2][:],
                                 w1s[:, (hc % 2) * D + c * 128:(hc % 2) * D + c * 128 + 128],
                                 slostr[:, c * 512:(c + 1) * 512],
                                 start=(c == 0), stop=(c == DC - 1))
                return r
            deps = [ldw1[hc]]
            if hc >= 2:
                deps.append(gel[hc - 2])
            mmh[hc] = es.ev("tensor", deps, mh)

            def gl(e, hc=hc):
                return e.activation(out=hcache[:, hc * 512:(hc + 1) * 512],
                                    in_=ph[hc % 2][:], func=AF.Gelu)
            gel[hc] = es.ev("scalar", [mmh[hc]], gl)
        for dh in range(2):
            ldw2 = {}
            mpr = {}
            for hc in range(32):
                deps = [mpr[hc - 2]] if hc >= 2 else ([evA[su] for su in range(4)] if dh == 1 else [])

                def lw2(e, hc=hc, dh=dh):
                    return e.dma_start(out=w2s[:, (hc % 2) * 512:(hc % 2) * 512 + 512],
                                       in_=w2d[hc * 128:(hc + 1) * 128, dh * 512:(dh + 1) * 512])
                ldw2[hc] = es.ev("sync", deps + ([ldw2[hc - 2]] if hc >= 2 else []), lw2,
                                 sem="dE" if hc % 2 == 0 else "dF", inc=16)

                def mp(e, hc=hc):
                    r = None
                    for su in range(4):
                        r = e.matmul(prs[su][:],
                                     hcache[:, hc * 512 + su * 128: hc * 512 + su * 128 + 128],
                                     w2s[:, (hc % 2) * 512:(hc % 2) * 512 + 512],
                                     start=(hc == 0), stop=(hc == 31))
                    return r
                mdeps = [gel[hc], ldw2[hc]]
                if hc == 0 and dh == 1:
                    mdeps.extend([evA[su] for su in range(4)])
                mpr[hc] = es.ev("tensor", mdeps, mp)
            for su in range(4):
                def pv(e, su=su, dh=dh):
                    return e.tensor_copy(out=procbuf[:, su * D + dh * 512: su * D + dh * 512 + 512],
                                         in_=prs[su][:])

                def pv_sc(e, su=su, dh=dh):
                    return e.copy(out=procbuf[:, su * D + dh * 512: su * D + dh * 512 + 512],
                                  in_=prs[su][:])
                if su % 2 == 0:
                    evA[su] = es.ev("vector", [mpr[31]], pv)
                else:
                    evA[su] = es.ev("scalar", [mpr[31]], pv_sc)
        es.emit(f"l0fD{tag}{et}")
        dctx.close()
        if upto < 8:
            octx.close()
            return

        es = ES(b)
        ectx = ExitStack()
        qP = [ectx.enter_context(nc.psum_tensor(f"l0q{i2}{tag}{et}", [128, 512], F32))
              for i2 in range(8)]
        ETet = ectx.enter_context(nc.sbuf_tensor(f"l0ete{tag}{et}", [128, 4 * T], BF16))
        tmq = ectx.enter_context(nc.sbuf_tensor(f"l0tmq{tag}{et}", [128, 2 * 512], F32))
        ETld = {}
        for st4 in range(4):
            def le_(e, st4=st4, et=et):
                s0 = et * 512 + st4 * 128
                return e.dma_start(out=ETet[:, st4 * T:(st4 + 1) * T], in_=Escr[s0:s0 + 128, :])
            ETld[st4] = es.ev("sync", [], le_, sem="dA" if st4 % 2 == 0 else "dB", inc=16)
        mmC = {}
        evq = {}
        adq = {}
        for tb in range(NB):
            for c in range(DC):
                deps = [ETld[s4] for s4 in range(4)]
                if tb >= 1:
                    deps.append(evq[(tb - 1, c)])

                def mq(e, tb=tb, c=c):
                    r = None
                    for st4 in range(4):
                        r = e.matmul(qP[c][:],
                                     procbuf[:, st4 * D + c * 128: st4 * D + c * 128 + 128],
                                     ETet[:, st4 * T + tb * 512: st4 * T + tb * 512 + 512],
                                     start=(st4 == 0), stop=(st4 == 3))
                    return r
                mmC[(tb, c)] = es.ev("tensor", deps, mq)
                kq = tb * DC + c

                def eq(e, tb=tb, c=c, kq=kq):
                    return e.tensor_mul(tmq[:, (kq % 2) * 512:(kq % 2) * 512 + 512],
                                        qP[c][:], bcT[:, tb * 512:(tb + 1) * 512])
                qdeps = [mmC[(tb, c)]]
                if kq >= 2:
                    qdeps.append(adq[kq - 2])
                evq[(tb, c)] = es.ev("vector", qdeps, eq)

                def aq(e, tb=tb, c=c, kq=kq):
                    sl = slice(c * T + tb * 512, c * T + tb * 512 + 512)
                    return e.tensor_add(xT[:, sl], xT[:, sl],
                                        tmq[:, (kq % 2) * 512:(kq % 2) * 512 + 512])
                adq[kq] = es.ev("gpsimd", [evq[(tb, c)]], aq)
        es.emit(f"l0fE{tag}{et}")
        ectx.close()
        if upto < 9 and et == 0:
            octx.close()
            return
    octx.close()
    return


def emit_l0attn_prologue(b: Bld):
    nc = b.nc
    pald = b.dr("pal_bf", [D, 256], BF16)
    wod = b.dr("Wo_bf", [D, D], BF16)
    wop = b.sbt("wop", [128, 2 * 1024], BF16)
    w1b = b.sbt("w1b", [17, HID], BF16)
    w2b = b.sbt("w2b", [HID, HID], BF16)
    w3b = b.sbt("w3b", [HID, 4], BF16)
    bcol = b.sbt("bcol", [HID, 4], F32)
    es = ES(b)
    with nc.sbuf_tensor("wopsl", [128, 2 * 640], BF16) as wsl, \
         nc.sbuf_tensor("wopf", [HID, 3 * HID], F32) as wf, \
         nc.psum_tensor("wopp0", [128, 512], F32) as p0, \
         nc.psum_tensor("wopp1", [128, 512], F32) as p1:
        ps = [p0, p1]
        l1 = es.ev("sync", [], lambda e: e.dma_start(out=wf[0:17, 0:HID], in_=b.ein("W1")),
                   sem="dC", inc=16)
        c1 = es.ev("vector", [l1], lambda e: e.tensor_copy(out=w1b[:], in_=wf[0:17, 0:HID]))
        l2 = es.ev("sync", [c1], lambda e: e.dma_start(out=wf[:, 0:HID], in_=b.ein("W2")), sem="dC", inc=16)
        c2 = es.ev("vector", [l2], lambda e: e.tensor_copy(out=w2b[:], in_=wf[:, 0:HID]))
        l3 = es.ev("sync", [c2], lambda e: e.dma_start(out=wf[:, 0:4], in_=b.ein("W3")), sem="dC", inc=16)
        c3 = es.ev("vector", [l3], lambda e: e.tensor_copy(out=w3b[:], in_=wf[:, 0:4]))
        hb1 = es.ev("sync", [c3], lambda e: e.dma_start(out=bcol[:, 0:1], in_=b.ein("b1")), sem="dC", inc=16)
        hb2 = es.ev("sync", [], lambda e: e.dma_start(out=bcol[:, 1:2], in_=b.ein("b2")), sem="dD", inc=16)
        hb3 = es.ev("sync", [], lambda e: e.dma_start(out=bcol[0:4, 2:3], in_=b.ein("b3")), sem="dE", inc=16)
        ldv = {}
        mmv = {}
        evv = {}
        ldw = {}
        for jj in range(4):
            cc, dh = jj // 2, jj % 2
            for c in range(DC):
                k = jj * DC + c
                deps = [mmv[k - 2]] if k >= 2 else []

                def ldp_(e, cc=cc, c=c, k=k):
                    return e.dma_start(out=wsl[:, (k % 2) * 640:(k % 2) * 640 + 128],
                                       in_=pald[c * 128:(c + 1) * 128, cc * 128:(cc + 1) * 128])
                ldv[k] = es.ev("sync", deps, ldp_, sem="dA" if k % 2 == 0 else "dB", inc=16)

                def ldw_(e, dh=dh, c=c, k=k):
                    return e.dma_start(out=wsl[:, (k % 2) * 640 + 128:(k % 2) * 640 + 640],
                                       in_=wod[c * 128:(c + 1) * 128, dh * 512:(dh + 1) * 512])
                wdeps = list(deps)
                if k == 0:
                    wdeps.append(hb3)
                elif k == 1:
                    wdeps.append(hb2)
                ldw[k] = es.ev("sync", wdeps, ldw_, sem="dE" if k % 2 == 0 else "dF", inc=16)

                def mm(e, jj=jj, c=c, k=k):
                    return e.matmul(ps[jj % 2][:], wsl[:, (k % 2) * 640:(k % 2) * 640 + 128],
                                    wsl[:, (k % 2) * 640 + 128:(k % 2) * 640 + 640],
                                    start=(c == 0), stop=(c == DC - 1))
                deps = [ldv[k], ldw[k]]
                if jj >= 2 and c == 0:
                    deps.append(evv[jj - 2])
                mmv[k] = es.ev("tensor", deps, mm)

            def ev_(e, cc=cc, dh=dh, jj=jj):
                return e.tensor_copy(out=wop[:, cc * 1024 + dh * 512: cc * 1024 + dh * 512 + 512],
                                     in_=ps[jj % 2][:])
            evv[jj] = es.ev("vector", [mmv[jj * DC + DC - 1]], ev_)
        es.emit("woprep")


def emit_l0attn(b: Bld, tag=""):
    nc = b.nc
    xT, xnT = b.sb["xT"], b.sb["xnT"]
    identb, identf = b.sb["identb"], b.sb["identf"]
    ones_bf, onesr_f32, onesr_bf = b.sb["ones_bf"], b.sb["onesr_f32"], b.sb["onesr_bf"]
    tri_bf, neg30r_bf = b.sb["tri_bf"], b.sb["neg30r_bf"]
    iota_col = b.sb["iota_col"]
    wid = b.dr("Wi_bf", [D, D], BF16)
    wpd = b.dr("Wp_bf", [D, D], BF16)
    PHAT = b.dr("l0a_phat", [2 * T, D], BF16)
    TKI = b.dr("l0a_tki", [T * 16], F32)
    KEEP = b.dr("l0a_keep", [T * 16], F32)
    DELTA = b.dr("l0a_delta", [T * 16], BF16)
    GRM = b.dr("l0a_grm", [256 * 128, 128], BF16)
    Z3 = b.dr("l0a_z3", [4, T * 16], BF16)
    ropec = b.ein("c_ropecos")
    ropes = b.ein("c_ropesin")

    itctx = ExitStack()
    ITb = itctx.enter_context(nc.sbuf_tensor(f"l0aI{tag}", [128, DC * T], BF16))
    PTb = itctx.enter_context(nc.sbuf_tensor(f"l0aP{tag}", [128, DC * T], BF16))

    for mi, (wsrc, dstT) in enumerate([(wid, ITb), (wpd, PTb)]):
        es = ES(b)
        with nc.sbuf_tensor(f"l0aw{tag}{mi}", [128, 2 * D], BF16) as wsl, \
             nc.sbuf_tensor(f"l0acs{tag}{mi}", [128, 4 * 512], BF16) as cssl, \
             nc.sbuf_tensor(f"l0at{tag}{mi}", [128, 2 * 512], F32) as tbuf, \
             nc.psum_tensor(f"l0ap0{tag}{mi}", [128, 512], F32) as q0, \
             nc.psum_tensor(f"l0ap1{tag}{mi}", [128, 512], F32) as q1:
            qs = [q0, q1]
            wld = {}
            mmd = {}
            rpd = {}
            csd = {}
            for p in range(4):
                for half in range(2):
                    o = p + 4 * half
                    jj = p * 2 + half
                    deps = [mmd[(p - 1, half)]] if p >= 1 else []

                    def ld(e, o=o, jj=jj, wsrc=wsrc):
                        return e.dma_start(
                            out=wsl[:, (jj % 2) * D:(jj % 2) * D + D].rearrange("p (c j) -> p c j", c=DC),
                            in_=wsrc[:, o * 128:(o + 1) * 128].rearrange("(c p) j -> p c j", p=128))
                    wld[(p, half)] = es.ev("sync", deps, ld, sem="dA" if jj % 2 == 0 else "dB", inc=16)
                for blk in range(NB):
                    if blk >= 2:
                        deps = [rpd[(p, blk - 2)]]
                    elif p >= 1:
                        deps = [rpd[(p - 1, blk + 2)]]
                    else:
                        deps = []

                    def ldco(e, p=p, blk=blk):
                        return e.dma_start(out=cssl[:, (blk % 2) * 1024:(blk % 2) * 1024 + 512],
                                           in_=ropec[p * 128:(p + 1) * 128, blk * 512:(blk + 1) * 512])
                    csd1 = es.ev("sync", deps, ldco, sem="dC" if blk % 2 == 0 else "dD", inc=16)

                    def ldsi(e, p=p, blk=blk):
                        return e.dma_start(out=cssl[:, (blk % 2) * 1024 + 512:(blk % 2) * 1024 + 1024],
                                           in_=ropes[p * 128:(p + 1) * 128, blk * 512:(blk + 1) * 512])
                    csd2 = es.ev("sync", list(deps), ldsi, sem="dE" if blk % 2 == 0 else "dF", inc=16)
                    csd[(p, blk)] = csd2
                    for half in range(2):
                        o = p + 4 * half

                        def mm(e, o=o, half=half, blk=blk, jj=p * 2 + half):
                            r = None
                            for c in range(DC):
                                r = e.matmul(qs[half][:],
                                             wsl[:, (jj % 2) * D + c * 128:(jj % 2) * D + c * 128 + 128],
                                             xnT[:, c * T + blk * 512: c * T + blk * 512 + 512],
                                             start=(c == 0), stop=(c == DC - 1))
                            return r
                        deps = [wld[(p, half)]]
                        if blk == 0 and p >= 1:
                            deps.append(rpd[(p - 1, NB - 1)])
                        elif blk >= 1:
                            deps.append(rpd[(p, blk - 1)])
                        mmd[(p, half, blk)] = es.ev("tensor", deps, mm)
                    dep2 = [mmd[(p, 0, blk)], mmd[(p, 1, blk)], csd1, csd2]

                    def rp1(e, p=p, blk=blk):
                        return e.tensor_mul(tbuf[:, 0:512], q0[:],
                                            cssl[:, (blk % 2) * 1024:(blk % 2) * 1024 + 512])
                    h1 = es.ev("vector", dep2, rp1)

                    def rp2(e, p=p, blk=blk):
                        return e.tensor_mul(tbuf[:, 512:1024], q1[:],
                                            cssl[:, (blk % 2) * 1024 + 512:(blk % 2) * 1024 + 1024])
                    h2 = es.ev("gpsimd", [h1], rp2) if False else es.ev("vector", [h1], rp2)

                    def rp3(e, p=p, blk=blk, dstT=dstT):
                        return e.scalar_tensor_tensor(
                            out=dstT[:, p * T + blk * 512: p * T + blk * 512 + 512],
                            in0=tbuf[:, 512:1024], scalar=-1.0, in1=tbuf[:, 0:512],
                            op0=ALU.mult, op1=ALU.add)
                    h3 = es.ev("vector", [h2], rp3)

                    def rp4(e, p=p, blk=blk):
                        return e.tensor_mul(tbuf[:, 0:512], q0[:],
                                            cssl[:, (blk % 2) * 1024 + 512:(blk % 2) * 1024 + 1024])
                    h4 = es.ev("vector", [h3], rp4)

                    def rp5(e, p=p, blk=blk):
                        return e.tensor_mul(tbuf[:, 512:1024], q1[:],
                                            cssl[:, (blk % 2) * 1024:(blk % 2) * 1024 + 512])
                    h5 = es.ev("vector", [h4], rp5)

                    def rp6(e, p=p, blk=blk, dstT=dstT):
                        return e.tensor_add(
                            dstT[:, (p + 4) * T + blk * 512: (p + 4) * T + blk * 512 + 512],
                            tbuf[:, 0:512], tbuf[:, 512:1024])
                    rpd[(p, blk)] = es.ev("vector", [h5], rp6)
                mmd[(p, 0)] = mmd[(p, 0, NB - 1)]
                mmd[(p, 1)] = mmd[(p, 1, NB - 1)]
            es.emit(f"l0aproj{tag}{mi}")

        es = ES(b)
        with nc.sbuf_tensor(f"l0an{tag}{mi}", [128, 1024], BF16) as sqb, \
             nc.sbuf_tensor(f"l0anr{tag}{mi}", [1, T], F32) as nrows, \
             nc.sbuf_tensor(f"l0abc{tag}{mi}", [128, 160], F32) as nbc, \
             nc.sbuf_tensor(f"l0aht{tag}{mi}", [128, 2 * 1024], BF16) as htok, \
             nc.psum_tensor(f"l0an0{tag}{mi}", [1, 512], F32) as pn0, \
             nc.psum_tensor(f"l0an1{tag}{mi}", [1, 512], F32) as pn1, \
             nc.psum_tensor(f"l0anb{tag}{mi}", [128, 512], F32) as pnb, \
             nc.psum_tensor(f"l0att{tag}{mi}", [128, 128], BF16) as ptt0, \
             nc.psum_tensor(f"l0att2{tag}{mi}", [128, 128], BF16) as ptt1:
            pns = [pn0, pn1]
            ptts = [ptt0, ptt1]
            sqd = {}
            mmn = {}
            evn = {}
            for blk in range(NB):
                for c in range(DC):
                    k = blk * DC + c
                    deps = [mmn[k - 2]] if k >= 2 else []

                    def sq(e, blk=blk, c=c, k=k, dstT=dstT):
                        return e.activation(out=sqb[:, (k % 2) * 512:(k % 2) * 512 + 512],
                                            in_=dstT[:, c * T + blk * 512: c * T + blk * 512 + 512],
                                            func=AF.Square)
                    sqd[k] = es.ev("scalar", deps, sq)

                    def mmq(e, blk=blk, c=c, k=k):
                        return e.matmul(pns[blk % 2][0:1, :], ones_bf[:, 0:1],
                                        sqb[:, (k % 2) * 512:(k % 2) * 512 + 512],
                                        start=(c == 0), stop=(c == DC - 1))
                    mmn[k] = es.ev("tensor", [sqd[k]] + ([evn[blk - 2]] if blk >= 2 and c == 0 else []), mmq)
                deps = [mmn[blk * DC + DC - 1]]
                if blk >= 1:
                    deps.append(deps[0])

                def evq(e, blk=blk):
                    return e.tensor_copy(out=nrows[:, blk * 512:(blk + 1) * 512], in_=pns[blk % 2][0:1, :])
                evn[blk] = es.ev("vector", deps, evq)
            hq = es.ev("scalar", [evn[NB - 1]], lambda e: e.activation(
                out=nrows[:], in_=nrows[:], func=AF.Sqrt))
            hm = es.ev("vector", [hq], lambda e: e.tensor_scalar_max(nrows[:], nrows[:], 1e-12))
            hln = es.ev("scalar", [hm], lambda e: e.activation(
                out=nrows[:], in_=nrows[:], func=AF.Ln))
            hex = es.ev("scalar", [hln], lambda e: e.activation(
                out=nrows[:], in_=nrows[:], func=AF.Exp, scale=-1.0))
            RSb = b.dr("l0a_rnb", [1, T], F32)
            hst = es.ev("sync", [hex], lambda e: e.dma_start(out=RSb, in_=nrows[:]),
                        sem="dE", inc=16)
            hld = es.ev("sync", [hst], lambda e: e.dma_start(
                out=nbc[0:16, 0:128], in_=RSb.rearrange("o (i p) -> i p", i=16)), sem="dF", inc=16)
            htr = es.ev("tensor", [hld], lambda e: e.transpose(
                pnb[0:128, 0:16], nbc[0:16, 0:128], identf[0:16, 0:16]))
            hrc = es.ev("vector", [htr], lambda e: e.tensor_copy(out=nbc[:, 128:128 + 16],
                                                                 in_=pnb[0:128, 0:16]))
            trd = {}
            tvd = {}
            std = {}
            for i in range(NT):
                for c in range(DC):
                    k = i * DC + c
                    deps = [hrc]
                    if k >= 2:
                        deps.append(tvd[k - 2])

                    def trh(e, i=i, c=c, k=k, dstT=dstT):
                        return e.transpose(ptts[k % 2][:],
                                           dstT[:, c * T + i * 128: c * T + (i + 1) * 128], identb[:])
                    trd[k] = es.ev("tensor", deps, trh)
                    deps = [trd[k]]
                    if i >= 2:
                        deps.append(std[i - 2])

                    def tvh(e, i=i, c=c, k=k):
                        return e.tensor_scalar_mul(
                            htok[:, (i % 2) * 1024 + c * 128:(i % 2) * 1024 + c * 128 + 128],
                            ptts[k % 2][:], nbc[:, 128 + i:128 + i + 1])
                    tvd[k] = es.ev("vector", deps, tvh)

                def sth(e, i=i, mi=mi):
                    return e.dma_start(out=PHAT[mi * T + i * 128: mi * T + (i + 1) * 128, :],
                                       in_=htok[:, (i % 2) * 1024:(i % 2) * 1024 + 1024])
                std[i] = es.ev("sync", [tvd[i * DC + DC - 1]], sth,
                               sem="dE" if i % 2 == 0 else "dF", inc=16)
            es.emit(f"l0anorm{tag}{mi}")

    es = ES(b)
    with nc.sbuf_tensor(f"l0aL{tag}", [128, T], F32) as Lrow, \
         nc.sbuf_tensor(f"l0atv{tag}", [128, 16], F32) as tv, \
         nc.sbuf_tensor(f"l0ati{tag}", [128, 16], U32) as tiu, \
         nc.sbuf_tensor(f"l0atf{tag}", [128, 4 * 16], F32) as tf, \
         nc.sbuf_tensor(f"l0adl{tag}", [128, 16], BF16) as dlt, \
         nc.sbuf_tensor(f"l0atc{tag}", [128, 2], F32) as tcc, \
         nc.psum_tensor(f"l0aq0{tag}", [128, 512], F32) as q0, \
         nc.psum_tensor(f"l0aq1{tag}", [128, 512], F32) as q1:
        qs = [q0, q1]
        prev_done = None
        for q in range(NT):
            nblk = q // 4 + 1
            evd = {}
            for tsb in range(nblk):
                boundary = (tsb == nblk - 1)

                deps = []
                if prev_done is not None and tsb < 2:
                    deps.extend(prev_done)
                if tsb >= 2:
                    deps.append(evd[tsb - 2])

                def mmq2(e, q=q, tsb=tsb, boundary=boundary):
                    r = None
                    for c in range(DC):
                        r = e.matmul(qs[tsb % 2][:],
                                     ITb[:, c * T + q * 128: c * T + (q + 1) * 128],
                                     PTb[:, c * T + tsb * 512: c * T + tsb * 512 + 512],
                                     start=(c == 0), stop=(c == DC - 1) and not boundary)
                    if boundary:
                        c0 = q * 128 - tsb * 512
                        r = e.matmul(qs[tsb % 2][:, c0:c0 + 128], tri_bf[:], identb[:],
                                     start=False, stop=(c0 + 128 >= 512))
                        if c0 + 128 < 512:
                            r = e.matmul(qs[tsb % 2][:, c0 + 128:512], neg30r_bf[:],
                                         onesr_bf[:, 0:512 - c0 - 128], start=False, stop=True)
                    return r
                hmm = es.ev("tensor", deps, mmq2)

                def evc(e, q=q, tsb=tsb):
                    return e.activation(out=Lrow[:, tsb * 512:(tsb + 1) * 512],
                                        in_=qs[tsb % 2][:], func=AF.Identity)
                evd[tsb] = es.ev("scalar", [hmm], evc)
            w = nblk * 512
            hdep = [evd[t2] for t2 in range(nblk)]

            ma = es.ev("vector", hdep, lambda e, w=w: e.max(out=tv[:, 0:8], in_=Lrow[:, 0:w]))
            ia = es.ev("vector", [ma], lambda e, w=w: e.max_index(
                out=tiu[:, 0:8], in_max=tv[:, 0:8], in_values=Lrow[:, 0:w]))
            rp = es.ev("vector", [ia], lambda e, w=w: e.match_replace(
                out=Lrow[:, 0:w], in_to_replace=tv[:, 0:8], in_values=Lrow[:, 0:w], imm_value=-1e30))
            mb = es.ev("vector", [rp], lambda e, w=w: e.max(out=tv[:, 8:16], in_=Lrow[:, 0:w]))
            h2 = es.ev("vector", [mb], lambda e, w=w: e.max_index(
                out=tiu[:, 8:16], in_max=tv[:, 8:16], in_values=Lrow[:, 0:w]))

            def kef(e):
                return e.tensor_single_scalar(out=tf[:, 16:32], in_=tv[:], scalar=-1e29, op=ALU.is_gt)
            h3 = es.ev("vector", [h2], kef)

            def idxf(e):
                return e.tensor_copy(out=tf[:, 0:16], in_=tiu[:])
            h4 = es.ev("vector", [h2], idxf)
            h5 = es.ev("vector", [h4], lambda e, q=q: e.tensor_scalar_add(tcc[:, 0:1], iota_col[:], float(128 * q)))
            h6 = es.ev("vector", [h5], lambda e: e.tensor_scalar_add(tf[:, 15:16], tcc[:, 0:1], float(T)))
            h7 = es.ev("vector", [h6], lambda e: e.tensor_scalar(
                out=tf[:, 32:48], in0=tf[:, 0:16], scalar1=tcc[:, 0:1], scalar2=-1.0 / T,
                op0=ALU.subtract, op1=ALU.mult))
            h8 = es.ev("vector", [h7], lambda e: e.tensor_scalar_max(tf[:, 32:48], tf[:, 32:48], 0.0))
            h9 = es.ev("vector", [h8, h3], lambda e: e.tensor_mul(dlt[:], tf[:, 32:48], tf[:, 16:32]))
            s1 = es.ev("sync", [h6], lambda e, q=q: e.dma_start(
                out=TKI.rearrange("(t k) -> t k", k=16)[q * 128:(q + 1) * 128, :], in_=tf[:, 0:16]),
                sem="dE", inc=16)
            s2 = es.ev("sync", [h3], lambda e, q=q: e.dma_start(
                out=KEEP.rearrange("(t k) -> t k", k=16)[q * 128:(q + 1) * 128, :], in_=tf[:, 16:32]),
                sem="dF", inc=16)
            s3 = es.ev("sync", [h9], lambda e, q=q: e.dma_start(
                out=DELTA.rearrange("(t k) -> t k", k=16)[q * 128:(q + 1) * 128, :], in_=dlt[:]),
                sem="dC", inc=16)
            prev_done = [h9, s1, s2, s3]
        es.emit(f"l0atopk{tag}")
    itctx.close()

    es = ES(b)
    with nc.sbuf_tensor(f"l0agx{tag}", [128, 256], U32) as gidxT, \
         nc.sbuf_tensor(f"l0akp{tag}", [128, 256], F32) as keepT, \
         nc.sbuf_tensor(f"l0atm{tag}", [128, 2 * 128], F32) as tmf, \
         nc.psum_tensor(f"l0agp{tag}", [128, 128], F32) as pg0:
        h3b = None
        for r in range(2):
            h1 = es.ev("sync", ([h3b] if h3b else []), lambda e, r=r: e.dma_start(
                out=tmf[:, 0:128] if r == 0 else tmf[:, 128:256],
                in_=TKI.rearrange("(r p j) -> r p j", r=2, p=128)[r]),
                sem="dA" if r == 0 else "dB", inc=16)
            h2 = es.ev("tensor", [h1], lambda e, r=r: e.transpose(
                pg0[:], tmf[:, r * 128:(r + 1) * 128], identf[:]))
            h3 = es.ev("vector", [h2], lambda e, r=r: e.tensor_copy(
                out=gidxT[:, r * 128:(r + 1) * 128], in_=pg0[:]))
            h1b = es.ev("sync", [h3], lambda e, r=r: e.dma_start(
                out=tmf[:, 0:128] if r == 0 else tmf[:, 128:256],
                in_=KEEP.rearrange("(r p j) -> r p j", r=2, p=128)[r]),
                sem="dA" if r == 0 else "dB", inc=16)
            h2b = es.ev("tensor", [h1b], lambda e, r=r: e.transpose(
                pg0[:], tmf[:, r * 128:(r + 1) * 128], identf[:]))
            h3b = es.ev("vector", [h2b], lambda e, r=r: e.tensor_copy(
                out=keepT[:, r * 128:(r + 1) * 128], in_=pg0[:]))
        es.emit(f"l0agidx{tag}")

        es = ES(b)
        with nc.sbuf_tensor(f"l0axg{tag}", [128, 2 * D], BF16) as Xg, \
             nc.sbuf_tensor(f"l0axt{tag}", [128, 2 * D], BF16) as XgT, \
             nc.sbuf_tensor(f"l0ags{tag}", [128, 8 * 128], BF16) as Gstr, \
             nc.psum_tensor(f"l0agt0{tag}", [128, 128], BF16) as pt0, \
             nc.psum_tensor(f"l0agt1{tag}", [128, 128], BF16) as pt1, \
             nc.psum_tensor(f"l0agt2{tag}", [128, 128], BF16) as pt2, \
             nc.psum_tensor(f"l0agt3{tag}", [128, 128], BF16) as pt3, \
             nc.psum_tensor(f"l0agg0{tag}", [128, 128], F32) as pG0, \
             nc.psum_tensor(f"l0agg1{tag}", [128, 128], F32) as pG1:
            pts = [pt0, pt1, pt2, pt3]
            pGs = [pG0, pG1]
            gld = {}
            zrd = {}
            trd = {}
            tvd = {}
            mmg = {}
            evg = {}
            stg = {}
            for g in range(256):
                deps = [trd[(g - 2, DC - 1)]] if g >= 2 else []

                def gl(e, g=g):
                    return e.indirect_dma_start(
                        Xg[:, (g % 2) * D:(g % 2) * D + D], None, PHAT,
                        bass.IndirectOffsetOnAxis(gidxT[:, g:g + 1], 0))
                gld[g] = es.ev("gpsimd", deps, gl, sem="dH" if g % 2 == 0 else "dI", inc=16)
                zdep = [gld[g]]
                if g < 2:
                    def zr(e, g=g):
                        return e.tensor_scalar_mul(Xg[:, (g % 2) * D:(g % 2) * D + D],
                                                   Xg[:, (g % 2) * D:(g % 2) * D + D],
                                                   keepT[:, g:g + 1])
                    zrd[g] = es.ev("vector", zdep, zr)
                    zdep = [zrd[g]]
                for c in range(DC):
                    k = g * DC + c
                    deps = list(zdep)
                    if k >= 4:
                        deps.append(tvd[k - 4])

                    def tr(e, g=g, c=c, k=k):
                        return e.transpose(pts[k % 4][:],
                                           Xg[:, (g % 2) * D + c * 128:(g % 2) * D + c * 128 + 128],
                                           identb[:])
                    trd[(g, c)] = es.ev("tensor", deps, tr)

                    def tvf(e, g=g, c=c, k=k):
                        return e.tensor_copy(out=XgT[:, (g % 2) * D + c * 128:(g % 2) * D + c * 128 + 128],
                                             in_=pts[k % 4][:])

                    def tvf_sc(e, g=g, c=c, k=k):
                        return e.copy(out=XgT[:, (g % 2) * D + c * 128:(g % 2) * D + c * 128 + 128],
                                      in_=pts[k % 4][:])
                    deps = [trd[(g, c)]]
                    if c == 0 and g >= 2:
                        deps.append(mmg[g - 2])
                    if k % 2 == 0:
                        tvd[k] = es.ev("vector", deps, tvf)
                    else:
                        tvd[k] = es.ev("scalar", deps, tvf_sc)

                def mg(e, g=g):
                    r = None
                    for c in range(DC):
                        r = e.matmul(pGs[g % 2][:],
                                     XgT[:, (g % 2) * D + c * 128:(g % 2) * D + c * 128 + 128],
                                     XgT[:, (g % 2) * D + c * 128:(g % 2) * D + c * 128 + 128],
                                     start=(c == 0), stop=(c == DC - 1))
                    return r
                deps = [tvd[g * DC + DC - 1], tvd[g * DC + DC - 2]]
                if g >= 2:
                    deps.append(evg[g - 2])
                mmg[g] = es.ev("tensor", deps, mg)

                def evc(e, g=g):
                    return e.tensor_scalar(out=Gstr[:, (g % 8) * 128:(g % 8) * 128 + 128],
                                           in0=pGs[g % 2][:], scalar1=1.0, scalar2=-1.0,
                                           op0=ALU.min, op1=ALU.max)
                deps = [mmg[g]]
                if g >= 8:
                    deps.append(stg[g // 8 - 1])
                evg[g] = es.ev("vector", deps, evc)
                if g % 8 == 7:
                    def st(e, g=g):
                        g0 = g - 7
                        return e.dma_start(
                            out=GRM[g0 * 128:(g0 + 8) * 128, :].rearrange("(gi p) j -> p gi j", p=128),
                            in_=Gstr[:].rearrange("p (gi j) -> p gi j", gi=8))
                    stg[g // 8] = es.ev("sync", [evg[g]], st,
                                        sem="dE" if (g // 8) % 2 == 0 else "dF", inc=16)
            es.emit(f"l0agram{tag}")

    es = ES(b)
    with nc.sbuf_tensor(f"l0ar{tag}", [17, 4096], BF16) as relT, \
         nc.sbuf_tensor(f"l0ah1{tag}", [HID, 4096], BF16) as h1b_, \
         nc.sbuf_tensor(f"l0ah2{tag}", [HID, 4096], BF16) as h2b_, \
         nc.sbuf_tensor(f"l0az{tag}", [4, 4096], BF16) as z3s, \
         nc.psum_tensor(f"l0am0{tag}", [64, 512], F32) as pm0, \
         nc.psum_tensor(f"l0am1{tag}", [64, 512], F32) as pm1, \
         nc.psum_tensor(f"l0az0{tag}", [4, 512], F32) as pz0, \
         nc.psum_tensor(f"l0az1{tag}", [4, 512], F32) as pz1:
        pms = [pm0, pm1]
        pzs = [pz0, pz1]
        w1b, w2b, w3b, bcol = b.sb["w1b"], b.sb["w2b"], b.sb["w3b"], b.sb["bcol"]
        prev_ch_done = None
        for ch in range(8):
            g0 = ch * 32

            def ldrel(e, ch=ch, g0=g0):
                return e.dma_start(
                    out=relT[0:16, :].rearrange("j (g i k) -> j g i k", g=32, i=8),
                    in_=GRM.rearrange("(g i j) k -> j g i k", g=256, i=8)[:, g0:g0 + 32, :, 0:16])
            hra = es.ev("sync", ([prev_ch_done] if prev_ch_done else []), ldrel, sem="dA", inc=16)

            def ldd(e, ch=ch):
                return e.dma_start(out=relT[16:17, :],
                                   in_=DELTA[ch * 4096:(ch + 1) * 4096].rearrange("(o n) -> o n", o=1))
            hrb = es.ev("sync", [], ldd, sem="dB", inc=16)
            hr = hrb
            hr_all = [hra, hrb]
            ev1 = {}
            ev2 = {}
            ev3 = {}
            for sb in range(8):
                def m1(e, sb=sb):
                    return e.matmul(pms[sb % 2][:], w1b[:], relT[:, sb * 512:(sb + 1) * 512],
                                    start=True, stop=True)
                deps = [hra, hrb] + ([ev1[sb - 2]] if sb >= 2 else [])
                hm1 = es.ev("tensor", deps, m1)

                def g1(e, sb=sb):
                    return e.activation(out=h1b_[:, sb * 512:(sb + 1) * 512], in_=pms[sb % 2][:],
                                        func=AF.Gelu, bias=bcol[0:HID, 0:1])
                ev1[sb] = es.ev("scalar", [hm1], g1)
            for sb in range(8):
                def m2(e, sb=sb):
                    return e.matmul(pms[sb % 2][:], w2b[:], h1b_[:, sb * 512:(sb + 1) * 512],
                                    start=True, stop=True)
                deps = [ev1[sb]] + ([ev2[sb - 2]] if sb >= 2 else [ev1[7]])
                hm2 = es.ev("tensor", deps, m2)

                def g2(e, sb=sb):
                    return e.activation(out=h2b_[:, sb * 512:(sb + 1) * 512], in_=pms[sb % 2][:],
                                        func=AF.Gelu, bias=bcol[0:HID, 1:2])
                ev2[sb] = es.ev("scalar", [hm2], g2)
            for sb in range(8):
                def m3(e, sb=sb):
                    return e.matmul(pzs[sb % 2][:], w3b[:], h2b_[:, sb * 512:(sb + 1) * 512],
                                    start=True, stop=True)
                deps = [ev2[sb]] + ([ev3[sb - 2]] if sb >= 2 else [ev2[7]])
                hm3 = es.ev("tensor", deps, m3)

                def g3a(e, sb=sb):
                    return e.activation(out=z3s[0:4, sb * 512:(sb + 1) * 512], in_=pzs[sb % 2][:],
                                        func=AF.Identity, bias=bcol[0:4, 2:3])
                hz = es.ev("scalar", [hm3], g3a)

                def g3b(e, sb=sb):
                    return e.activation(out=z3s[0:2, sb * 512:(sb + 1) * 512],
                                        in_=z3s[0:2, sb * 512:(sb + 1) * 512], func=AF.Tanh)
                ev3[sb] = es.ev("scalar", [hz], g3b)

            def stz(e, ch=ch):
                return e.dma_start(out=Z3[:, ch * 4096:(ch + 1) * 4096], in_=z3s[:])
            prev_ch_done = es.ev("sync", [ev3[7]], stz, sem="dE", inc=16)
        es.emit(f"l0amlp{tag}")

    es = ES(b)
    wop = b.sb["wop"]
    hat_tile = b.sb["hat_tile"]
    with nc.sbuf_tensor(f"l0azt{tag}", [128, 2 * 80], BF16) as zt, \
         nc.sbuf_tensor(f"l0akp2{tag}", [128, 2 * 16], F32) as kp2, \
         nc.sbuf_tensor(f"l0aw{tag}", [128, 2 * 48], F32) as wsc, \
         nc.sbuf_tensor(f"l0ahat{tag}", [128, 2 * 3 * 256], BF16) as hats, \
         nc.sbuf_tensor(f"l0aA{tag}", [128, 2 * 4 * 256], F32) as Aac, \
         nc.sbuf_tensor(f"l0aAb{tag}", [128, 2 * 256], BF16) as Abf, \
         nc.psum_tensor(f"l0ay0{tag}", [128, 128], F32) as py0, \
         nc.psum_tensor(f"l0ay1{tag}", [128, 128], F32) as py1, \
         nc.psum_tensor(f"l0ay2{tag}", [128, 128], F32) as py2, \
         nc.psum_tensor(f"l0ay3{tag}", [128, 128], F32) as py3, \
         nc.psum_tensor(f"l0aat{tag}", [128, 128], BF16) as pat0, \
         nc.psum_tensor(f"l0aat2{tag}", [128, 128], BF16) as pat1:
        pys4 = [[py0, py1], [py2, py3]]
        pats = [pat0, pat1]
        hatrow = 2 * 3 * 256
        Arow = 2 * 4 * 256

        def bc_inner(buf, off, rowlen):
            return bass.AP(buf, off, [[rowlen, 128], [1, 16], [0, 16]])

        def bc_outer(buf, off, rowlen):
            return bass.AP(buf, off, [[rowlen, 128], [0, 16], [1, 16]])

        def v3(buf, off, rowlen):
            return bass.AP(buf, off, [[rowlen, 128], [16, 16], [1, 16]])
        prevAq = {}
        for q in range(NT):
            p = q % 2
            zo, wo, ho, ao, bo = p * 80, p * 48, p * 768, p * 1024, p * 256
            dep0 = [prevAq[q - 2]] if q >= 2 else []

            def ldz(e, q=q, zo=zo):
                return e.dma_start(
                    out=zt[:, zo:zo + 48].rearrange("p (r k) -> p r k", r=3),
                    in_=Z3[0:3, :].rearrange("r (t k) -> t r k", k=16)[q * 128:(q + 1) * 128, :, :])
            hz = es.ev("sync", dep0, ldz, sem="dA" if p == 0 else "dB", inc=16)
            hk = es.ev("sync", dep0, lambda e, q=q, p=p: e.dma_start(
                out=kp2[:, p * 16:p * 16 + 16],
                in_=KEEP.rearrange("(t k) -> t k", k=16)[q * 128:(q + 1) * 128, :]),
                sem="dC" if p == 0 else "dD", inc=16)
            kps = kp2[:, p * 16:p * 16 + 16]
            h1 = es.ev("vector", [hz, hk], lambda e, wo=wo, zo=zo, kps=kps: e.tensor_mul(
                wsc[:, wo + 32:wo + 48], zt[:, zo + 32:zo + 48], kps))
            h2 = es.ev("vector", [h1], lambda e, wo=wo, kps=kps: e.tensor_scalar(
                out=wsc[:, wo + 16:wo + 32], in0=kps, scalar1=1e30, scalar2=-1e30,
                op0=ALU.mult, op1=ALU.add))
            h3 = es.ev("vector", [h2], lambda e, wo=wo: e.tensor_add(
                wsc[:, wo + 32:wo + 48], wsc[:, wo + 32:wo + 48], wsc[:, wo + 16:wo + 32]))
            h4 = es.ev("vector", [h3], lambda e, wo=wo: e.memset(wsc[:, wo + 47:wo + 48], -1e30))
            h5 = es.ev("vector", [h4], lambda e, wo=wo: e.reduce_max(
                out=wsc[:, wo + 1:wo + 2], in_=wsc[:, wo + 32:wo + 48], axis=AXX))
            h6 = es.ev("vector", [h5], lambda e, wo=wo: e.tensor_scalar_mul(
                wsc[:, wo + 2:wo + 3], wsc[:, wo + 1:wo + 2], -1.0))
            h7 = es.ev("scalar", [h6], lambda e, wo=wo: e.activation(
                out=wsc[:, wo + 16:wo + 32], in_=wsc[:, wo + 32:wo + 48],
                func=AF.Exp, bias=wsc[:, wo + 2:wo + 3]))
            h8 = es.ev("vector", [h7], lambda e, wo=wo: e.reduce_sum(
                out=wsc[:, wo + 3:wo + 4], in_=wsc[:, wo + 16:wo + 32], axis=AXX))
            h9 = es.ev("vector", [h8], lambda e, wo=wo: e.reciprocal(
                out=wsc[:, wo + 4:wo + 5], in_=wsc[:, wo + 3:wo + 4]))
            hw = es.ev("vector", [h9], lambda e, wo=wo: e.tensor_scalar_mul(
                wsc[:, wo + 16:wo + 32], wsc[:, wo + 16:wo + 32], wsc[:, wo + 4:wo + 5]))
            hf = {}
            for ax in range(2):
                a1 = es.ev("scalar", [hz] + ([hf[0]] if ax == 1 else []),
                           lambda e, ax=ax, zo=zo: e.activation(
                               out=zt[:, zo + 64:zo + 80], in_=zt[:, zo + ax * 16:zo + (ax + 1) * 16],
                               func=AF.Identity, scale=7.5, bias=konst(b, 7.5)))
                a2 = es.ev("vector", [a1], lambda e, zo=zo: e.tensor_scalar(
                    out=zt[:, zo + 64:zo + 80], in0=zt[:, zo + 64:zo + 80], scalar1=15.0, scalar2=0.0,
                    op0=ALU.min, op1=ALU.max))
                a3 = es.ev("vector", [a2], lambda e, ax=ax, zo=zo, ho=ho: e.tensor_tensor(
                    out=v3(hats, ho + ax * 256, hatrow), in0=bc_inner(zt, zo + 64, 2 * 80),
                    in1=v3(hat_tile, 0, 256), op=ALU.subtract))
                a4 = es.ev("scalar", [a3], lambda e, ax=ax, ho=ho: e.activation(
                    out=hats[:, ho + ax * 256:ho + (ax + 1) * 256],
                    in_=hats[:, ho + ax * 256:ho + (ax + 1) * 256], func=AF.Abs))
                a5 = es.ev("scalar", [a4], lambda e, ax=ax, ho=ho: e.activation(
                    out=hats[:, ho + ax * 256:ho + (ax + 1) * 256],
                    in_=hats[:, ho + ax * 256:ho + (ax + 1) * 256],
                    func=AF.Relu, scale=-1.0, bias=konst(b, 1.0)))
                hf[ax] = a5
            hwy = es.ev("vector", [hf[1], hw], lambda e, ho=ho, wo=wo: e.tensor_tensor(
                out=v3(hats, ho + 512, hatrow), in0=v3(hats, ho + 256, hatrow),
                in1=bc_inner(wsc, wo + 16, 2 * 48), op=ALU.mult))
            depA = dep0

            def mulk(e, k2, dst, ho=ho):
                return e.tensor_tensor(
                    out=bass.AP(Aac, dst, [[Arow, 128], [16, 16], [1, 16]]),
                    in0=bc_inner(hats, ho + 512 + k2 * 16, hatrow),
                    in1=bc_outer(hats, ho + 0 + k2 * 16, hatrow), op=ALU.mult)
            hE = es.ev("vector", [hwy, hf[0]] + depA, lambda e, mulk=mulk, ao=ao: mulk(e, 0, ao + 0))
            hO = es.ev("gpsimd", [hwy, hf[0]] + depA, lambda e, mulk=mulk, ao=ao: mulk(e, 1, ao + 256))
            for k2 in range(2, 15):
                if k2 % 2 == 0:
                    hs = es.ev("vector", [hE], lambda e, k2=k2, mulk=mulk, ao=ao: mulk(e, k2, ao + 512))
                    hE = es.ev("vector", [hs], lambda e, ao=ao: e.tensor_add(
                        Aac[:, ao:ao + 256], Aac[:, ao:ao + 256], Aac[:, ao + 512:ao + 768]))
                else:
                    hs = es.ev("gpsimd", [hO], lambda e, k2=k2, mulk=mulk, ao=ao: mulk(e, k2, ao + 768))
                    hO = es.ev("gpsimd", [hs], lambda e, ao=ao: e.tensor_add(
                        Aac[:, ao + 256:ao + 512], Aac[:, ao + 256:ao + 512],
                        Aac[:, ao + 768:ao + 1024]))
            hA = es.ev("vector", [hE, hO], lambda e, ao=ao: e.tensor_add(
                Aac[:, ao:ao + 256], Aac[:, ao:ao + 256], Aac[:, ao + 256:ao + 512]))
            hcst = es.ev("vector", [hA], lambda e, bo=bo, ao=ao: e.tensor_copy(
                out=Abf[:, bo:bo + 256], in_=Aac[:, ao:ao + 256]))
            trA = {}
            for cc in range(2):
                t1 = es.ev("tensor", [hcst] if cc == 0 else [trA[0]],
                           lambda e, cc=cc, bo=bo, p=p: e.transpose(
                               pats[p][:], Abf[:, bo + cc * 128:bo + (cc + 1) * 128], identb[:]))
                trA[cc] = es.ev("vector", [t1], lambda e, cc=cc, ho=ho, p=p: e.tensor_copy(
                    out=hats[:, ho + 512 + cc * 128:ho + 512 + cc * 128 + 128], in_=pats[p][:]))
            prevy = {}
            for c in range(DC):
                def ym2(e, c=c, ho=ho, p=p):
                    r = None
                    for cc in range(2):
                        r = e.matmul(pys4[p][c % 2][:],
                                     wop[:, cc * 1024 + c * 128: cc * 1024 + c * 128 + 128],
                                     hats[:, ho + 512 + cc * 128:ho + 512 + cc * 128 + 128],
                                     start=(cc == 0), stop=(cc == 1))
                    return r
                deps = [trA[1]]
                if c >= 2:
                    deps.append(prevy[c - 2])
                elif q >= 2:
                    deps.append(prevAq[q - 2])
                hy = es.ev("tensor", deps, ym2)

                def ya(e, c=c, q=q, p=p):
                    sl = slice(c * T + q * 128, c * T + (q + 1) * 128)
                    return e.tensor_add(xT[:, sl], xT[:, sl], pys4[p][c % 2][:])
                prevy[c] = es.ev("vector", [hy], ya)
            prevAq[q] = prevy[DC - 1]
        es.emit(f"l0apal{tag}")


def build_stage1():
    b = Bld()
    emit_setup(b)
    emit_weight_prep(b)
    for seq in range(SPB):
        emit_load_x(b, seq, tag=f"s{seq}")
        emit_ln(b, b.sb["xT"], b.sb["xnT"], True, True, tag=f"s{seq}")
        emit_l1ffn(b, tag=f"s{seq}")
        emit_store_x(b, seq, tag=f"s{seq}")
    return b


def build_stage3():
    b = Bld()
    emit_setup(b)
    emit_weight_prep(b)
    emit_seprep(b)
    for seq in range(SPB):
        emit_load_x(b, seq, tag=f"s{seq}")
        emit_ln(b, b.sb["xT"], b.sb["xnT"], True, True, tag=f"s{seq}")
        emit_l0ffn(b, tag=f"s{seq}")
        emit_store_x(b, seq, tag=f"s{seq}")
    return b


def build_stage4():
    b = Bld()
    emit_setup(b)
    emit_weight_prep(b)
    emit_l0attn_prologue(b)
    for seq in range(SPB):
        emit_load_x(b, seq, tag=f"s{seq}")
        emit_ln(b, b.sb["xT"], b.sb["xnT"], True, False, tag=f"s{seq}")
        emit_l0attn(b, tag=f"s{seq}")
        emit_store_x(b, seq, tag=f"s{seq}")
    return b


def build_stage2():
    b = Bld()
    emit_setup(b)
    emit_weight_prep(b)
    for seq in range(SPB):
        emit_load_x(b, seq, tag=f"s{seq}")
        emit_ln(b, b.sb["xT"], b.sb["xnT"], True, False, tag=f"s{seq}")
        emit_l1attn(b, tag=f"s{seq}")
        emit_store_x(b, seq, tag=f"s{seq}")
    return b


def build_full():
    b = Bld()
    emit_setup(b)
    emit_weight_prep(b)
    emit_seprep(b)
    emit_l0attn_prologue(b)
    for seq in range(SPB):
        emit_load_x(b, seq, tag=f"s{seq}")
        emit_ln(b, b.sb["xT"], b.sb["xnT"], True, True, tag=f"al{seq}")
        emit_l0attn(b, tag=f"s{seq}")
        emit_l0ffn(b, tag=f"s{seq}")
        emit_ln(b, b.sb["xT"], b.sb["xnT"], True, False, tag=f"bl{seq}")
        emit_l1attn(b, tag=f"s{seq}")
        emit_l1ffn(b, tag=f"s{seq}")
        emit_store_x(b, seq, tag=f"s{seq}")
    return b


_CACHED = {}


def kernel(**inputs):
    b = build_full()
    in_maps = [host_inputs_for_core(inputs, c) for c in range(8)]
    res = run_bass_kernel_spmd(b.nc, in_maps, core_ids=list(range(8)))
    out = np.concatenate([r["y_tok"].reshape(SPB, T, D) for r in res.results], axis=0)
    return out
